# version 01
# baseline (speedup 1.0000x reference)
import contextlib
import numpy as np
import ml_dtypes
import concourse.bass as bass
import concourse.mybir as mybir
from concourse.bass_utils import run_bass_kernel_spmd

F32 = mybir.dt.float32
BF16 = mybir.dt.bfloat16
AF = mybir.ActivationFunctionType
ALU = mybir.AluOpType
AX = mybir.AxisListType
bfd = ml_dtypes.bfloat16

B, T, C, H, KD = 4, 2048, 2048, 32, 64
TL = 1024
NSEG = 16
SEGW = TL + 1
G = 128
NCH = TL // G
NG = 4
NPG = NSEG // NG
HPG = H // NG
CQ = NPG * 128
EPS_LN = 1e-5
EPS_GN = 0.00064

DEBUG = []
TRUNC = 9999


class Sched:
    ENG = ("pe", "dv", "ac", "po", "sy")

    def __init__(self, nc):
        self.nc = nc
        self.items = {e: [] for e in self.ENG}
        self.sems = {}
        self.cnt = {}

    def sem(self, name):
        if name not in self.sems:
            self.sems[name] = self.nc.alloc_semaphore(name)
            self.cnt[name] = 0
        return self.sems[name]

    def op(self, eng, fn, waits=(), inc=None, amt=1):
        v = None
        waits = [w for w in waits if w is not None and w[1] > 0]
        if eng in ("dv", "ac", "po"):
            self.sem(eng)
            if self.cnt[eng] > 0:
                waits.append((eng, self.cnt[eng]))
        if inc is not None:
            self.sem(inc)
            self.cnt[inc] += amt
            v = self.cnt[inc]
        for s, _ in waits:
            self.sem(s)
        self.items[eng].append((fn, tuple(waits), inc, amt))
        return v

    def dma(self, eng, out, in_, inc, waits=(), **kw):
        return self.op(eng, lambda e: e.dma_start(out, in_, **kw),
                       waits=waits, inc=inc, amt=16)

    def tpose(self, out, in_, inc, waits=()):
        return self.op("ac", lambda e: e.dma_start_transpose(out, in_),
                       waits=waits, inc=inc, amt=16)

    def emit(self):
        nc = self.nc
        lastw = {}

        def run(eng_obj, name):
            for fn, waits, inc, amt in self.items[name]:
                for s, v in waits:
                    k = (name, s)
                    if lastw.get(k, 0) < v:
                        eng_obj.wait_ge(self.sems[s], v)
                        lastw[k] = v
                ins = fn(eng_obj)
                if inc is not None:
                    ins.then_inc(self.sems[inc], amt)

        with nc.Block() as block:
            @block.tensor
            def _(e):
                run(e, "pe")

            @block.vector
            def _(e):
                run(e, "dv")

            @block.scalar
            def _(e):
                run(e, "ac")

            @block.gpsimd
            def _(e):
                run(e, "po")

            @block.sync
            def _(e):
                run(e, "sy")


def build_program():
    nc = bass.Bass("TRN2", target_bir_lowering=False)
    S = Sched(nc)

    d_x = nc.dram_tensor("x", [C, SEGW], F32, kind="ExternalInput")
    d_w = {n: nc.dram_tensor(n, [C, C], F32, kind="ExternalInput")
           for n in ("wrt", "wkt", "wvt", "wgt", "wot")}
    d_qw = nc.dram_tensor("qw", [C, C], F32, kind="ExternalInput")
    d_cols = {n: nc.dram_tensor(n, [128, NSEG], F32, kind="ExternalInput")
              for n in ("tmr", "tmk", "tmv", "tmg", "uu", "dec", "qb")}
    d_xhalo = nc.dram_tensor("xhalo", [128, NSEG], F32, kind="ExternalInput")
    d_iota = nc.dram_tensor("iota", [128, TL], F32, kind="ExternalInput")
    d_lnw = nc.dram_tensor("lnwbc", [128, C], F32, kind="ExternalInput")
    d_lnb = nc.dram_tensor("lnbbc", [128, C], F32, kind="ExternalInput")
    d_wkvp = nc.dram_tensor("wkvp", [NSEG * 128, KD], F32, kind="ExternalInput")
    d_mask = nc.dram_tensor("mask2", [128, 256], F32, kind="ExternalInput")
    d_bones = nc.dram_tensor("bones", [NSEG * 128, 16], F32, kind="ExternalInput")
    d_sel = nc.dram_tensor("sel", [128, 8], F32, kind="ExternalInput")
    d_hahb = nc.dram_tensor("hahb", [128, 4], F32, kind="ExternalInput")
    d_out = nc.dram_tensor("out", [C, TL], F32, kind="ExternalOutput")
    d_ccin = nc.dram_tensor("ccin", [NPG * 128, KD], F32)
    d_ccout = nc.dram_tensor("ccout", [8 * NPG * 128, KD], F32)
    dbg = {}
    for nm in DEBUG:
        shp = {"xn": [128, NSEG * SEGW], "y": [128, NCH * NG * CQ],
               "yng": [128, NCH * NG * CQ], "gffn": [128, NSEG * TL],
               "yngT": [128, NSEG * TL], "spre": [128, NG * NPG * NCH * KD],
               "bb": [16, NG * TL], "r": [128, NG * TL],
               "kw": [128, NG * TL], "vtm": [128, NCH * CQ],
               "vtm2": [128, NCH * CQ], "vtm3": [128, NCH * CQ],
               "esb": [128, 3 * NSEG],
               "bonus0": [128, CQ], "attm0": [128, 256],
               "rw0": [128, NPG * G]}[nm]
        dbg[nm] = nc.dram_tensor("dbg_" + nm, shp, F32, kind="ExternalOutput")

    t_ones128 = nc.inline_tensor(np.ones((128, 1), dtype=bfd), name="i_ones128")
    t_onesk1 = nc.inline_tensor(np.ones((1, 128), dtype=bfd), name="i_onesk1")

    def A(name, shape, dt):
        return nc.alloc_sbuf_tensor(name, shape, dt, side="left")

    iota = A("iota_sb", [128, TL], F32)
    cols = {n: A(n + "_sb", [128, NSEG], F32) for n in d_cols}
    tm1m = {n: A(n + "_1m", [128, NSEG], F32) for n in ("tmr", "tmk", "tmv", "tmg")}
    e_sb = A("e_sb", [128, NSEG], F32)
    ebias = A("ebias_sb", [128, NSEG], F32)
    eneg = A("eneg_sb", [128, NSEG], F32)
    lnw = A("lnw_sb", [128, C], BF16)
    lnb = A("lnb_sb", [128, C], BF16)
    wkvp = A("wkvp_sb", [128, NSEG * KD], BF16)
    mask2 = A("mask2_sb", [128, 256], BF16)
    lhsTu = A("lhsTu_sb", [128, NSEG * 16], BF16)
    sel = A("sel_sb", [128, 8], F32)
    hahb = A("hahb_sb", [128, 4], F32)
    xhalo = A("xhalo_sb", [128, NSEG], F32)
    o128 = A("o128_sb", [128, 1], BF16)
    ok1 = A("ok1_sb", [1, 128], BF16)
    gsink = A("gsink", [128, 1], F32)
    yngT = A("yngT", [128, NSEG * TL], BF16)
    xn = A("xn", [128, NSEG * SEGW], BF16)

    PJ = [nc.alloc_psum_tensor(f"pj{i}", [128, TL], F32) for i in range(4)]
    ss_sl = [[PJ[3][0:64, 0:KD], PJ[3][64:128, 512:512 + KD]],
             [PJ[2][0:64, 0:KD], PJ[2][64:128, 512:512 + KD]]]
    str0 = PJ[2][0:1, 0:512]
    str1 = PJ[3][0:1, 0:512]

    def Rg(es, name, shape, dt):
        return es.enter_context(nc.sbuf_tensor(name, shape, dt, side="right"))


    x3 = d_x.ap().rearrange("(s p) t -> p s t", p=128)
    S.dma("sy", iota[:], d_iota.ap(), inc="ld0")
    for n in d_cols:
        S.dma("sy", cols[n][:], d_cols[n].ap(), inc="ld0")
    S.dma("sy", sel[:], d_sel.ap(), inc="ld0")
    S.dma("sy", xhalo[:], d_xhalo.ap(), inc="ld0")
    S.dma("sy", hahb[:], d_hahb.ap(), inc="ld0")
    v_cols = v_iota = S.cnt["ld0"]
    S.dma("po", lnw[:], d_lnw.ap(), inc="w0")
    S.dma("po", lnb[:], d_lnb.ap(), inc="w0")
    S.dma("po", mask2[:], d_mask.ap(), inc="w0")
    S.dma("po", wkvp[:].rearrange("p (s k) -> p s k", s=NSEG),
          d_wkvp.ap().rearrange("(s p) k -> p s k", p=128), inc="w0")
    S.dma("po", lhsTu[:].rearrange("p (s h) -> p s h", s=NSEG),
          d_bones.ap().rearrange("(s p) h -> p s h", p=128), inc="w0")
    S.dma("po", o128[:], t_ones128.ap(), inc="w0")
    S.dma("po", ok1[:], t_onesk1.ap(), inc="w0")
    v_wconst = S.cnt["w0"]

    v_e = S.op("ac", lambda e: e.activation(e_sb[:], cols["dec"][:], AF.Exp),
               waits=[("ld0", v_cols)], inc="ac")
    S.op("dv", lambda e: e.tensor_scalar_mul(ebias[:], e_sb[:], -2047.0),
         waits=[("ac", v_e)], inc="dv")
    S.op("dv", lambda e: e.tensor_scalar_mul(eneg[:], e_sb[:], -1.0),
         waits=[("ac", v_e)], inc="dv")
    for n in tm1m:
        S.op("dv", lambda e, n=n: e.tensor_scalar(
            tm1m[n][:], cols[n][:], -1.0, 1.0, ALU.mult, ALU.add),
            waits=[("ld0", v_cols)], inc="dv")
    v_tiny = None
    for s in range(NSEG):
        v_tiny = S.op("dv", lambda e, s=s: e.tensor_scalar(
            lhsTu[:, s * 16:(s + 1) * 16], lhsTu[:, s * 16:(s + 1) * 16],
            cols["uu"][:, s:s + 1], None, ALU.mult),
            waits=[("w0", v_wconst), ("ld0", v_cols)], inc="dv")

    es_ln = contextlib.ExitStack()
    xpc = [Rg(es_ln, f"xpc{i}", [128, 512], F32) for i in range(4)]
    xbc = [Rg(es_ln, f"xbc{i}", [128, 512], BF16) for i in range(2)]
    xqc = [Rg(es_ln, f"xqc{i}", [128, 512], BF16) for i in range(2)]
    mrow = Rg(es_ln, "mrow", [1, SEGW], F32)
    e2row = Rg(es_ln, "e2row", [1, SEGW], F32)
    varrow = Rg(es_ln, "varrow", [1, SEGW], F32)
    mbf = Rg(es_ln, "mbf", [1, SEGW], BF16)
    rbf = Rg(es_ln, "rbf", [1, SEGW], BF16)

    chunks = [(0, 512), (512, 512), (1024, 1)]
    v_cpd, v_sqd, v_stev = {}, {}, {}
    v_rowev = 0
    for ci, (off, cw) in enumerate(chunks):
        for s in range(NSEG):
            idx = ci * NSEG + s
            pb, b2 = idx % 4, idx % 2
            wd = []
            if idx - 4 in v_cpd:
                wd += [("dv", v_cpd[idx - 4]), ("ac", v_sqd[idx - 4])]
            v_pc = S.dma("sy", xpc[pb][:, 0:cw], x3[:, s, off:off + cw],
                         inc=f"xp{pb}", waits=wd,
                         allow_slow_non_contiguous=(cw == 1))
            wst = [("pe", v_stev[idx - 2])] if idx - 2 in v_stev else []
            v_cpd[idx] = S.op("dv", lambda e, pb=pb, b2=b2, cw=cw: e.tensor_copy(
                xbc[b2][:, 0:cw], xpc[pb][:, 0:cw]),
                waits=[(f"xp{pb}", v_pc)] + wst, inc="dv")
            v_sqd[idx] = S.op("ac", lambda e, pb=pb, b2=b2, cw=cw: e.activation(
                xqc[b2][:, 0:cw], xpc[pb][:, 0:cw], AF.Square),
                waits=[(f"xp{pb}", v_pc)] + wst, inc="ac")

            def stat2(e, b2=b2, cw=cw, s=s):
                e.matmul(str0[:, 0:cw], o128[:], xbc[b2][:, 0:cw],
                         start=(s == 0), stop=(s == NSEG - 1))
                return e.matmul(str1[:, 0:cw], o128[:], xqc[b2][:, 0:cw],
                                start=(s == 0), stop=(s == NSEG - 1))
            w = [("dv", v_cpd[idx]), ("ac", v_sqd[idx]), ("w0", v_wconst)]
            if ci > 0 and s == 0:
                w.append(("dv", v_rowev))
            v_stev[idx] = S.op("pe", stat2, waits=w, inc="pe")
        v_pe = v_stev[ci * NSEG + NSEG - 1]
        S.op("dv", lambda e, off=off, cw=cw: e.tensor_scalar_mul(
            mrow[0:1, off:off + cw], str0[:, 0:cw], 1.0 / C),
            waits=[("pe", v_pe)], inc="dv")
        v_rowev = S.op("dv", lambda e, off=off, cw=cw: e.tensor_scalar_mul(
            e2row[0:1, off:off + cw], str1[:, 0:cw], 1.0 / C),
            waits=[("pe", v_pe)], inc="dv")

    S.op("dv", lambda e: e.tensor_mul(varrow[:], mrow[:], mrow[:]),
         waits=[("dv", v_rowev)], inc="dv")
    v_var = S.op("dv", lambda e: e.tensor_sub(varrow[:], e2row[:], varrow[:]),
                 inc="dv")
    v_std = S.op("ac", lambda e: e.activation(varrow[:], varrow[:], AF.Sqrt,
                                              bias=hahb[0:1, 2:3]),
                 waits=[("dv", v_var), ("ld0", v_cols)], inc="ac")
    v_rec = S.op("dv", lambda e: e.reciprocal(varrow[:], varrow[:]),
                 waits=[("ac", v_std)], inc="dv")
    S.op("dv", lambda e: e.tensor_copy(mbf[:], mrow[:]), inc="dv")
    v_rows = S.op("dv", lambda e: e.tensor_copy(rbf[:], varrow[:]),
                  waits=[("dv", v_rec)], inc="dv")

    v_apl = {}
    for ci, (off, cw) in enumerate(chunks):
        def bcast(e, off=off, cw=cw):
            e.matmul(PJ[0][:, 0:cw], ok1[:], mbf[0:1, off:off + cw],
                     start=True, stop=True)
            return e.matmul(PJ[1][:, 0:cw], ok1[:], rbf[0:1, off:off + cw],
                            start=True, stop=True)
        w = [("dv", v_rows)]
        if ci > 0:
            w.append(("dv", v_apl[(ci - 1) * NSEG + NSEG - 1]))
        v_bc = S.op("pe", bcast, waits=w, inc="pe")
        for s in range(NSEG):
            idx = ci * NSEG + s
            pb, b2 = idx % 4, idx % 2
            if idx - 4 in v_apl:
                wd = [("dv", v_apl[idx - 4])]
            else:
                wd = [("dv", v_cpd[44 + pb]), ("ac", v_sqd[44 + pb])]
            v_pc = S.dma("sy", xpc[pb][:, 0:cw], x3[:, s, off:off + cw],
                         inc=f"xp{pb}", waits=wd,
                         allow_slow_non_contiguous=(cw == 1))
            S.op("dv", lambda e, pb=pb, b2=b2, cw=cw: e.tensor_tensor(
                xbc[b2][:, 0:cw], xpc[pb][:, 0:cw], PJ[0][:, 0:cw], ALU.subtract),
                waits=[(f"xp{pb}", v_pc), ("pe", v_bc)], inc="dv")
            v_apl[idx] = S.op("dv", lambda e, s=s, b2=b2, off=off, cw=cw:
                              e.tensor_tensor(
                xn[:, s * SEGW + off: s * SEGW + off + cw], xbc[b2][:, 0:cw],
                PJ[1][:, 0:cw], ALU.mult), inc="dv")
    v_xnraw = v_apl[2 * NSEG + NSEG - 1]
    hv = xn[:].rearrange("p (s t) -> p s t", s=NSEG)[:, :, 0]
    S.op("dv", lambda e: e.tensor_scalar(hv, hv, hahb[:, 0:1], None, ALU.mult),
         waits=[("dv", v_xnraw), ("ld0", v_cols)], inc="dv")
    v_xn = S.op("dv", lambda e: e.scalar_tensor_tensor(
        hv, xhalo[:], hahb[:, 1:2], hv, ALU.mult, ALU.add), inc="dv")
    if TRUNC <= 1:
        S.emit()
        return nc
    es_ln.close()

    es_g = contextlib.ExitStack()
    wslice = [Rg(es_g, f"wslice{i}", [128, 512], BF16) for i in range(2)]
    tmpb = [Rg(es_g, f"tmpb{i}", [128, TL], BF16) for i in range(2)]
    vg_tmp = [Rg(es_g, f"vg_tmp{i}", [128, TL], BF16) for i in range(2)]
    r_pl = Rg(es_g, "r_pl", [128, NPG * TL], BF16)
    kw_pl = Rg(es_g, "kw_pl", [128, NPG * TL], BF16)
    vtm_pl = Rg(es_g, "vtm_pl", [128, NCH * CQ], BF16)
    gtm_pl = Rg(es_g, "gtm_pl", [128, NCH * CQ], BF16)
    ruk_pl = Rg(es_g, "ruk_pl", [128, NPG * TL], BF16)
    wfwb = [Rg(es_g, f"wfwb{i}", [128, TL], BF16) for i in range(2)]
    rwbuf = Rg(es_g, "rwbuf", [128, NPG * G], BF16)
    spre = Rg(es_g, "spre", [128, NPG * NCH * KD], BF16)
    sfin = Rg(es_g, "sfin", [128, NPG * KD], F32)
    shalf = Rg(es_g, "shalf", [128, NPG * KD], F32)
    recv = Rg(es_g, "recv", [128, 8 * NPG * KD], F32)
    bb_sb = Rg(es_g, "bb_sb", [16, TL], BF16)
    kwT = [Rg(es_g, f"kwT{i}", [128, 128], BF16) for i in range(4)]
    attm = [Rg(es_g, f"attm{i}", [128, 256], BF16) for i in range(2)]
    bbT = [Rg(es_g, f"bbT{i}", [128, 16], BF16) for i in range(2)]
    bonus = [Rg(es_g, f"bonus{i}", [128, CQ], BF16) for i in range(2)]
    ybuf = [Rg(es_g, f"ybuf{i}", [128, CQ], BF16) for i in range(2)]
    y2buf = Rg(es_g, "y2buf", [128, CQ], BF16)
    tgn = [Rg(es_g, f"tgn{i}", [128, CQ], BF16) for i in range(2)]
    gn8 = [Rg(es_g, f"gn8_{i}", [128, HPG], F32) for i in range(5)]
    dbgb = Rg(es_g, "dbgb", [128, TL], F32) if DEBUG else None

    def dump_plane(nm, src, width, dst_off=0, waits=()):
        if nm not in dbg:
            return
        np_ = src.shape[0]
        nchunk = (width + TL - 1) // TL
        for i in range(nchunk):
            lo = i * TL
            hi = min(width, lo + TL)
            vv = S.op("dv", lambda e, lo=lo, hi=hi, np_=np_: e.tensor_copy(
                dbgb[0:np_, 0:hi - lo], src[0:np_, lo:hi]),
                waits=list(waits) + [("sto", S.cnt.get("sto", 0))], inc="dv")
            S.dma("sy", dbg[nm].ap()[0:np_, dst_off + lo:dst_off + hi],
                  dbgb[0:np_, 0:hi - lo], inc="sto", waits=[("dv", vv)])

    dump_plane("xn", xn, NSEG * SEGW, waits=[("dv", v_xn)])

    pj_free = [("dv", v_apl[2 * NSEG + NSEG - 1]),
               ("dv", v_apl[2 * NSEG + NSEG - 1]),
               ("dv", v_rowev), ("dv", v_rowev)]

    def emit_proj(wt_dram, GG, nm, v_free_blendsrc, evac, extra_kt0=(),
                  first_waits=()):
        v_pe_kt = {}
        v_bl = {}
        v_wd = {}
        for kt in range(NSEG):
            b = kt % 2
            wdw = ([("pe", v_pe_kt[kt - 2])] if kt - 2 in v_pe_kt
                   else list(first_waits))
            v_wd[kt] = S.dma("po", wslice[b][:],
                             wt_dram.ap()[kt * 128:(kt + 1) * 128,
                                          GG * 512:(GG + 1) * 512],
                             inc=f"ws{b}", waits=wdw)
            cur = xn[:, kt * SEGW + 1:(kt + 1) * SEGW]
            prev = xn[:, kt * SEGW:(kt + 1) * SEGW - 1]
            wb = [("dv", v_xn), ("pe", v_free_blendsrc)]
            if kt - 2 in v_pe_kt:
                wb = [("pe", v_pe_kt[kt - 2])]
            vd = S.op("po", lambda e, b=b, cur=cur, prev=prev: e.tensor_tensor(
                tmpb[b][:], cur, prev, ALU.subtract), waits=wb, inc="po")
            va = S.op("ac", lambda e, b=b, kt=kt: e.activation(
                tmpb[b][:], tmpb[b][:], AF.Copy, scale=cols[nm][:, kt:kt + 1]),
                waits=[("po", vd)], inc="ac")
            v_bl[kt] = ("dv", S.op("dv", lambda e, b=b, prev=prev:
                                   e.tensor_add(tmpb[b][:], tmpb[b][:], prev),
                                   waits=[("ac", va)], inc="dv"))

            def mms(e, b=b, kt=kt):
                ins = None
                for ob in range(NPG):
                    lhsT = wslice[b][:, ob * 128:(ob + 1) * 128]
                    for hf in range(2):
                        ins = e.matmul(PJ[ob][:, hf * 512:(hf + 1) * 512], lhsT,
                                       tmpb[b][:, hf * 512:(hf + 1) * 512],
                                       start=(kt == 0), stop=(kt == NSEG - 1))
                return ins
            w = [(f"ws{kt % 2}", v_wd[kt]), v_bl[kt]]
            if kt == 0:
                w += list(pj_free) + list(extra_kt0)
            v_pe_kt[kt] = S.op("pe", mms, waits=w, inc="pe")
        v_last = v_pe_kt[NSEG - 1]
        evs = []
        for i in range(NPG):
            evs.append(evac(i, PJ[i], [("pe", v_last)]))
            pj_free[i] = evs[i]
        return evs, v_last

    v_allmm_prev = 0
    v_evac_prev = None
    v_sfin_prev = 0
    v_selsum_prev = 0
    v_y5T_prev = 0
    v_red_prev = 0
    v_gnpool_prev = 0
    v_bt_prev = [0, 0]
    v_ci_prev = 0
    v_evac = {}

    for GG in range(NG):
        def evr(i, pj, w):
            eng = "ac" if i < 2 else "dv"
            if eng == "ac":
                v = S.op("ac", lambda e, i=i, pj=pj: e.activation(
                    r_pl[:, i * TL:(i + 1) * TL], pj[:], AF.Copy),
                    waits=w, inc="ac")
            else:
                v = S.op("dv", lambda e, i=i, pj=pj: e.tensor_copy(
                    r_pl[:, i * TL:(i + 1) * TL], pj[:]), waits=w, inc="dv")
            return (eng, v)
        evs, v_pe_r = emit_proj(d_w["wrt"], GG, "tmr", v_allmm_prev, evr,
                                extra_kt0=[v_evac_prev] if v_evac_prev else [],
                                first_waits=[("dv", v_xn)] if GG == 0 else
                                [("pe", v_allmm_prev)])
        v_r_ac = max([v for d, v in evs if d == "ac"], default=0)
        v_r_dv = max([v for d, v in evs if d == "dv"], default=0)
        v_r = [("ac", v_r_ac), ("dv", v_r_dv)]
        if TRUNC <= 2:
            S.emit()
            return nc

        def evk(i, pj, w):
            eng = "ac" if i < 2 else "dv"
            if eng == "ac":
                v = S.op("ac", lambda e, i=i, pj=pj: e.activation(
                    ruk_pl[:, i * TL:(i + 1) * TL], pj[:], AF.Copy),
                    waits=w, inc="ac")
            else:
                v = S.op("dv", lambda e, i=i, pj=pj: e.tensor_copy(
                    ruk_pl[:, i * TL:(i + 1) * TL], pj[:]), waits=w, inc="dv")
            return (eng, v)
        evs, v_pe_k = emit_proj(d_w["wkt"], GG, "tmk", v_pe_r, evk,
                                first_waits=[("pe", v_pe_r)])
        v_kc_ac = max([v for d, v in evs if d == "ac"], default=0)
        v_kc_dv = max([v for d, v in evs if d == "dv"], default=0)

        v_kwd = {}
        for i in range(NPG):
            b = i % 2
            wwb = [("dv", v_kwd[i - 2])] if i - 2 in v_kwd else []
            v_wb = S.op("ac", lambda e, i=i, b=b, GG=GG: e.activation(
                wfwb[b][:], iota[:], AF.Exp,
                bias=ebias[:, GG * NPG + i: GG * NPG + i + 1],
                scale=e_sb[:, GG * NPG + i: GG * NPG + i + 1]),
                waits=wwb + [("ld0", v_iota), ("dv", v_tiny),
                             ("ac", v_kc_ac), ("dv", v_kc_dv)], inc="ac")
            v_kwd[i] = S.op("dv", lambda e, i=i, b=b: e.tensor_tensor(
                kw_pl[:, i * TL:(i + 1) * TL], ruk_pl[:, i * TL:(i + 1) * TL],
                wfwb[b][:], ALU.mult), waits=[("ac", v_wb)], inc="dv")
        v_ruk = {}
        for i in range(NPG):
            v_ruk[i] = S.op("dv", lambda e, i=i: e.tensor_tensor(
                ruk_pl[:, i * TL:(i + 1) * TL], ruk_pl[:, i * TL:(i + 1) * TL],
                r_pl[:, i * TL:(i + 1) * TL], ALU.mult),
                waits=[("dv", v_kwd[i])] + v_r, inc="dv")
        v_k = S.cnt["dv"]

        def bbmm(e, GG=GG):
            ins = None
            for hf in range(2):
                for li in range(NPG):
                    ins = e.matmul(PJ[1][0:16, hf * 512:(hf + 1) * 512],
                                   lhsTu[:, (GG * NPG + li) * 16:
                                         (GG * NPG + li + 1) * 16],
                                   ruk_pl[:, li * TL + hf * 512:
                                          li * TL + hf * 512 + 512],
                                   start=(li == 0), stop=(li == NPG - 1))
            return ins
        v_bbpe = S.op("pe", bbmm,
                      waits=[("dv", v_ruk[NPG - 1])] + [pj_free[1]], inc="pe")
        v_bb = S.op("dv", lambda e: e.tensor_copy(bb_sb[:], PJ[1][0:16, 0:TL]),
                    waits=[("pe", v_bbpe), ("bt0", v_bt_prev[0]),
                           ("bt1", v_bt_prev[1])], inc="dv")
        if TRUNC <= 3:
            S.emit()
            return nc
        pj_free[1] = ("dv", v_bb)
        dump_plane("bb", bb_sb, TL, dst_off=GG * TL, waits=[("dv", v_bb)])

        v_vt = {}

        def evv(i, pj, w):
            b = i % 2
            wv = list(w)
            if i - 2 in v_vt:
                wv.append((f"vt{b}", v_vt[i - 2]))
            v1 = S.op("dv", lambda e, b=b, pj=pj: e.tensor_copy(
                vg_tmp[b][:], pj[:]), waits=wv, inc="dv")
            for n in range(NCH):
                v_vt[i] = S.tpose(
                    vtm_pl[:, n * CQ + i * 128: n * CQ + i * 128 + 128],
                    vg_tmp[b][:, n * 128:(n + 1) * 128],
                    inc=f"vt{b}", waits=[("dv", v1)])
            return ("dv", v1)
        evs, v_pe_v = emit_proj(d_w["wvt"], GG, "tmv", v_bbpe, evv,
                                first_waits=[("pe", v_pe_k)])
        if GG == 0:
            dump_plane("vtm2", vtm_pl, NCH * CQ,
                       waits=[("vt0", S.cnt.get("vt0", 0)),
                              ("vt1", S.cnt.get("vt1", 0))])
        v_vtm2 = [("vt0", S.cnt.get("vt0", 0)), ("vt1", S.cnt.get("vt1", 0))]
        v_v = evs[NPG - 1]

        z = spre[:].rearrange("p (l n k) -> p l n k", l=NPG, n=NCH)[:, :, 0, :]
        v_z = S.op("dv", lambda e: e.memset(z, 0.0),
                   waits=[("pe", v_allmm_prev)], inc="dv")
        v_zf = S.op("dv", lambda e: e.memset(sfin[:], 0.0),
                    waits=[("cci", v_ci_prev)], inc="dv")
        v_supd, v_ssev = {}, {}
        v_sfin = None
        for li in range(NPG):
            ss = ss_sl[li % 2]
            sfs = sfin[:, li * KD:(li + 1) * KD]
            for n in range(NCH):
                j = li * NCH + n
                kb = j % 4
                wt = [("dv", v_kwd[li])] + v_vtm2
                if j - 3 in v_supd:
                    wt.append(("pe", v_supd[j - 3]))
                v_t = S.tpose(kwT[kb][:], kw_pl[:, li * TL + n * 128:
                                                li * TL + n * 128 + 128],
                              inc=f"kwt{kb}", waits=wt)

                def supd(e, li=li, n=n, kb=kb, ss=ss):
                    ins = None
                    for hh in range(2):
                        ins = e.matmul(ss[hh],
                                       kwT[kb][:, hh * 64:(hh + 1) * 64],
                                       vtm_pl[:, n * CQ + li * 128 + hh * 64:
                                              n * CQ + li * 128 + hh * 64 + 64],
                                       start=True, stop=True,
                                       tile_position=(0, hh * 64))
                    return ins
                w = [(f"kwt{kb}", v_t)]
                if (li, n - 1) in v_ssev:
                    w.append(("dv", v_ssev[(li, n - 1)]))
                if n == 0:
                    w.append(("dv", v_zf))
                    if (li - 2, NCH - 1) in v_ssev:
                        w.append(("dv", v_ssev[(li - 2, NCH - 1)]))
                    elif v_sfin_prev:
                        w.append(("dv", v_sfin_prev))
                    if li < 2:
                        w.append(pj_free[3])
                        w.append(pj_free[2])
                v_supd[j] = S.op("pe", supd, waits=w, inc="pe")
                S.op("dv", lambda e, ss=ss, sfs=sfs: e.tensor_add(
                    sfs[0:64, :], sfs[0:64, :], ss[0]),
                    waits=[("pe", v_supd[j])], inc="dv")
                vadd = S.op("dv", lambda e, ss=ss, sfs=sfs: e.tensor_add(
                    sfs[64:128, :], sfs[64:128, :], ss[1]), inc="dv")
                if n < NCH - 1:
                    v_ssev[(li, n)] = S.op(
                        "dv", lambda e, li=li, n=n, sfs=sfs: e.tensor_copy(
                            spre[:, (li * NCH + n + 1) * KD:
                                 (li * NCH + n + 2) * KD], sfs),
                        inc="dv")
                else:
                    v_ssev[(li, n)] = vadd
                    v_sfin = vadd
        v_sfin_prev = v_sfin
        pj_free[3] = ("dv", v_sfin)
        pj_free[2] = ("dv", v_sfin)
        if TRUNC <= 4:
            S.emit()
            return nc

        v_ci = S.dma("po", d_ccin.ap().rearrange("(l p) k -> p l k", p=128),
                     sfin[:].rearrange("p (l k) -> p l k", l=NPG),
                     inc="cci", waits=[("dv", v_sfin)])
        v_ci_prev = v_ci
        v_cc = S.op("po", lambda e: e.collective_compute(
            "AllGather", ALU.bypass, replica_groups=[list(range(8))],
            ins=[d_ccin.ap().opt()], outs=[d_ccout.ap().opt()]),
            waits=[("cci", v_ci)], inc="cc")
        v_rc = S.dma("po", recv[:].rearrange("p (q l k) -> p q l k", q=8, l=NPG),
                     d_ccout.ap().rearrange("(q l p) k -> p q l k", p=128, l=NPG),
                     inc="ccr", waits=[("cc", v_cc), ("dv", v_selsum_prev)])
        S.op("dv", lambda e: e.tensor_scalar(
            shalf[:], recv[:, 0:NPG * KD], sel[:, 0:1], None, ALU.mult),
            waits=[("ccr", v_rc), ("ld0", v_cols)], inc="dv")
        for rr in range(1, 8):
            v_selsum_prev = S.op("dv", lambda e, rr=rr: e.scalar_tensor_tensor(
                shalf[:], recv[:, rr * NPG * KD:(rr + 1) * NPG * KD],
                sel[:, rr:rr + 1], shalf[:], ALU.mult, ALU.add), inc="dv")
        spv = spre[:].rearrange("p (l n k) -> p l n k", l=NPG, n=NCH)
        shv = shalf[:].rearrange("p (l k) -> p l k", l=NPG).unsqueeze(2) \
            .broadcast_to([128, NPG, NCH, KD])
        v_spre = S.op("dv", lambda e: e.tensor_tensor(spv, spv, shv, ALU.add),
                      waits=[("dv", v_z), ("dv", v_sfin)], inc="dv")
        dump_plane("spre", spre, NPG * NCH * KD, dst_off=GG * NPG * NCH * KD,
                   waits=[("dv", v_spre)])
        if GG == 0:
            dump_plane("vtm3", vtm_pl, NCH * CQ, waits=[("dv", v_spre)])
            dump_plane("esb", e_sb, NSEG, dst_off=0, waits=[("dv", v_tiny)])
            dump_plane("esb", eneg, NSEG, dst_off=NSEG, waits=[("dv", v_tiny)])
            dump_plane("esb", ebias, NSEG, dst_off=2 * NSEG,
                       waits=[("dv", v_tiny)])
        if TRUNC <= 5:
            S.emit()
            return nc

        v_gt = {}

        def evg(i, pj, w):
            b = i % 2
            wv = list(w)
            if i - 2 in v_gt:
                wv.append((f"gt{i % 2}", v_gt[i - 2]))
            va = S.op("ac", lambda e, b=b, pj=pj: e.activation(
                vg_tmp[b][:], pj[:], AF.Silu), waits=wv, inc="ac")
            for n in range(NCH):
                v_gt[i] = S.tpose(
                    gtm_pl[:, n * CQ + i * 128: n * CQ + i * 128 + 128],
                    vg_tmp[b][:, n * 128:(n + 1) * 128],
                    inc=f"gt{b}", waits=[("ac", va)])
            return ("ac", va)
        evs, v_pe_g = emit_proj(d_w["wgt"], GG, "tmg", v_pe_v, evg,
                                first_waits=[("pe", v_pe_v)])
        v_gtm2 = [("gt0", S.cnt.get("gt0", 0)), ("gt1", S.cnt.get("gt1", 0))]
        if GG == 0:
            dump_plane("vtm", vtm_pl, NCH * CQ,
                       waits=[("pe", v_pe_g), ("gt0", S.cnt.get("gt0", 0)),
                              ("gt1", S.cnt.get("gt1", 0))])
            dump_plane("rw0", r_pl, NPG * G,
                       waits=[("pe", v_pe_g), ("gt0", S.cnt.get("gt0", 0)),
                              ("gt1", S.cnt.get("gt1", 0))])
        if TRUNC <= 51:
            S.emit()
            return nc
        v_g_ac = max(v for d, v in evs)

        dump_plane("r", r_pl, TL, dst_off=GG * TL, waits=v_r)
        dump_plane("kw", kw_pl, TL, dst_off=GG * TL,
                   waits=[("dv", v_kwd[NPG - 1])])

        v_mask = {}
        v_evac = {}
        v_pairev = {}
        v_y2c = {}
        v_t1c = {}
        for n in range(NCH):
            pj = PJ[n % 2]
            v_rwp = {}
            for li in range(NPG):
                s = GG * NPG + li
                va = S.op("ac", lambda e, li=li, s=s, n=n: e.activation(
                    rwbuf[:, li * G:(li + 1) * G], iota[:, n * G:(n + 1) * G],
                    AF.Exp, scale=eneg[:, s:s + 1]),
                    waits=[("dv", v_tiny), ("ld0", v_iota),
                           ("pe", v_allmm_prev)], inc="ac")
                v_rwp[li] = S.op("dv", lambda e, li=li, n=n: e.tensor_tensor(
                    rwbuf[:, li * G:(li + 1) * G],
                    r_pl[:, li * TL + n * G: li * TL + (n + 1) * G],
                    rwbuf[:, li * G:(li + 1) * G], ALU.mult),
                    waits=[("ac", va)] + v_r, inc="dv")
            nb = n % 2
            hoff = (GG % 2) * 8
            wbt = [("dv", v_bb)]
            if n - 2 in v_evac:
                wbt.append(("dv", v_evac[n - 2]))
            v_bt = S.tpose(bbT[nb][:], bb_sb[0:16, n * 128:(n + 1) * 128],
                           inc=f"bt{nb}", waits=wbt)
            v_bt_prev[nb] = v_bt
            v_bo = S.op("dv", lambda e, n=n, nb=nb, hoff=hoff: e.tensor_tensor(
                bonus[nb][:].rearrange("p (h k) -> p h k", h=HPG),
                vtm_pl[:, n * CQ:(n + 1) * CQ].rearrange(
                    "p (h k) -> p h k", h=HPG),
                bbT[nb][:, hoff:hoff + HPG].unsqueeze(2)
                .broadcast_to([128, HPG, KD]), ALU.mult),
                waits=[(f"bt{nb}", v_bt)] + v_vtm2, inc="dv")
            if TRUNC <= 522 and n == 0:
                S.emit()
                return nc
            for li in range(NPG):
                j = n * NPG + li
                ab = j % 2
                att = PJ[2 + ab]

                def attmm(e, li=li, n=n, att=att):
                    ins = None
                    for hh in range(2):
                        ins = e.matmul(att[:, hh * 512: hh * 512 + 128],
                                       kw_pl[hh * 64:(hh + 1) * 64,
                                             li * TL + n * G:
                                             li * TL + (n + 1) * G],
                                       r_pl[hh * 64:(hh + 1) * 64,
                                            li * TL + n * G:
                                            li * TL + (n + 1) * G],
                                       start=True, stop=True,
                                       tile_position=(hh * 64, 0))
                    return ins
                w = [("dv", v_kwd[li])] + v_r
                if j - 2 in v_mask:
                    w.append(("dv", v_mask[j - 2]))
                elif n == 0 and li < 2:
                    w.append(pj_free[2 + ab])
                v_apc = S.op("pe", attmm, waits=w, inc="pe")
                mb = li % 2
                attv2 = att[:].rearrange("p (b c) -> p b c", b=2)[:, :, 0:128]
                v_mask[j] = S.op("dv", lambda e, mb=mb, attv2=attv2:
                                 e.tensor_tensor(
                    attm[mb][:].rearrange("p (b c) -> p b c", b=2), attv2,
                    mask2[:].rearrange("p (b c) -> p b c", b=2), ALU.mult),
                    waits=[("pe", v_apc), ("w0", v_wconst)], inc="dv")

                if TRUNC <= 523 and n == 0 and li == 1:
                    S.emit()
                    return nc

                def wkvmm(e, li=li, n=n, mb=mb, GG=GG):
                    sl = PJ[(n * NPG + li) % 2]
                    ins = None
                    for hh in range(2):
                        e.matmul(sl[:, hh * 512: hh * 512 + 64],
                                 attm[mb][:, hh * 128:(hh + 1) * 128],
                                 vtm_pl[:, n * CQ + li * 128 + hh * 64:
                                        n * CQ + li * 128 + hh * 64 + 64],
                                 start=True, stop=False)
                        e.matmul(sl[:, hh * 512: hh * 512 + 64],
                                 r_pl[hh * 64:(hh + 1) * 64,
                                      li * TL + n * G: li * TL + (n + 1) * G],
                                 spre[hh * 64:(hh + 1) * 64,
                                      (li * NCH + n) * KD:
                                      (li * NCH + n + 1) * KD],
                                 start=False, stop=False,
                                 tile_position=(hh * 64, 0))
                        ins = e.matmul(sl[:, hh * 512: hh * 512 + 64],
                                       rwbuf[hh * 64:(hh + 1) * 64,
                                             li * G:(li + 1) * G],
                                       wkvp[hh * 64:(hh + 1) * 64,
                                            (GG * NPG + li) * KD:
                                            (GG * NPG + li + 1) * KD],
                                       start=False, stop=True,
                                       tile_position=(hh * 64, 0))
                    return ins
                w = [("dv", v_mask[j]), ("dv", v_rwp[li]), ("dv", v_spre),
                     ("w0", v_wconst)] + v_vtm2
                js = n * NPG + li
                if js - 2 in v_pairev:
                    w.append(("dv", v_pairev[js - 2]))
                elif js < 2:
                    w.append(pj_free[0])
                v_wk = S.op("pe", wkvmm, waits=w, inc="pe")
                if TRUNC <= 524 and n == 0 and li == 0:
                    S.emit()
                    return nc
                wev = []
                if li == 0 and n - 2 in v_y2c:
                    wev = [("ac", v_y2c[n - 2]), ("po", v_t1c[n - 2])]
                slv = PJ[js % 2][:].rearrange("p (b c) -> p b c", b=2)[:, :, 0:64]
                v_pairev[js] = S.op(
                    "dv", lambda e, li=li, nb=n % 2, slv=slv: e.tensor_tensor(
                        ybuf[nb][:, li * 128:(li + 1) * 128].rearrange(
                            "p (b c) -> p b c", b=2),
                        slv,
                        bonus[nb][:, li * 128:(li + 1) * 128].rearrange(
                            "p (b c) -> p b c", b=2), ALU.add),
                    waits=[("pe", v_wk), ("dv", v_bo)] + wev, inc="dv")
            v_allmm = S.cnt["pe"]
            v_allmm_prev = v_allmm
            v_evac[n] = v_pairev[n * NPG + NPG - 1]
            if TRUNC <= 52 and n == 0:
                S.emit()
                return nc
            dump_plane("y", ybuf[nb], CQ, dst_off=(GG * NCH + n) * CQ,
                       waits=[("dv", v_evac[n])])

            v_y2 = S.op("ac", lambda e, nb=nb: e.activation(
                y2buf[:], ybuf[nb][:], AF.Square),
                waits=[("dv", v_evac[n]), ("dv", v_red_prev)], inc="ac")
            v_y2c[n] = v_y2
            yv = ybuf[nb][:].rearrange("p (h k) -> p h k", h=HPG)
            S.op("dv", lambda e, yv=yv: e.tensor_reduce(
                gn8[0][:], yv, AX.X, ALU.add),
                waits=[("dv", v_evac[n]), ("po", v_gnpool_prev)], inc="dv")
            S.op("dv", lambda e: e.tensor_reduce(
                gn8[1][:], y2buf[:].rearrange("p (h k) -> p h k", h=HPG),
                AX.X, ALU.add), waits=[("ac", v_y2)], inc="dv")
            S.op("dv", lambda e: e.tensor_scalar_mul(gn8[0][:], gn8[0][:],
                                                     1.0 / KD), inc="dv")
            S.op("dv", lambda e: e.tensor_scalar_mul(gn8[1][:], gn8[1][:],
                                                     1.0 / KD), inc="dv")
            S.op("dv", lambda e: e.tensor_mul(gn8[2][:], gn8[0][:], gn8[0][:]),
                 inc="dv")
            v_gv = S.op("dv", lambda e: e.tensor_sub(gn8[2][:], gn8[1][:],
                                                     gn8[2][:]), inc="dv")
            v_gs = S.op("ac", lambda e: e.activation(gn8[3][:], gn8[2][:],
                                                     AF.Sqrt, bias=hahb[:, 3:4]),
                        waits=[("dv", v_gv)], inc="ac")
            v_gr = S.op("dv", lambda e: e.reciprocal(gn8[4][:], gn8[3][:]),
                        waits=[("ac", v_gs)], inc="dv")
            v_red_prev = v_gr
            if TRUNC <= 53 and n == 0:
                S.emit()
                return nc
            muv = gn8[0][:].unsqueeze(2).broadcast_to([128, HPG, KD])
            rsv = gn8[4][:].unsqueeze(2).broadcast_to([128, HPG, KD])
            v_t1 = S.op("po", lambda e, yv=yv, muv=muv: e.tensor_tensor(
                tgn[0][:].rearrange("p (h k) -> p h k", h=HPG), yv, muv,
                ALU.subtract),
                waits=[("dv", v_gr), ("yngt", v_y5T_prev)], inc="po")
            v_t1c[n] = v_t1
            v_t2 = S.op("po", lambda e, rsv=rsv: e.tensor_tensor(
                tgn[1][:].rearrange("p (h k) -> p h k", h=HPG),
                tgn[0][:].rearrange("p (h k) -> p h k", h=HPG), rsv, ALU.mult),
                inc="po")
            lo = GG * CQ
            v_t3 = S.op("po", lambda e, lo=lo: e.tensor_tensor(
                tgn[0][:], tgn[1][:], lnw[:, lo:lo + CQ], ALU.mult),
                waits=[("w0", v_wconst)], inc="po")
            v_gnpool_prev = v_t3
            v_t4 = S.op("dv", lambda e, lo=lo: e.tensor_tensor(
                tgn[1][:], tgn[0][:], lnb[:, lo:lo + CQ], ALU.add),
                waits=[("po", v_t3)], inc="dv")
            v_y5 = S.op("dv", lambda e, n=n: e.tensor_tensor(
                tgn[0][:], tgn[1][:], gtm_pl[:, n * CQ:(n + 1) * CQ], ALU.mult),
                waits=v_gtm2 + [("dv", v_t4), ("ac", v_g_ac)], inc="dv")
            if TRUNC <= 54 and n == 0:
                S.emit()
                return nc
            for li in range(NPG):
                v_y5T_prev = S.tpose(
                    yngT[:, (GG * NPG + li) * TL + n * 128:
                         (GG * NPG + li) * TL + n * 128 + 128],
                    tgn[0][:, li * 128:(li + 1) * 128],
                    inc="yngt", waits=[("dv", v_y5)])
            dump_plane("yng", tgn[0], CQ, dst_off=(GG * NCH + n) * CQ,
                       waits=[("dv", v_y5)])
        if TRUNC <= 6:
            S.emit()
            return nc
        v_evac_prev = ("dv", v_evac[NCH - 1])
        last = NCH * NPG - 1
        pj_free[0] = ("dv", v_pairev[last - 1])
        pj_free[1] = ("dv", v_pairev[last])
        pj_free[2] = ("dv", v_mask[last - 1])
        pj_free[3] = ("dv", v_mask[last])
    v_yngT = S.cnt["yngt"]
    dump_plane("yngT", yngT, NSEG * TL, waits=[("yngt", v_yngT)])

    v_dbg_done = S.cnt["dv"]
    era_gate = [("pe", S.cnt["pe"]), ("yngt", S.cnt["yngt"]),
                ("po", S.cnt["po"]), ("ac", S.cnt["ac"]),
                ("dv", v_dbg_done)]
    es_g.close()

    es_f = contextlib.ExitStack()
    gffn = Rg(es_f, "gffn", [128, NSEG * TL], BF16)
    wband = [Rg(es_f, f"wband{i}", [128, NSEG * 128], BF16) for i in range(2)]
    qs = [Rg(es_f, f"qs{i}", [128, C], F32) for i in range(2)]
    qnat = [Rg(es_f, f"qnat{i}", [128, C], BF16) for i in range(2)]
    qTs = [Rg(es_f, f"qTs{i}", [128, NSEG * 128], BF16) for i in range(2)]
    qred = [Rg(es_f, f"qred{i}", [128, 1], F32) for i in range(4)]
    x2b = [Rg(es_f, f"x2b{i}", [128, TL], F32) for i in range(2)]
    outb = [Rg(es_f, f"outb{i}", [128, TL], F32) for i in range(2)]

    v_ffn, v_qpe, v_ts, v_cmpn = {}, {}, {}, {}
    for ot in range(NSEG):
        b = ot % 2
        wq = ([("dv", v_cmpn[ot - 2])] if ot - 2 in v_cmpn
              else list(era_gate))
        v_qd = S.dma("sy", qs[b][:], d_qw.ap()[ot * 128:(ot + 1) * 128, :],
                     inc=f"q{b}", waits=wq)
        v_red = S.op("dv", lambda e, b=b: e.tensor_reduce(
            qred[b][:], qs[b][:], AX.X, ALU.add, apply_absolute_value=True),
            waits=[(f"q{b}", v_qd)], inc="dv")
        S.op("dv", lambda e, b=b: e.tensor_scalar_mul(
            qred[b][:], qred[b][:], 0.7 / C), waits=[("dv", v_red)], inc="dv")
        v_nth = S.op("dv", lambda e, b=b: e.tensor_scalar_mul(
            qred[2 + b][:], qred[b][:], -1.0), inc="dv")
        v_p1 = S.op("dv", lambda e, b=b: e.tensor_scalar(
            qnat[b][:], qs[b][:], qred[b][:, 0:1], None, ALU.is_gt),
            waits=[("dv", v_nth)] +
            ([(f"qt{b}", v_ts[ot - 2])] if ot - 2 in v_ts else []), inc="dv")
        v_p2 = S.op("dv", lambda e, b=b: e.tensor_scalar(
            qTs[b][:], qs[b][:], qred[2 + b][:, 0:1], None, ALU.is_lt),
            waits=([("pe", v_qpe[ot - 2])] if ot - 2 in v_qpe else []),
            inc="dv")
        v_cmpn[ot] = v_p2
        v_qn = S.op("dv", lambda e, b=b: e.tensor_sub(
            qnat[b][:], qnat[b][:], qTs[b][:]), waits=[("dv", v_p2)], inc="dv")
        for kt in range(NSEG):
            v_ts[ot] = S.tpose(qTs[b][:, kt * 128:(kt + 1) * 128],
                               qnat[b][:, kt * 128:(kt + 1) * 128],
                               inc=f"qt{b}", waits=[("dv", v_qn)])
        pj = PJ[ot % 2]

        def qmms(e, b=b, pj=pj):
            ins = None
            for kt in range(NSEG):
                lhsT = qTs[b][:, kt * 128:(kt + 1) * 128]
                for hf in range(2):
                    ins = e.matmul(pj[:, hf * 512:(hf + 1) * 512], lhsT,
                                   xn[:, kt * SEGW + 1 + hf * 512:
                                      kt * SEGW + 1 + hf * 512 + 512],
                                   start=(kt == 0), stop=(kt == NSEG - 1))
            return ins
        w = [(f"qt{b}", v_ts[ot]), ("dv", v_xn)]
        if ot - 2 in v_ffn:
            w.append(("ac", v_ffn[ot - 2]))
        elif ot < 2:
            w.append(pj_free[ot])
        v_qpe[ot] = S.op("pe", qmms, waits=w, inc="pe")
        v_ffn[ot] = S.op("ac", lambda e, ot=ot, pj=pj: e.activation(
            gffn[:, ot * TL:(ot + 1) * TL], pj[:], AF.Silu,
            bias=cols["qb"][:, ot:ot + 1]),
            waits=[("pe", v_qpe[ot]), ("ld0", v_cols)], inc="ac")
    v_gffn = v_ffn[NSEG - 1]
    if TRUNC <= 8:
        S.emit()
        return nc

    def dump_plane2(nm, src, width):
        if nm not in dbg:
            return
        for i in range((width + TL - 1) // TL):
            lo = i * TL
            hi = min(width, lo + TL)
            vv = S.op("dv", lambda e, lo=lo, hi=hi: e.tensor_copy(
                x2b[0][:, 0:hi - lo], src[:, lo:hi]),
                waits=[("ac", v_gffn)], inc="dv")
            S.dma("sy", dbg[nm].ap()[:, lo:hi], x2b[0][:, 0:hi - lo],
                  inc="sto", waits=[("dv", vv)])
    dump_plane2("gffn", gffn, NSEG * TL)

    v_wo, v_oev, v_st = {}, {}, {}
    v_gfd = S.cnt["dv"]
    for ot in range(NSEG):
        b = ot % 2
        w = ([("pe", v_wo[ot - 2])] if ot - 2 in v_wo
             else list(era_gate))
        v_wd = S.dma("po",
                     wband[b][:].rearrange("p (kt oc) -> p kt oc", kt=NSEG),
                     d_w["wot"].ap().rearrange("(kt p) o -> p kt o", p=128)
                     [:, :, ot * 128:(ot + 1) * 128],
                     inc=f"wb{b}", waits=w)
        w2 = ([("dv", v_oev[ot - 2])] if ot - 2 in v_oev
              else [("dv", v_gfd)] + list(era_gate))
        v_x2 = S.dma("sy", x2b[b][:], d_x.ap()[ot * 128:(ot + 1) * 128, 1:SEGW],
                     inc=f"x2{b}", waits=w2)
        pj = PJ[2 + ot % 2]

        def womms(e, b=b, pj=pj):
            ins = None
            for kt in range(NSEG):
                lhsT = wband[b][:, kt * 128:(kt + 1) * 128]
                for hf in range(2):
                    ins = e.matmul(pj[:, hf * 512:(hf + 1) * 512], lhsT,
                                   yngT[:, kt * TL + hf * 512:
                                        kt * TL + hf * 512 + 512],
                                   start=(kt == 0), stop=(kt == NSEG - 1))
            return ins
        w3 = [(f"wb{b}", v_wd), ("yngt", v_yngT)]
        if ot - 2 in v_oev:
            w3.append(("dv", v_oev[ot - 2]))
        elif ot < 2:
            w3.append(pj_free[2 + ot])
        v_wo[ot] = S.op("pe", womms, waits=w3, inc="pe")
        wob = [("pe", v_wo[ot]), ("ac", v_gffn)]
        if ot - 2 in v_st:
            wob.append((f"st{b}", v_st[ot - 2]))
        S.op("dv", lambda e, ot=ot, b=b, pj=pj: e.tensor_tensor(
            outb[b][:], pj[:], gffn[:, ot * TL:(ot + 1) * TL], ALU.mult),
            waits=wob, inc="dv")
        v_oev[ot] = S.op("dv", lambda e, b=b: e.tensor_add(
            outb[b][:], outb[b][:], x2b[b][:]),
            waits=[(f"x2{b}", v_x2)], inc="dv")
        v_st[ot] = S.dma("sy", d_out.ap()[ot * 128:(ot + 1) * 128, :], outb[b][:],
                         inc=f"st{b}", waits=[("dv", v_oev[ot])])

    S.op("sy", lambda e: e.nop(),
         waits=[("st0", S.cnt.get("st0", 0)), ("st1", S.cnt.get("st1", 0)),
                ("sto", S.cnt.get("sto", 0))])
    es_f.close()
    S.emit()
    return nc


def _prep_inputs(inputs):
    f32 = np.float32
    x = np.ascontiguousarray(np.asarray(inputs["L0_attn_x"], f32))
    state = np.asarray(inputs["L0_attn_state"], f32).reshape(C)
    wkvstate = np.asarray(inputs["L0_attn_wkvstate"], f32)

    W = {}
    for nm, key in (("wrt", "L0_attn_w_r"), ("wkt", "L0_attn_w_k"),
                    ("wvt", "L0_attn_w_v"), ("wgt", "L0_attn_w_g"),
                    ("wot", "L0_attn_w_o")):
        W[nm] = np.ascontiguousarray(np.asarray(inputs[key], f32).T)
    qw = np.ascontiguousarray(np.asarray(inputs["L0_ffn_weight"], f32))

    def fmcols(v):
        return np.ascontiguousarray(np.asarray(v, f32).reshape(NSEG, 128).T)

    colmap = {"tmr": inputs["L0_attn_time_mix_r"],
              "tmk": inputs["L0_attn_time_mix_k"],
              "tmv": inputs["L0_attn_time_mix_v"],
              "tmg": inputs["L0_attn_time_mix_g"],
              "uu": inputs["L0_attn_time_faaaa"],
              "dec": inputs["L0_attn_time_decay"],
              "qb": inputs["L0_ffn_bias"]}
    cols = {k: fmcols(np.asarray(v, f32).reshape(C)) for k, v in colmap.items()}
    lnwbc = np.ascontiguousarray(np.broadcast_to(
        np.asarray(inputs["L0_attn_ln_w"], f32).reshape(1, C), (128, C)))
    lnbbc = np.ascontiguousarray(np.broadcast_to(
        np.asarray(inputs["L0_attn_ln_b"], f32).reshape(1, C), (128, C)))
    wkvp = np.ascontiguousarray(wkvstate.reshape(NSEG, 128, KD)).reshape(
        NSEG * 128, KD)
    mq = (np.arange(128)[:, None] < np.arange(128)[None, :]).astype(f32)
    mask2 = np.ascontiguousarray(np.concatenate([mq, mq], axis=1))
    ct = np.arange(NSEG * 128)
    bones = np.ascontiguousarray(
        ((ct[:, None] // KD) % 16 == np.arange(16)[None, :]).astype(f32))

    common = dict(qw=qw, lnwbc=lnwbc, lnbbc=lnbbc, wkvp=wkvp, mask2=mask2,
                  bones=bones, **W, **cols)
    in_maps = []
    for core in range(8):
        b, half = core // 2, core % 2
        t0 = half * TL
        xs = x[b, t0:t0 + TL]
        halo = state if half == 0 else x[b, t0 - 1]
        xfm = np.ascontiguousarray(
            np.concatenate([halo[None, :], xs], axis=0).T)
        xhalov = np.ascontiguousarray(halo.astype(f32).reshape(NSEG, 128).T)
        iota = np.ascontiguousarray(np.broadcast_to(
            (t0 + np.arange(TL)).astype(f32)[None, :], (128, TL)))
        selv = np.zeros((128, 8), f32)
        if half == 1:
            selv[:, core - 1] = 1.0
        hahbv = np.zeros((128, 4), f32)
        hahbv[:, 0] = 1.0 if half == 1 else 0.0
        hahbv[:, 1] = 0.0 if half == 1 else 1.0
        hahbv[:, 2] = EPS_LN
        hahbv[:, 3] = EPS_GN
        im = dict(common)
        im.update(x=xfm, xhalo=xhalov, iota=iota, sel=selv, hahb=hahbv)
        in_maps.append(im)
    return in_maps


_CACHED = {}


def kernel(**inputs):
    if "nc" not in _CACHED:
        _CACHED["nc"] = build_program()
    nc = _CACHED["nc"]
    in_maps = _prep_inputs(inputs)
    res = run_bass_kernel_spmd(nc, in_maps, core_ids=list(range(8)))
    out = np.zeros((B, T, C), np.float32)
    for core in range(8):
        b, half = core // 2, core % 2
        o = res.results[core]["out"]
        out[b, half * TL:(half + 1) * TL, :] = o.T
    if DEBUG:
        kernel.dbg = [{k: np.asarray(res.results[c]["dbg_" + k]) for k in DEBUG}
                      for c in range(8)]
    return out


# version 02
# speedup vs baseline: 1.1695x; 1.1695x over previous
import contextlib
import numpy as np
import ml_dtypes
import concourse.bass as bass
import concourse.mybir as mybir
from concourse.bass_utils import run_bass_kernel_spmd

F32 = mybir.dt.float32
BF16 = mybir.dt.bfloat16
AF = mybir.ActivationFunctionType
ALU = mybir.AluOpType
AX = mybir.AxisListType
bfd = ml_dtypes.bfloat16

B, T, C, H, KD = 4, 2048, 2048, 32, 64
TL = 1024
NSEG = 16
SEGW = TL + 1
G = 128
NCH = TL // G
NG = 4
NPG = NSEG // NG
HPG = H // NG
CQ = NPG * 128
EPS_LN = 1e-5
EPS_GN = 0.00064

DEBUG = []
TRUNC = 9999


class Sched:
    ENG = ("pe", "dv", "ac", "po", "sy")

    def __init__(self, nc):
        self.nc = nc
        self.items = {e: [] for e in self.ENG}
        self.sems = {}
        self.cnt = {}

    def sem(self, name):
        if name not in self.sems:
            self.sems[name] = self.nc.alloc_semaphore(name)
            self.cnt[name] = 0
        return self.sems[name]

    def op(self, eng, fn, waits=(), inc=None, amt=1):
        v = None
        waits = [w for w in waits if w is not None and w[1] > 0]
        if eng in ("dv", "ac", "po"):
            self.sem(eng)
            if self.cnt[eng] > 0:
                waits.append((eng, self.cnt[eng]))
        if inc is not None:
            self.sem(inc)
            self.cnt[inc] += amt
            v = self.cnt[inc]
        for s, _ in waits:
            self.sem(s)
        self.items[eng].append((fn, tuple(waits), inc, amt))
        return v

    def dma(self, eng, out, in_, inc, waits=(), **kw):
        return self.op(eng, lambda e: e.dma_start(out, in_, **kw),
                       waits=waits, inc=inc, amt=16)

    def tpose(self, out, in_, inc, waits=()):
        return self.op("sy", lambda e: e.dma_start_transpose(out, in_),
                       waits=waits, inc=inc, amt=16)

    def emit(self):
        nc = self.nc
        lastw = {}

        def run(eng_obj, name):
            for fn, waits, inc, amt in self.items[name]:
                for s, v in waits:
                    k = (name, s)
                    if lastw.get(k, 0) < v:
                        eng_obj.wait_ge(self.sems[s], v)
                        lastw[k] = v
                ins = fn(eng_obj)
                if inc is not None:
                    ins.then_inc(self.sems[inc], amt)

        with nc.Block() as block:
            @block.tensor
            def _(e):
                run(e, "pe")

            @block.vector
            def _(e):
                run(e, "dv")

            @block.scalar
            def _(e):
                run(e, "ac")

            @block.gpsimd
            def _(e):
                run(e, "po")

            @block.sync
            def _(e):
                run(e, "sy")


def build_program():
    nc = bass.Bass("TRN2", target_bir_lowering=False)
    S = Sched(nc)

    d_x = nc.dram_tensor("x", [C, SEGW], F32, kind="ExternalInput")
    d_w = {n: nc.dram_tensor(n, [C, C], F32, kind="ExternalInput")
           for n in ("wrt", "wkt", "wvt", "wgt", "wot")}
    d_qw = nc.dram_tensor("qw", [C, C], F32, kind="ExternalInput")
    d_cols = {n: nc.dram_tensor(n, [128, NSEG], F32, kind="ExternalInput")
              for n in ("tmr", "tmk", "tmv", "tmg", "uu", "dec", "qb")}
    d_xhalo = nc.dram_tensor("xhalo", [128, NSEG], F32, kind="ExternalInput")
    d_iota = nc.dram_tensor("iota", [128, TL], F32, kind="ExternalInput")
    d_lnw = nc.dram_tensor("lnwbc", [128, C], F32, kind="ExternalInput")
    d_lnb = nc.dram_tensor("lnbbc", [128, C], F32, kind="ExternalInput")
    d_wkvp = nc.dram_tensor("wkvp", [NSEG * 128, KD], F32, kind="ExternalInput")
    d_mask = nc.dram_tensor("mask2", [128, 256], F32, kind="ExternalInput")
    d_bones = nc.dram_tensor("bones", [NSEG * 128, 16], F32, kind="ExternalInput")
    d_sel = nc.dram_tensor("sel", [128, 8], F32, kind="ExternalInput")
    d_hahb = nc.dram_tensor("hahb", [128, 4], F32, kind="ExternalInput")
    d_out = nc.dram_tensor("out", [C, TL], F32, kind="ExternalOutput")
    d_ccin = nc.dram_tensor("ccin", [NPG * 128, KD], F32)
    d_ccout = nc.dram_tensor("ccout", [8 * NPG * 128, KD], F32)
    dbg = {}
    for nm in DEBUG:
        shp = {"xn": [128, NSEG * SEGW], "y": [128, NCH * NG * CQ],
               "yng": [128, NCH * NG * CQ], "gffn": [128, NSEG * TL],
               "yngT": [128, NSEG * TL], "spre": [128, NG * NPG * NCH * KD],
               "bb": [16, NG * TL], "r": [128, NG * TL],
               "kw": [128, NG * TL], "vtm": [128, NCH * CQ],
               "vtm2": [128, NCH * CQ], "vtm3": [128, NCH * CQ],
               "esb": [128, 3 * NSEG],
               "bonus0": [128, CQ], "attm0": [128, 256],
               "rw0": [128, NPG * G]}[nm]
        dbg[nm] = nc.dram_tensor("dbg_" + nm, shp, F32, kind="ExternalOutput")

    t_ones128 = nc.inline_tensor(np.ones((128, 1), dtype=bfd), name="i_ones128")
    t_onesk1 = nc.inline_tensor(np.ones((1, 128), dtype=bfd), name="i_onesk1")

    def A(name, shape, dt):
        return nc.alloc_sbuf_tensor(name, shape, dt, side="left")

    iota = A("iota_sb", [128, TL], F32)
    cols = {n: A(n + "_sb", [128, NSEG], F32) for n in d_cols}
    tm1m = {n: A(n + "_1m", [128, NSEG], F32) for n in ("tmr", "tmk", "tmv", "tmg")}
    e_sb = A("e_sb", [128, NSEG], F32)
    ebias = A("ebias_sb", [128, NSEG], F32)
    eneg = A("eneg_sb", [128, NSEG], F32)
    lnw = A("lnw_sb", [128, C], BF16)
    lnb = A("lnb_sb", [128, C], BF16)
    wkvp = A("wkvp_sb", [128, NSEG * KD], BF16)
    mask2 = A("mask2_sb", [128, 256], BF16)
    lhsTu = A("lhsTu_sb", [128, NSEG * 16], BF16)
    sel = A("sel_sb", [128, 8], F32)
    hahb = A("hahb_sb", [128, 4], F32)
    xhalo = A("xhalo_sb", [128, NSEG], F32)
    o128 = A("o128_sb", [128, 1], BF16)
    ok1 = A("ok1_sb", [1, 128], BF16)
    gsink = A("gsink", [128, 1], F32)
    yngT = A("yngT", [128, NSEG * TL], BF16)
    xn = A("xn", [128, NSEG * SEGW], BF16)

    PJ = [nc.alloc_psum_tensor(f"pj{i}", [128, TL], F32) for i in range(4)]
    ss_sl = [[PJ[3][0:64, 0:KD], PJ[3][64:128, 512:512 + KD]],
             [PJ[2][0:64, 0:KD], PJ[2][64:128, 512:512 + KD]]]
    str0 = PJ[2][0:1, 0:512]
    str1 = PJ[3][0:1, 0:512]

    def Rg(es, name, shape, dt):
        return es.enter_context(nc.sbuf_tensor(name, shape, dt, side="right"))


    x3 = d_x.ap().rearrange("(s p) t -> p s t", p=128)
    S.dma("sy", iota[:], d_iota.ap(), inc="ld0")
    for n in d_cols:
        S.dma("sy", cols[n][:], d_cols[n].ap(), inc="ld0")
    S.dma("sy", sel[:], d_sel.ap(), inc="ld0")
    S.dma("sy", xhalo[:], d_xhalo.ap(), inc="ld0")
    S.dma("sy", hahb[:], d_hahb.ap(), inc="ld0")
    v_cols = v_iota = S.cnt["ld0"]
    S.dma("po", lnw[:], d_lnw.ap(), inc="w0")
    S.dma("po", lnb[:], d_lnb.ap(), inc="w0")
    S.dma("po", mask2[:], d_mask.ap(), inc="w0")
    S.dma("po", wkvp[:].rearrange("p (s k) -> p s k", s=NSEG),
          d_wkvp.ap().rearrange("(s p) k -> p s k", p=128), inc="w0")
    S.dma("po", lhsTu[:].rearrange("p (s h) -> p s h", s=NSEG),
          d_bones.ap().rearrange("(s p) h -> p s h", p=128), inc="w0")
    S.dma("po", o128[:], t_ones128.ap(), inc="w0")
    S.dma("po", ok1[:], t_onesk1.ap(), inc="w0")
    v_wconst = S.cnt["w0"]

    v_e = S.op("ac", lambda e: e.activation(e_sb[:], cols["dec"][:], AF.Exp),
               waits=[("ld0", v_cols)], inc="ac")
    S.op("dv", lambda e: e.tensor_scalar_mul(ebias[:], e_sb[:], -2047.0),
         waits=[("ac", v_e)], inc="dv")
    S.op("dv", lambda e: e.tensor_scalar_mul(eneg[:], e_sb[:], -1.0),
         waits=[("ac", v_e)], inc="dv")
    for n in tm1m:
        S.op("dv", lambda e, n=n: e.tensor_scalar(
            tm1m[n][:], cols[n][:], -1.0, 1.0, ALU.mult, ALU.add),
            waits=[("ld0", v_cols)], inc="dv")
    v_tiny = None
    for s in range(NSEG):
        v_tiny = S.op("dv", lambda e, s=s: e.tensor_scalar(
            lhsTu[:, s * 16:(s + 1) * 16], lhsTu[:, s * 16:(s + 1) * 16],
            cols["uu"][:, s:s + 1], None, ALU.mult),
            waits=[("w0", v_wconst), ("ld0", v_cols)], inc="dv")

    es_ln = contextlib.ExitStack()
    xpc = [Rg(es_ln, f"xpc{i}", [128, 512], F32) for i in range(4)]
    xbc = [Rg(es_ln, f"xbc{i}", [128, 512], BF16) for i in range(2)]
    xqc = [Rg(es_ln, f"xqc{i}", [128, 512], BF16) for i in range(2)]
    mrow = Rg(es_ln, "mrow", [1, SEGW], F32)
    e2row = Rg(es_ln, "e2row", [1, SEGW], F32)
    varrow = Rg(es_ln, "varrow", [1, SEGW], F32)
    mbf = Rg(es_ln, "mbf", [1, SEGW], BF16)
    rbf = Rg(es_ln, "rbf", [1, SEGW], BF16)

    chunks = [(0, 512), (512, 512), (1024, 1)]
    v_cpd, v_sqd, v_stev = {}, {}, {}
    v_rowev = 0
    for ci, (off, cw) in enumerate(chunks):
        for s in range(NSEG):
            idx = ci * NSEG + s
            pb, b2 = idx % 4, idx % 2
            wd = []
            if idx - 4 in v_cpd:
                wd += [("dv", v_cpd[idx - 4]), ("ac", v_sqd[idx - 4])]
            v_pc = S.dma("sy", xpc[pb][:, 0:cw], x3[:, s, off:off + cw],
                         inc=f"xp{pb}", waits=wd,
                         allow_slow_non_contiguous=(cw == 1))
            wst = [("pe", v_stev[idx - 2])] if idx - 2 in v_stev else []
            v_cpd[idx] = S.op("dv", lambda e, pb=pb, b2=b2, cw=cw: e.tensor_copy(
                xbc[b2][:, 0:cw], xpc[pb][:, 0:cw]),
                waits=[(f"xp{pb}", v_pc)] + wst, inc="dv")
            v_sqd[idx] = S.op("ac", lambda e, pb=pb, b2=b2, cw=cw: e.activation(
                xqc[b2][:, 0:cw], xpc[pb][:, 0:cw], AF.Square),
                waits=[(f"xp{pb}", v_pc)] + wst, inc="ac")

            def stat2(e, b2=b2, cw=cw, s=s):
                e.matmul(str0[:, 0:cw], o128[:], xbc[b2][:, 0:cw],
                         start=(s == 0), stop=(s == NSEG - 1))
                return e.matmul(str1[:, 0:cw], o128[:], xqc[b2][:, 0:cw],
                                start=(s == 0), stop=(s == NSEG - 1))
            w = [("dv", v_cpd[idx]), ("ac", v_sqd[idx]), ("w0", v_wconst)]
            if ci > 0 and s == 0:
                w.append(("dv", v_rowev))
            v_stev[idx] = S.op("pe", stat2, waits=w, inc="pe")
        v_pe = v_stev[ci * NSEG + NSEG - 1]
        S.op("dv", lambda e, off=off, cw=cw: e.tensor_scalar_mul(
            mrow[0:1, off:off + cw], str0[:, 0:cw], 1.0 / C),
            waits=[("pe", v_pe)], inc="dv")
        v_rowev = S.op("dv", lambda e, off=off, cw=cw: e.tensor_scalar_mul(
            e2row[0:1, off:off + cw], str1[:, 0:cw], 1.0 / C),
            waits=[("pe", v_pe)], inc="dv")

    S.op("dv", lambda e: e.tensor_mul(varrow[:], mrow[:], mrow[:]),
         waits=[("dv", v_rowev)], inc="dv")
    v_var = S.op("dv", lambda e: e.tensor_sub(varrow[:], e2row[:], varrow[:]),
                 inc="dv")
    v_std = S.op("ac", lambda e: e.activation(varrow[:], varrow[:], AF.Sqrt,
                                              bias=hahb[0:1, 2:3]),
                 waits=[("dv", v_var), ("ld0", v_cols)], inc="ac")
    v_rec = S.op("dv", lambda e: e.reciprocal(varrow[:], varrow[:]),
                 waits=[("ac", v_std)], inc="dv")
    S.op("dv", lambda e: e.tensor_copy(mbf[:], mrow[:]), inc="dv")
    v_rows = S.op("dv", lambda e: e.tensor_copy(rbf[:], varrow[:]),
                  waits=[("dv", v_rec)], inc="dv")

    v_apl = {}
    for ci, (off, cw) in enumerate(chunks):
        def bcast(e, off=off, cw=cw):
            e.matmul(PJ[0][:, 0:cw], ok1[:], mbf[0:1, off:off + cw],
                     start=True, stop=True)
            return e.matmul(PJ[1][:, 0:cw], ok1[:], rbf[0:1, off:off + cw],
                            start=True, stop=True)
        w = [("dv", v_rows)]
        if ci > 0:
            w.append(("dv", v_apl[(ci - 1) * NSEG + NSEG - 1]))
        v_bc = S.op("pe", bcast, waits=w, inc="pe")
        for s in range(NSEG):
            idx = ci * NSEG + s
            pb, b2 = idx % 4, idx % 2
            if idx - 4 in v_apl:
                wd = [("dv", v_apl[idx - 4])]
            else:
                wd = [("dv", v_cpd[44 + pb]), ("ac", v_sqd[44 + pb])]
            v_pc = S.dma("sy", xpc[pb][:, 0:cw], x3[:, s, off:off + cw],
                         inc=f"xp{pb}", waits=wd,
                         allow_slow_non_contiguous=(cw == 1))
            S.op("dv", lambda e, pb=pb, b2=b2, cw=cw: e.tensor_tensor(
                xbc[b2][:, 0:cw], xpc[pb][:, 0:cw], PJ[0][:, 0:cw], ALU.subtract),
                waits=[(f"xp{pb}", v_pc), ("pe", v_bc)], inc="dv")
            v_apl[idx] = S.op("dv", lambda e, s=s, b2=b2, off=off, cw=cw:
                              e.tensor_tensor(
                xn[:, s * SEGW + off: s * SEGW + off + cw], xbc[b2][:, 0:cw],
                PJ[1][:, 0:cw], ALU.mult), inc="dv")
    v_xnraw = v_apl[2 * NSEG + NSEG - 1]
    hv = xn[:].rearrange("p (s t) -> p s t", s=NSEG)[:, :, 0]
    S.op("dv", lambda e: e.tensor_scalar(hv, hv, hahb[:, 0:1], None, ALU.mult),
         waits=[("dv", v_xnraw), ("ld0", v_cols)], inc="dv")
    v_xn = S.op("dv", lambda e: e.scalar_tensor_tensor(
        hv, xhalo[:], hahb[:, 1:2], hv, ALU.mult, ALU.add), inc="dv")
    if TRUNC <= 1:
        S.emit()
        return nc
    es_ln.close()

    es_g = contextlib.ExitStack()
    wslice = [Rg(es_g, f"wslice{i}", [128, 512], BF16) for i in range(2)]
    tmpb = [Rg(es_g, f"tmpb{i}", [128, TL], BF16) for i in range(2)]
    vg_tmp = [Rg(es_g, f"vg_tmp{i}", [128, TL], BF16) for i in range(2)]
    r_pl = Rg(es_g, "r_pl", [128, NPG * TL], BF16)
    kw_pl = Rg(es_g, "kw_pl", [128, NPG * TL], BF16)
    vtm_pl = Rg(es_g, "vtm_pl", [128, NCH * CQ], BF16)
    gtm_pl = Rg(es_g, "gtm_pl", [128, NCH * CQ], BF16)
    ruk_pl = Rg(es_g, "ruk_pl", [128, NPG * TL], BF16)
    gwfor = Rg(es_g, "gwfor", [128, NPG * TL], BF16)
    gwback = Rg(es_g, "gwback", [128, NPG * TL], BF16)
    rwbuf = Rg(es_g, "rwbuf", [128, NPG * G], BF16)
    spre = Rg(es_g, "spre", [128, NPG * NCH * KD], BF16)
    sfin = Rg(es_g, "sfin", [128, NPG * KD], F32)
    shalf = Rg(es_g, "shalf", [128, NPG * KD], F32)
    recv = Rg(es_g, "recv", [128, 8 * NPG * KD], F32)
    bb_sb = Rg(es_g, "bb_sb", [16, TL], BF16)
    kwT = [Rg(es_g, f"kwT{i}", [128, 128], BF16) for i in range(4)]
    attm = [Rg(es_g, f"attm{i}", [128, 256], BF16) for i in range(2)]
    bbT = [Rg(es_g, f"bbT{i}", [128, 16], BF16) for i in range(2)]
    bonus = [Rg(es_g, f"bonus{i}", [128, CQ], BF16) for i in range(2)]
    ybuf = [Rg(es_g, f"ybuf{i}", [128, CQ], BF16) for i in range(2)]
    y2buf = Rg(es_g, "y2buf", [128, CQ], BF16)
    tgn = [Rg(es_g, f"tgn{i}", [128, CQ], BF16) for i in range(2)]
    gn8 = [Rg(es_g, f"gn8_{i}", [128, HPG], F32) for i in range(5)]
    dbgb = Rg(es_g, "dbgb", [128, TL], F32) if DEBUG else None

    def dump_plane(nm, src, width, dst_off=0, waits=()):
        if nm not in dbg:
            return
        np_ = src.shape[0]
        nchunk = (width + TL - 1) // TL
        for i in range(nchunk):
            lo = i * TL
            hi = min(width, lo + TL)
            vv = S.op("dv", lambda e, lo=lo, hi=hi, np_=np_: e.tensor_copy(
                dbgb[0:np_, 0:hi - lo], src[0:np_, lo:hi]),
                waits=list(waits) + [("sto", S.cnt.get("sto", 0))], inc="dv")
            S.dma("sy", dbg[nm].ap()[0:np_, dst_off + lo:dst_off + hi],
                  dbgb[0:np_, 0:hi - lo], inc="sto", waits=[("dv", vv)])

    dump_plane("xn", xn, NSEG * SEGW, waits=[("dv", v_xn)])

    pj_free = [("dv", v_apl[2 * NSEG + NSEG - 1]),
               ("dv", v_apl[2 * NSEG + NSEG - 1]),
               ("dv", v_rowev), ("dv", v_rowev)]

    def emit_proj(wt_dram, GG, nm, v_free_blendsrc, evac, extra_kt0=(),
                  first_waits=()):
        v_pe_kt = {}
        v_bl = {}
        v_wd = {}
        for kt in range(NSEG):
            b = kt % 2
            wdw = ([("pe", v_pe_kt[kt - 2])] if kt - 2 in v_pe_kt
                   else list(first_waits))
            v_wd[kt] = S.dma("po", wslice[b][:],
                             wt_dram.ap()[kt * 128:(kt + 1) * 128,
                                          GG * 512:(GG + 1) * 512],
                             inc=f"ws{b}", waits=wdw)
            cur = xn[:, kt * SEGW + 1:(kt + 1) * SEGW]
            prev = xn[:, kt * SEGW:(kt + 1) * SEGW - 1]
            wb = [("dv", v_xn), ("pe", v_free_blendsrc)]
            if kt - 2 in v_pe_kt:
                wb = [("pe", v_pe_kt[kt - 2])]
            vd = S.op("dv", lambda e, b=b, cur=cur, prev=prev: e.tensor_tensor(
                tmpb[b][:], cur, prev, ALU.subtract), waits=wb, inc="dv")
            v_bl[kt] = ("dv", S.op(
                "dv", lambda e, b=b, prev=prev, kt=kt: e.scalar_tensor_tensor(
                    tmpb[b][:], tmpb[b][:], cols[nm][:, kt:kt + 1], prev,
                    ALU.mult, ALU.add),
                waits=[("dv", vd)], inc="dv"))

            def mms(e, b=b, kt=kt):
                ins = None
                for ob in range(NPG):
                    lhsT = wslice[b][:, ob * 128:(ob + 1) * 128]
                    for hf in range(2):
                        ins = e.matmul(PJ[ob][:, hf * 512:(hf + 1) * 512], lhsT,
                                       tmpb[b][:, hf * 512:(hf + 1) * 512],
                                       start=(kt == 0), stop=(kt == NSEG - 1))
                return ins
            w = [(f"ws{kt % 2}", v_wd[kt]), v_bl[kt]]
            if kt == 0:
                w += list(pj_free) + list(extra_kt0)
            v_pe_kt[kt] = S.op("pe", mms, waits=w, inc="pe")
        v_last = v_pe_kt[NSEG - 1]
        evs = []
        for i in range(NPG):
            evs.append(evac(i, PJ[i], [("pe", v_last)]))
            pj_free[i] = evs[i]
        return evs, v_last

    v_allmm_prev = 0
    v_evac_prev = None
    v_sfin_prev = 0
    v_selsum_prev = 0
    v_y5T_prev = 0
    v_red_prev = 0
    v_gnpool_prev = 0
    v_bt_prev = [0, 0]
    v_ci_prev = 0
    v_kwlast = 0
    v_rwlast = 0
    v_evac = {}

    for GG in range(NG):
        v_wf = {}
        v_wbk = {}
        for i in range(NPG):
            s = GG * NPG + i
            wv = [("xin", 0), ("dv", v_tiny), ("pe", v_allmm_prev)]
            v_wbk[i] = S.op("ac", lambda e, i=i, s=s: e.activation(
                gwback[:, i * TL:(i + 1) * TL], iota[:], AF.Exp,
                bias=ebias[:, s:s + 1], scale=e_sb[:, s:s + 1]),
                waits=[("ld0", v_iota), ("dv", v_tiny),
                       ("dv", v_kwlast), ("dv", v_rwlast)], inc="ac")
            v_wf[i] = S.op("ac", lambda e, i=i, s=s: e.activation(
                gwfor[:, i * TL:(i + 1) * TL], iota[:], AF.Exp,
                scale=eneg[:, s:s + 1]), inc="ac")

        def evr(i, pj, w):
            eng = "ac" if i < 2 else "dv"
            if eng == "ac":
                v = S.op("ac", lambda e, i=i, pj=pj: e.activation(
                    r_pl[:, i * TL:(i + 1) * TL], pj[:], AF.Copy),
                    waits=w, inc="ac")
            else:
                v = S.op("dv", lambda e, i=i, pj=pj: e.tensor_copy(
                    r_pl[:, i * TL:(i + 1) * TL], pj[:]), waits=w, inc="dv")
            return (eng, v)
        evs, v_pe_r = emit_proj(d_w["wrt"], GG, "tmr", v_allmm_prev, evr,
                                extra_kt0=[v_evac_prev] if v_evac_prev else [],
                                first_waits=[("dv", v_xn)] if GG == 0 else
                                [("pe", v_allmm_prev)])
        v_r_ac = max([v for d, v in evs if d == "ac"], default=0)
        v_r_dv = max([v for d, v in evs if d == "dv"], default=0)
        v_r = [("ac", v_r_ac), ("dv", v_r_dv)]
        if TRUNC <= 2:
            S.emit()
            return nc

        def evk(i, pj, w):
            eng = "ac" if i < 2 else "dv"
            if eng == "ac":
                v = S.op("ac", lambda e, i=i, pj=pj: e.activation(
                    ruk_pl[:, i * TL:(i + 1) * TL], pj[:], AF.Copy),
                    waits=w, inc="ac")
            else:
                v = S.op("dv", lambda e, i=i, pj=pj: e.tensor_copy(
                    ruk_pl[:, i * TL:(i + 1) * TL], pj[:]), waits=w, inc="dv")
            return (eng, v)
        evs, v_pe_k = emit_proj(d_w["wkt"], GG, "tmk", v_pe_r, evk,
                                first_waits=[("pe", v_pe_r)])
        v_kc_ac = max([v for d, v in evs if d == "ac"], default=0)
        v_kc_dv = max([v for d, v in evs if d == "dv"], default=0)

        v_kwd = {}
        for i in range(NPG):
            v_kwd[i] = S.op("dv", lambda e, i=i: e.tensor_tensor(
                kw_pl[:, i * TL:(i + 1) * TL], ruk_pl[:, i * TL:(i + 1) * TL],
                gwback[:, i * TL:(i + 1) * TL], ALU.mult),
                waits=[("ac", v_wbk[i]), ("ac", v_kc_ac), ("dv", v_kc_dv)],
                inc="dv")
        v_kwlast = v_kwd[NPG - 1]
        v_ruk = {}
        for i in range(NPG):
            v_ruk[i] = S.op("dv", lambda e, i=i: e.tensor_tensor(
                ruk_pl[:, i * TL:(i + 1) * TL], ruk_pl[:, i * TL:(i + 1) * TL],
                r_pl[:, i * TL:(i + 1) * TL], ALU.mult),
                waits=[("dv", v_kwd[i])] + v_r, inc="dv")
        v_k = S.cnt["dv"]

        def bbmm(e, GG=GG):
            ins = None
            for hf in range(2):
                for li in range(NPG):
                    ins = e.matmul(PJ[1][0:16, hf * 512:(hf + 1) * 512],
                                   lhsTu[:, (GG * NPG + li) * 16:
                                         (GG * NPG + li + 1) * 16],
                                   ruk_pl[:, li * TL + hf * 512:
                                          li * TL + hf * 512 + 512],
                                   start=(li == 0), stop=(li == NPG - 1))
            return ins
        v_bbpe = S.op("pe", bbmm,
                      waits=[("dv", v_ruk[NPG - 1])] + [pj_free[1]], inc="pe")
        v_bb = S.op("dv", lambda e: e.tensor_copy(bb_sb[:], PJ[1][0:16, 0:TL]),
                    waits=[("pe", v_bbpe), ("bt0", v_bt_prev[0]),
                           ("bt1", v_bt_prev[1])], inc="dv")
        if TRUNC <= 3:
            S.emit()
            return nc
        pj_free[1] = ("dv", v_bb)
        dump_plane("bb", bb_sb, TL, dst_off=GG * TL, waits=[("dv", v_bb)])

        v_vt = {}

        def evv(i, pj, w):
            b = i % 2
            wv = list(w)
            if i - 2 in v_vt:
                wv.append((f"vt{b}", v_vt[i - 2]))
            v1 = S.op("dv", lambda e, b=b, pj=pj: e.tensor_copy(
                vg_tmp[b][:], pj[:]), waits=wv, inc="dv")
            for n in range(NCH):
                v_vt[i] = S.tpose(
                    vtm_pl[:, n * CQ + i * 128: n * CQ + i * 128 + 128],
                    vg_tmp[b][:, n * 128:(n + 1) * 128],
                    inc=f"vt{b}", waits=[("dv", v1)])
            return ("dv", v1)
        evs, v_pe_v = emit_proj(d_w["wvt"], GG, "tmv", v_bbpe, evv,
                                first_waits=[("pe", v_pe_k)])
        if GG == 0:
            dump_plane("vtm2", vtm_pl, NCH * CQ,
                       waits=[("vt0", S.cnt.get("vt0", 0)),
                              ("vt1", S.cnt.get("vt1", 0))])
        v_vtm2 = [("vt0", S.cnt.get("vt0", 0)), ("vt1", S.cnt.get("vt1", 0))]
        v_v = evs[NPG - 1]

        z = spre[:].rearrange("p (l n k) -> p l n k", l=NPG, n=NCH)[:, :, 0, :]
        v_z = S.op("dv", lambda e: e.memset(z, 0.0),
                   waits=[("pe", v_allmm_prev)], inc="dv")
        v_zf = S.op("dv", lambda e: e.memset(sfin[:], 0.0),
                    waits=[("cci", v_ci_prev)], inc="dv")
        v_supd, v_ssev = {}, {}
        v_sfin = None
        for li in range(NPG):
            ss = ss_sl[li % 2]
            sfs = sfin[:, li * KD:(li + 1) * KD]
            for n in range(NCH):
                j = li * NCH + n
                kb = j % 4
                wt = [("dv", v_kwd[li])] + v_vtm2
                if j - 3 in v_supd:
                    wt.append(("pe", v_supd[j - 3]))
                v_t = S.tpose(kwT[kb][:], kw_pl[:, li * TL + n * 128:
                                                li * TL + n * 128 + 128],
                              inc=f"kwt{kb}", waits=wt)

                def supd(e, li=li, n=n, kb=kb, ss=ss):
                    ins = None
                    for hh in range(2):
                        ins = e.matmul(ss[hh],
                                       kwT[kb][:, hh * 64:(hh + 1) * 64],
                                       vtm_pl[:, n * CQ + li * 128 + hh * 64:
                                              n * CQ + li * 128 + hh * 64 + 64],
                                       start=True, stop=True,
                                       tile_position=(0, hh * 64))
                    return ins
                w = [(f"kwt{kb}", v_t)]
                if (li, n - 1) in v_ssev:
                    w.append(("dv", v_ssev[(li, n - 1)]))
                if n == 0:
                    w.append(("dv", v_zf))
                    if (li - 2, NCH - 1) in v_ssev:
                        w.append(("dv", v_ssev[(li - 2, NCH - 1)]))
                    elif v_sfin_prev:
                        w.append(("dv", v_sfin_prev))
                    if li < 2:
                        w.append(pj_free[3])
                        w.append(pj_free[2])
                v_supd[j] = S.op("pe", supd, waits=w, inc="pe")
                S.op("dv", lambda e, ss=ss, sfs=sfs: e.tensor_add(
                    sfs[0:64, :], sfs[0:64, :], ss[0]),
                    waits=[("pe", v_supd[j])], inc="dv")
                vadd = S.op("dv", lambda e, ss=ss, sfs=sfs: e.tensor_add(
                    sfs[64:128, :], sfs[64:128, :], ss[1]), inc="dv")
                if n < NCH - 1:
                    v_ssev[(li, n)] = S.op(
                        "dv", lambda e, li=li, n=n, sfs=sfs: e.tensor_copy(
                            spre[:, (li * NCH + n + 1) * KD:
                                 (li * NCH + n + 2) * KD], sfs),
                        inc="dv")
                else:
                    v_ssev[(li, n)] = vadd
                    v_sfin = vadd
        v_sfin_prev = v_sfin
        pj_free[3] = ("dv", v_sfin)
        pj_free[2] = ("dv", v_sfin)
        if TRUNC <= 4:
            S.emit()
            return nc

        v_ci = S.dma("po", d_ccin.ap().rearrange("(l p) k -> p l k", p=128),
                     sfin[:].rearrange("p (l k) -> p l k", l=NPG),
                     inc="cci", waits=[("dv", v_sfin)])
        v_ci_prev = v_ci
        v_cc = S.op("po", lambda e: e.collective_compute(
            "AllGather", ALU.bypass, replica_groups=[list(range(8))],
            ins=[d_ccin.ap().opt()], outs=[d_ccout.ap().opt()]),
            waits=[("cci", v_ci)], inc="cc")
        v_rc = S.dma("po", recv[:].rearrange("p (q l k) -> p q l k", q=8, l=NPG),
                     d_ccout.ap().rearrange("(q l p) k -> p q l k", p=128, l=NPG),
                     inc="ccr", waits=[("cc", v_cc), ("dv", v_selsum_prev)])
        S.op("dv", lambda e: e.tensor_scalar(
            shalf[:], recv[:, 0:NPG * KD], sel[:, 0:1], None, ALU.mult),
            waits=[("ccr", v_rc), ("ld0", v_cols)], inc="dv")
        for rr in range(1, 8):
            v_selsum_prev = S.op("dv", lambda e, rr=rr: e.scalar_tensor_tensor(
                shalf[:], recv[:, rr * NPG * KD:(rr + 1) * NPG * KD],
                sel[:, rr:rr + 1], shalf[:], ALU.mult, ALU.add), inc="dv")
        spv = spre[:].rearrange("p (l n k) -> p l n k", l=NPG, n=NCH)
        shv = shalf[:].rearrange("p (l k) -> p l k", l=NPG).unsqueeze(2) \
            .broadcast_to([128, NPG, NCH, KD])
        v_spre = S.op("dv", lambda e: e.tensor_tensor(spv, spv, shv, ALU.add),
                      waits=[("dv", v_z), ("dv", v_sfin)], inc="dv")
        dump_plane("spre", spre, NPG * NCH * KD, dst_off=GG * NPG * NCH * KD,
                   waits=[("dv", v_spre)])
        if GG == 0:
            dump_plane("vtm3", vtm_pl, NCH * CQ, waits=[("dv", v_spre)])
            dump_plane("esb", e_sb, NSEG, dst_off=0, waits=[("dv", v_tiny)])
            dump_plane("esb", eneg, NSEG, dst_off=NSEG, waits=[("dv", v_tiny)])
            dump_plane("esb", ebias, NSEG, dst_off=2 * NSEG,
                       waits=[("dv", v_tiny)])
        if TRUNC <= 5:
            S.emit()
            return nc

        v_gt = {}

        def evg(i, pj, w):
            b = i % 2
            wv = list(w)
            if i - 2 in v_gt:
                wv.append((f"gt{i % 2}", v_gt[i - 2]))
            va = S.op("ac", lambda e, b=b, pj=pj: e.activation(
                vg_tmp[b][:], pj[:], AF.Silu), waits=wv, inc="ac")
            for n in range(NCH):
                v_gt[i] = S.tpose(
                    gtm_pl[:, n * CQ + i * 128: n * CQ + i * 128 + 128],
                    vg_tmp[b][:, n * 128:(n + 1) * 128],
                    inc=f"gt{b}", waits=[("ac", va)])
            return ("ac", va)
        evs, v_pe_g = emit_proj(d_w["wgt"], GG, "tmg", v_pe_v, evg,
                                first_waits=[("pe", v_pe_v)])
        v_gtm2 = [("gt0", S.cnt.get("gt0", 0)), ("gt1", S.cnt.get("gt1", 0))]
        if GG == 0:
            dump_plane("vtm", vtm_pl, NCH * CQ,
                       waits=[("pe", v_pe_g), ("gt0", S.cnt.get("gt0", 0)),
                              ("gt1", S.cnt.get("gt1", 0))])
            dump_plane("rw0", r_pl, NPG * G,
                       waits=[("pe", v_pe_g), ("gt0", S.cnt.get("gt0", 0)),
                              ("gt1", S.cnt.get("gt1", 0))])
        if TRUNC <= 51:
            S.emit()
            return nc
        v_g_ac = max(v for d, v in evs)

        dump_plane("r", r_pl, TL, dst_off=GG * TL, waits=v_r)
        dump_plane("kw", kw_pl, TL, dst_off=GG * TL,
                   waits=[("dv", v_kwd[NPG - 1])])

        v_mask = {}
        v_evac = {}
        v_pairev = {}
        v_y2c = {}
        v_t1c = {}
        for n in range(NCH):
            pj = PJ[n % 2]
            v_rwp = {}
            for li in range(NPG):
                s = GG * NPG + li
                va = S.op("ac", lambda e, li=li, s=s, n=n: e.activation(
                    rwbuf[:, li * G:(li + 1) * G], iota[:, n * G:(n + 1) * G],
                    AF.Exp, scale=eneg[:, s:s + 1]),
                    waits=[("dv", v_tiny), ("ld0", v_iota),
                           ("pe", v_allmm_prev)], inc="ac")
                v_rwp[li] = S.op("dv", lambda e, li=li, n=n: e.tensor_tensor(
                    rwbuf[:, li * G:(li + 1) * G],
                    r_pl[:, li * TL + n * G: li * TL + (n + 1) * G],
                    rwbuf[:, li * G:(li + 1) * G], ALU.mult),
                    waits=[("ac", va)] + v_r, inc="dv")
            nb = n % 2
            hoff = (GG % 2) * 8
            wbt = [("dv", v_bb)]
            if n - 2 in v_evac:
                wbt.append(("dv", v_evac[n - 2]))
            v_bt = S.tpose(bbT[nb][:], bb_sb[0:16, n * 128:(n + 1) * 128],
                           inc=f"bt{nb}", waits=wbt)
            v_bt_prev[nb] = v_bt
            v_bo = S.op("dv", lambda e, n=n, nb=nb, hoff=hoff: e.tensor_tensor(
                bonus[nb][:].rearrange("p (h k) -> p h k", h=HPG),
                vtm_pl[:, n * CQ:(n + 1) * CQ].rearrange(
                    "p (h k) -> p h k", h=HPG),
                bbT[nb][:, hoff:hoff + HPG].unsqueeze(2)
                .broadcast_to([128, HPG, KD]), ALU.mult),
                waits=[(f"bt{nb}", v_bt)] + v_vtm2, inc="dv")
            if TRUNC <= 522 and n == 0:
                S.emit()
                return nc
            for li in range(NPG):
                j = n * NPG + li
                ab = j % 2
                att = PJ[2 + ab]

                def attmm(e, li=li, n=n, att=att):
                    ins = None
                    for hh in range(2):
                        ins = e.matmul(att[:, hh * 512: hh * 512 + 128],
                                       kw_pl[hh * 64:(hh + 1) * 64,
                                             li * TL + n * G:
                                             li * TL + (n + 1) * G],
                                       r_pl[hh * 64:(hh + 1) * 64,
                                            li * TL + n * G:
                                            li * TL + (n + 1) * G],
                                       start=True, stop=True,
                                       tile_position=(hh * 64, 0))
                    return ins
                w = [("dv", v_kwd[li])] + v_r
                if j - 2 in v_mask:
                    w.append(("dv", v_mask[j - 2]))
                elif n == 0 and li < 2:
                    w.append(pj_free[2 + ab])
                v_apc = S.op("pe", attmm, waits=w, inc="pe")
                mb = li % 2
                attv2 = att[:].rearrange("p (b c) -> p b c", b=2)[:, :, 0:128]
                v_mask[j] = S.op("dv", lambda e, mb=mb, attv2=attv2:
                                 e.tensor_tensor(
                    attm[mb][:].rearrange("p (b c) -> p b c", b=2), attv2,
                    mask2[:].rearrange("p (b c) -> p b c", b=2), ALU.mult),
                    waits=[("pe", v_apc), ("w0", v_wconst)], inc="dv")

                if TRUNC <= 523 and n == 0 and li == 1:
                    S.emit()
                    return nc

                def wkvmm(e, li=li, n=n, mb=mb, GG=GG):
                    sl = PJ[(n * NPG + li) % 2]
                    ins = None
                    for hh in range(2):
                        e.matmul(sl[:, hh * 512: hh * 512 + 64],
                                 attm[mb][:, hh * 128:(hh + 1) * 128],
                                 vtm_pl[:, n * CQ + li * 128 + hh * 64:
                                        n * CQ + li * 128 + hh * 64 + 64],
                                 start=True, stop=False)
                        e.matmul(sl[:, hh * 512: hh * 512 + 64],
                                 r_pl[hh * 64:(hh + 1) * 64,
                                      li * TL + n * G: li * TL + (n + 1) * G],
                                 spre[hh * 64:(hh + 1) * 64,
                                      (li * NCH + n) * KD:
                                      (li * NCH + n + 1) * KD],
                                 start=False, stop=False,
                                 tile_position=(hh * 64, 0))
                        ins = e.matmul(sl[:, hh * 512: hh * 512 + 64],
                                       rwbuf[hh * 64:(hh + 1) * 64,
                                             li * G:(li + 1) * G],
                                       wkvp[hh * 64:(hh + 1) * 64,
                                            (GG * NPG + li) * KD:
                                            (GG * NPG + li + 1) * KD],
                                       start=False, stop=True,
                                       tile_position=(hh * 64, 0))
                    return ins
                w = [("dv", v_mask[j]), ("dv", v_rwp[li]), ("dv", v_spre),
                     ("w0", v_wconst)] + v_vtm2
                js = n * NPG + li
                if js - 2 in v_pairev:
                    w.append(("dv", v_pairev[js - 2]))
                elif js < 2:
                    w.append(pj_free[0])
                v_wk = S.op("pe", wkvmm, waits=w, inc="pe")
                if TRUNC <= 524 and n == 0 and li == 0:
                    S.emit()
                    return nc
                wev = []
                if li == 0 and n - 2 in v_y2c:
                    wev = [("ac", v_y2c[n - 2]), ("po", v_t1c[n - 2])]
                slv = PJ[js % 2][:].rearrange("p (b c) -> p b c", b=2)[:, :, 0:64]
                v_pairev[js] = S.op(
                    "dv", lambda e, li=li, nb=n % 2, slv=slv: e.tensor_tensor(
                        ybuf[nb][:, li * 128:(li + 1) * 128].rearrange(
                            "p (b c) -> p b c", b=2),
                        slv,
                        bonus[nb][:, li * 128:(li + 1) * 128].rearrange(
                            "p (b c) -> p b c", b=2), ALU.add),
                    waits=[("pe", v_wk), ("dv", v_bo)] + wev, inc="dv")
            v_allmm = S.cnt["pe"]
            v_allmm_prev = v_allmm
            v_evac[n] = v_pairev[n * NPG + NPG - 1]
            if TRUNC <= 52 and n == 0:
                S.emit()
                return nc
            dump_plane("y", ybuf[nb], CQ, dst_off=(GG * NCH + n) * CQ,
                       waits=[("dv", v_evac[n])])

            v_y2 = S.op("ac", lambda e, nb=nb: e.activation(
                y2buf[:], ybuf[nb][:], AF.Square),
                waits=[("dv", v_evac[n]), ("dv", v_red_prev)], inc="ac")
            v_y2c[n] = v_y2
            yv = ybuf[nb][:].rearrange("p (h k) -> p h k", h=HPG)
            S.op("dv", lambda e, yv=yv: e.tensor_reduce(
                gn8[0][:], yv, AX.X, ALU.add),
                waits=[("dv", v_evac[n]), ("po", v_gnpool_prev)], inc="dv")
            S.op("dv", lambda e: e.tensor_reduce(
                gn8[1][:], y2buf[:].rearrange("p (h k) -> p h k", h=HPG),
                AX.X, ALU.add), waits=[("ac", v_y2)], inc="dv")
            S.op("dv", lambda e: e.tensor_scalar_mul(gn8[0][:], gn8[0][:],
                                                     1.0 / KD), inc="dv")
            S.op("dv", lambda e: e.tensor_scalar_mul(gn8[1][:], gn8[1][:],
                                                     1.0 / KD), inc="dv")
            S.op("dv", lambda e: e.tensor_mul(gn8[2][:], gn8[0][:], gn8[0][:]),
                 inc="dv")
            v_gv = S.op("dv", lambda e: e.tensor_sub(gn8[2][:], gn8[1][:],
                                                     gn8[2][:]), inc="dv")
            v_gs = S.op("ac", lambda e: e.activation(gn8[3][:], gn8[2][:],
                                                     AF.Sqrt, bias=hahb[:, 3:4]),
                        waits=[("dv", v_gv)], inc="ac")
            v_gr = S.op("dv", lambda e: e.reciprocal(gn8[4][:], gn8[3][:]),
                        waits=[("ac", v_gs)], inc="dv")
            v_red_prev = v_gr
            if TRUNC <= 53 and n == 0:
                S.emit()
                return nc
            muv = gn8[0][:].unsqueeze(2).broadcast_to([128, HPG, KD])
            rsv = gn8[4][:].unsqueeze(2).broadcast_to([128, HPG, KD])
            v_t1 = S.op("po", lambda e, yv=yv, muv=muv: e.tensor_tensor(
                tgn[0][:].rearrange("p (h k) -> p h k", h=HPG), yv, muv,
                ALU.subtract),
                waits=[("dv", v_gr), ("yngt", v_y5T_prev)], inc="po")
            v_t1c[n] = v_t1
            v_t2 = S.op("po", lambda e, rsv=rsv: e.tensor_tensor(
                tgn[1][:].rearrange("p (h k) -> p h k", h=HPG),
                tgn[0][:].rearrange("p (h k) -> p h k", h=HPG), rsv, ALU.mult),
                inc="po")
            lo = GG * CQ
            v_t3 = S.op("po", lambda e, lo=lo: e.tensor_tensor(
                tgn[0][:], tgn[1][:], lnw[:, lo:lo + CQ], ALU.mult),
                waits=[("w0", v_wconst)], inc="po")
            v_gnpool_prev = v_t3
            v_t4 = S.op("dv", lambda e, lo=lo: e.tensor_tensor(
                tgn[1][:], tgn[0][:], lnb[:, lo:lo + CQ], ALU.add),
                waits=[("po", v_t3)], inc="dv")
            v_y5 = S.op("dv", lambda e, n=n: e.tensor_tensor(
                tgn[0][:], tgn[1][:], gtm_pl[:, n * CQ:(n + 1) * CQ], ALU.mult),
                waits=v_gtm2 + [("dv", v_t4), ("ac", v_g_ac)], inc="dv")
            if TRUNC <= 54 and n == 0:
                S.emit()
                return nc
            for li in range(NPG):
                v_y5T_prev = S.tpose(
                    yngT[:, (GG * NPG + li) * TL + n * 128:
                         (GG * NPG + li) * TL + n * 128 + 128],
                    tgn[0][:, li * 128:(li + 1) * 128],
                    inc="yngt", waits=[("dv", v_y5)])
            dump_plane("yng", tgn[0], CQ, dst_off=(GG * NCH + n) * CQ,
                       waits=[("dv", v_y5)])
        if TRUNC <= 6:
            S.emit()
            return nc
        v_evac_prev = ("dv", v_evac[NCH - 1])
        last = NCH * NPG - 1
        pj_free[0] = ("dv", v_pairev[last - 1])
        pj_free[1] = ("dv", v_pairev[last])
        pj_free[2] = ("dv", v_mask[last - 1])
        pj_free[3] = ("dv", v_mask[last])
    v_yngT = S.cnt["yngt"]
    dump_plane("yngT", yngT, NSEG * TL, waits=[("yngt", v_yngT)])

    v_dbg_done = S.cnt["dv"]
    era_gate = [("pe", S.cnt["pe"]), ("yngt", S.cnt["yngt"]),
                ("po", S.cnt["po"]), ("ac", S.cnt["ac"]),
                ("dv", v_dbg_done)]
    es_g.close()

    es_f = contextlib.ExitStack()
    gffn = Rg(es_f, "gffn", [128, NSEG * TL], BF16)
    wband = [Rg(es_f, f"wband{i}", [128, NSEG * 128], BF16) for i in range(2)]
    qs = [Rg(es_f, f"qs{i}", [128, C], F32) for i in range(2)]
    qnat = [Rg(es_f, f"qnat{i}", [128, C], BF16) for i in range(2)]
    qTs = [Rg(es_f, f"qTs{i}", [128, NSEG * 128], BF16) for i in range(2)]
    qred = [Rg(es_f, f"qred{i}", [128, 1], F32) for i in range(4)]
    x2b = [Rg(es_f, f"x2b{i}", [128, TL], F32) for i in range(2)]
    outb = [Rg(es_f, f"outb{i}", [128, TL], F32) for i in range(2)]

    v_ffn, v_qpe, v_ts, v_cmpn = {}, {}, {}, {}
    for ot in range(NSEG):
        b = ot % 2
        wq = ([("dv", v_cmpn[ot - 2])] if ot - 2 in v_cmpn
              else list(era_gate))
        v_qd = S.dma("sy", qs[b][:], d_qw.ap()[ot * 128:(ot + 1) * 128, :],
                     inc=f"q{b}", waits=wq)
        v_red = S.op("dv", lambda e, b=b: e.tensor_reduce(
            qred[b][:], qs[b][:], AX.X, ALU.add, apply_absolute_value=True),
            waits=[(f"q{b}", v_qd)], inc="dv")
        S.op("dv", lambda e, b=b: e.tensor_scalar_mul(
            qred[b][:], qred[b][:], 0.7 / C), waits=[("dv", v_red)], inc="dv")
        v_nth = S.op("dv", lambda e, b=b: e.tensor_scalar_mul(
            qred[2 + b][:], qred[b][:], -1.0), inc="dv")
        v_p1 = S.op("dv", lambda e, b=b: e.tensor_scalar(
            qnat[b][:], qs[b][:], qred[b][:, 0:1], None, ALU.is_gt),
            waits=[("dv", v_nth)] +
            ([(f"qt{b}", v_ts[ot - 2])] if ot - 2 in v_ts else []), inc="dv")
        v_p2 = S.op("dv", lambda e, b=b: e.tensor_scalar(
            qTs[b][:], qs[b][:], qred[2 + b][:, 0:1], None, ALU.is_lt),
            waits=([("pe", v_qpe[ot - 2])] if ot - 2 in v_qpe else []),
            inc="dv")
        v_cmpn[ot] = v_p2
        v_qn = S.op("dv", lambda e, b=b: e.tensor_sub(
            qnat[b][:], qnat[b][:], qTs[b][:]), waits=[("dv", v_p2)], inc="dv")
        for kt in range(NSEG):
            v_ts[ot] = S.tpose(qTs[b][:, kt * 128:(kt + 1) * 128],
                               qnat[b][:, kt * 128:(kt + 1) * 128],
                               inc=f"qt{b}", waits=[("dv", v_qn)])
        pj = PJ[ot % 2]

        def qmms(e, b=b, pj=pj):
            ins = None
            for kt in range(NSEG):
                lhsT = qTs[b][:, kt * 128:(kt + 1) * 128]
                for hf in range(2):
                    ins = e.matmul(pj[:, hf * 512:(hf + 1) * 512], lhsT,
                                   xn[:, kt * SEGW + 1 + hf * 512:
                                      kt * SEGW + 1 + hf * 512 + 512],
                                   start=(kt == 0), stop=(kt == NSEG - 1))
            return ins
        w = [(f"qt{b}", v_ts[ot]), ("dv", v_xn)]
        if ot - 2 in v_ffn:
            w.append(("ac", v_ffn[ot - 2]))
        elif ot < 2:
            w.append(pj_free[ot])
        v_qpe[ot] = S.op("pe", qmms, waits=w, inc="pe")
        v_ffn[ot] = S.op("ac", lambda e, ot=ot, pj=pj: e.activation(
            gffn[:, ot * TL:(ot + 1) * TL], pj[:], AF.Silu,
            bias=cols["qb"][:, ot:ot + 1]),
            waits=[("pe", v_qpe[ot]), ("ld0", v_cols)], inc="ac")
    v_gffn = v_ffn[NSEG - 1]
    if TRUNC <= 8:
        S.emit()
        return nc

    def dump_plane2(nm, src, width):
        if nm not in dbg:
            return
        for i in range((width + TL - 1) // TL):
            lo = i * TL
            hi = min(width, lo + TL)
            vv = S.op("dv", lambda e, lo=lo, hi=hi: e.tensor_copy(
                x2b[0][:, 0:hi - lo], src[:, lo:hi]),
                waits=[("ac", v_gffn)], inc="dv")
            S.dma("sy", dbg[nm].ap()[:, lo:hi], x2b[0][:, 0:hi - lo],
                  inc="sto", waits=[("dv", vv)])
    dump_plane2("gffn", gffn, NSEG * TL)

    v_wo, v_oev, v_st = {}, {}, {}
    v_gfd = S.cnt["dv"]
    for ot in range(NSEG):
        b = ot % 2
        w = ([("pe", v_wo[ot - 2])] if ot - 2 in v_wo
             else list(era_gate))
        v_wd = S.dma("po",
                     wband[b][:].rearrange("p (kt oc) -> p kt oc", kt=NSEG),
                     d_w["wot"].ap().rearrange("(kt p) o -> p kt o", p=128)
                     [:, :, ot * 128:(ot + 1) * 128],
                     inc=f"wb{b}", waits=w)
        w2 = ([("dv", v_oev[ot - 2])] if ot - 2 in v_oev
              else [("dv", v_gfd)] + list(era_gate))
        v_x2 = S.dma("sy", x2b[b][:], d_x.ap()[ot * 128:(ot + 1) * 128, 1:SEGW],
                     inc=f"x2{b}", waits=w2)
        pj = PJ[2 + ot % 2]

        def womms(e, b=b, pj=pj):
            ins = None
            for kt in range(NSEG):
                lhsT = wband[b][:, kt * 128:(kt + 1) * 128]
                for hf in range(2):
                    ins = e.matmul(pj[:, hf * 512:(hf + 1) * 512], lhsT,
                                   yngT[:, kt * TL + hf * 512:
                                        kt * TL + hf * 512 + 512],
                                   start=(kt == 0), stop=(kt == NSEG - 1))
            return ins
        w3 = [(f"wb{b}", v_wd), ("yngt", v_yngT)]
        if ot - 2 in v_oev:
            w3.append(("dv", v_oev[ot - 2]))
        elif ot < 2:
            w3.append(pj_free[2 + ot])
        v_wo[ot] = S.op("pe", womms, waits=w3, inc="pe")
        wob = [("pe", v_wo[ot]), ("ac", v_gffn)]
        if ot - 2 in v_st:
            wob.append((f"st{b}", v_st[ot - 2]))
        S.op("dv", lambda e, ot=ot, b=b, pj=pj: e.tensor_tensor(
            outb[b][:], pj[:], gffn[:, ot * TL:(ot + 1) * TL], ALU.mult),
            waits=wob, inc="dv")
        v_oev[ot] = S.op("dv", lambda e, b=b: e.tensor_add(
            outb[b][:], outb[b][:], x2b[b][:]),
            waits=[(f"x2{b}", v_x2)], inc="dv")
        v_st[ot] = S.dma("sy", d_out.ap()[ot * 128:(ot + 1) * 128, :], outb[b][:],
                         inc=f"st{b}", waits=[("dv", v_oev[ot])])

    S.op("sy", lambda e: e.nop(),
         waits=[("st0", S.cnt.get("st0", 0)), ("st1", S.cnt.get("st1", 0)),
                ("sto", S.cnt.get("sto", 0))])
    es_f.close()
    S.emit()
    return nc


def _prep_inputs(inputs):
    f32 = np.float32
    x = np.ascontiguousarray(np.asarray(inputs["L0_attn_x"], f32))
    state = np.asarray(inputs["L0_attn_state"], f32).reshape(C)
    wkvstate = np.asarray(inputs["L0_attn_wkvstate"], f32)

    W = {}
    for nm, key in (("wrt", "L0_attn_w_r"), ("wkt", "L0_attn_w_k"),
                    ("wvt", "L0_attn_w_v"), ("wgt", "L0_attn_w_g"),
                    ("wot", "L0_attn_w_o")):
        W[nm] = np.ascontiguousarray(np.asarray(inputs[key], f32).T)
    qw = np.ascontiguousarray(np.asarray(inputs["L0_ffn_weight"], f32))

    def fmcols(v):
        return np.ascontiguousarray(np.asarray(v, f32).reshape(NSEG, 128).T)

    colmap = {"tmr": inputs["L0_attn_time_mix_r"],
              "tmk": inputs["L0_attn_time_mix_k"],
              "tmv": inputs["L0_attn_time_mix_v"],
              "tmg": inputs["L0_attn_time_mix_g"],
              "uu": inputs["L0_attn_time_faaaa"],
              "dec": inputs["L0_attn_time_decay"],
              "qb": inputs["L0_ffn_bias"]}
    cols = {k: fmcols(np.asarray(v, f32).reshape(C)) for k, v in colmap.items()}
    lnwbc = np.ascontiguousarray(np.broadcast_to(
        np.asarray(inputs["L0_attn_ln_w"], f32).reshape(1, C), (128, C)))
    lnbbc = np.ascontiguousarray(np.broadcast_to(
        np.asarray(inputs["L0_attn_ln_b"], f32).reshape(1, C), (128, C)))
    wkvp = np.ascontiguousarray(wkvstate.reshape(NSEG, 128, KD)).reshape(
        NSEG * 128, KD)
    mq = (np.arange(128)[:, None] < np.arange(128)[None, :]).astype(f32)
    mask2 = np.ascontiguousarray(np.concatenate([mq, mq], axis=1))
    ct = np.arange(NSEG * 128)
    bones = np.ascontiguousarray(
        ((ct[:, None] // KD) % 16 == np.arange(16)[None, :]).astype(f32))

    common = dict(qw=qw, lnwbc=lnwbc, lnbbc=lnbbc, wkvp=wkvp, mask2=mask2,
                  bones=bones, **W, **cols)
    in_maps = []
    for core in range(8):
        b, half = core // 2, core % 2
        t0 = half * TL
        xs = x[b, t0:t0 + TL]
        halo = state if half == 0 else x[b, t0 - 1]
        xfm = np.ascontiguousarray(
            np.concatenate([halo[None, :], xs], axis=0).T)
        xhalov = np.ascontiguousarray(halo.astype(f32).reshape(NSEG, 128).T)
        iota = np.ascontiguousarray(np.broadcast_to(
            (t0 + np.arange(TL)).astype(f32)[None, :], (128, TL)))
        selv = np.zeros((128, 8), f32)
        if half == 1:
            selv[:, core - 1] = 1.0
        hahbv = np.zeros((128, 4), f32)
        hahbv[:, 0] = 1.0 if half == 1 else 0.0
        hahbv[:, 1] = 0.0 if half == 1 else 1.0
        hahbv[:, 2] = EPS_LN
        hahbv[:, 3] = EPS_GN
        im = dict(common)
        im.update(x=xfm, xhalo=xhalov, iota=iota, sel=selv, hahb=hahbv)
        in_maps.append(im)
    return in_maps


_CACHED = {}


def kernel(**inputs):
    if "nc" not in _CACHED:
        _CACHED["nc"] = build_program()
    nc = _CACHED["nc"]
    in_maps = _prep_inputs(inputs)
    res = run_bass_kernel_spmd(nc, in_maps, core_ids=list(range(8)))
    out = np.zeros((B, T, C), np.float32)
    for core in range(8):
        b, half = core // 2, core % 2
        o = res.results[core]["out"]
        out[b, half * TL:(half + 1) * TL, :] = o.T
    if DEBUG:
        kernel.dbg = [{k: np.asarray(res.results[c]["dbg_" + k]) for k in DEBUG}
                      for c in range(8)]
    return out


# version 04 (final)
# speedup vs baseline: 1.4921x; 1.2759x over previous
import contextlib
import numpy as np
import ml_dtypes
import concourse.bass as bass
import concourse.mybir as mybir
from concourse.bass_utils import run_bass_kernel_spmd

F32 = mybir.dt.float32
BF16 = mybir.dt.bfloat16
AF = mybir.ActivationFunctionType
ALU = mybir.AluOpType
AX = mybir.AxisListType
bfd = ml_dtypes.bfloat16

B, T, C, H, KD = 4, 2048, 2048, 32, 64
TL = 1024
NSEG = 16
SEGW = TL + 1
G = 128
NCH = TL // G
NG = 4
NPG = NSEG // NG
HPG = H // NG
CQ = NPG * 128
EPS_LN = 1e-5
EPS_GN = 0.00064

DEBUG = []
TRUNC = 9999


class Sched:
    ENG = ("pe", "dv", "ac", "po", "sy")

    def __init__(self, nc):
        self.nc = nc
        self.items = {e: [] for e in self.ENG}
        self.sems = {}
        self.cnt = {}

    def sem(self, name):
        if name not in self.sems:
            self.sems[name] = self.nc.alloc_semaphore(name)
            self.cnt[name] = 0
        return self.sems[name]

    def op(self, eng, fn, waits=(), inc=None, amt=1):
        v = None
        waits = [w for w in waits if w is not None and w[1] > 0]
        if eng in ("dv", "ac", "po"):
            self.sem(eng)
            if self.cnt[eng] > 0:
                waits.append((eng, self.cnt[eng]))
        if inc is not None:
            self.sem(inc)
            self.cnt[inc] += amt
            v = self.cnt[inc]
        for s, _ in waits:
            self.sem(s)
        self.items[eng].append((fn, tuple(waits), inc, amt))
        return v

    def dma(self, eng, out, in_, inc, waits=(), **kw):
        return self.op(eng, lambda e: e.dma_start(out, in_, **kw),
                       waits=waits, inc=inc, amt=16)

    def tpose(self, out, in_, inc, waits=()):
        return self.op("sy", lambda e: e.dma_start_transpose(out, in_),
                       waits=waits, inc=inc, amt=16)

    def emit(self):
        nc = self.nc
        lastw = {}

        def run(eng_obj, name):
            for fn, waits, inc, amt in self.items[name]:
                for s, v in waits:
                    k = (name, s)
                    if lastw.get(k, 0) < v:
                        eng_obj.wait_ge(self.sems[s], v)
                        lastw[k] = v
                ins = fn(eng_obj)
                if inc is not None:
                    ins.then_inc(self.sems[inc], amt)

        with nc.Block() as block:
            @block.tensor
            def _(e):
                run(e, "pe")

            @block.vector
            def _(e):
                run(e, "dv")

            @block.scalar
            def _(e):
                run(e, "ac")

            @block.gpsimd
            def _(e):
                run(e, "po")

            @block.sync
            def _(e):
                run(e, "sy")


def build_program():
    nc = bass.Bass("TRN2", target_bir_lowering=False)
    S = Sched(nc)

    d_x = nc.dram_tensor("x", [C, SEGW], F32, kind="ExternalInput")
    d_w = {n: nc.dram_tensor(n, [C, C], F32, kind="ExternalInput")
           for n in ("wrt", "wkt", "wvt", "wgt", "wot")}
    d_qw = nc.dram_tensor("qw", [C, C], F32, kind="ExternalInput")
    d_qwt = nc.dram_tensor("qwt", [NSEG, 128, NSEG * 128], F32,
                           kind="ExternalInput")
    d_throw = nc.dram_tensor("throw", [128, NSEG], F32)
    d_cols = {n: nc.dram_tensor(n, [128, NSEG], F32, kind="ExternalInput")
              for n in ("tmr", "tmk", "tmv", "tmg", "uu", "dec", "qb")}
    d_xhalo = nc.dram_tensor("xhalo", [128, NSEG], F32, kind="ExternalInput")
    d_iota = nc.dram_tensor("iota", [128, TL], F32, kind="ExternalInput")
    d_lnw = nc.dram_tensor("lnwbc", [128, C], F32, kind="ExternalInput")
    d_lnb = nc.dram_tensor("lnbbc", [128, C], F32, kind="ExternalInput")
    d_wkvp = nc.dram_tensor("wkvp", [NSEG * 128, KD], F32, kind="ExternalInput")
    d_mask = nc.dram_tensor("mask2", [128, 256], F32, kind="ExternalInput")
    d_bones = nc.dram_tensor("bones", [NSEG * 128, 16], F32, kind="ExternalInput")
    d_sel = nc.dram_tensor("sel", [128, 8], F32, kind="ExternalInput")
    d_hahb = nc.dram_tensor("hahb", [128, 4], F32, kind="ExternalInput")
    d_out = nc.dram_tensor("out", [C, TL], F32, kind="ExternalOutput")
    d_ccin = nc.dram_tensor("ccin", [NPG * 128, KD], F32)
    d_ccout = nc.dram_tensor("ccout", [8 * NPG * 128, KD], F32)
    dbg = {}
    for nm in DEBUG:
        shp = {"xn": [128, NSEG * SEGW], "y": [128, NCH * NG * CQ],
               "yng": [128, NCH * NG * CQ], "gffn": [128, NSEG * TL],
               "yngT": [128, NSEG * TL], "spre": [128, NG * NPG * NCH * KD],
               "bb": [16, NG * TL], "r": [128, NG * TL],
               "kw": [128, NG * TL], "vtm": [128, NCH * CQ],
               "vtm2": [128, NCH * CQ], "vtm3": [128, NCH * CQ],
               "esb": [128, 3 * NSEG],
               "bonus0": [128, CQ], "attm0": [128, 256],
               "rw0": [128, NPG * G]}[nm]
        dbg[nm] = nc.dram_tensor("dbg_" + nm, shp, F32, kind="ExternalOutput")

    t_ones128 = nc.inline_tensor(np.ones((128, 1), dtype=bfd), name="i_ones128")
    t_onesk1 = nc.inline_tensor(np.ones((1, 128), dtype=bfd), name="i_onesk1")
    t_onesk1f = nc.inline_tensor(np.ones((1, 128), dtype=np.float32),
                                 name="i_onesk1f")

    def A(name, shape, dt):
        return nc.alloc_sbuf_tensor(name, shape, dt, side="left")

    iota = A("iota_sb", [128, TL], F32)
    cols = {n: A(n + "_sb", [128, NSEG], F32) for n in d_cols}
    tm1m = {n: A(n + "_1m", [128, NSEG], F32) for n in ("tmr", "tmk", "tmv", "tmg")}
    e_sb = A("e_sb", [128, NSEG], F32)
    ebias = A("ebias_sb", [128, NSEG], F32)
    eneg = A("eneg_sb", [128, NSEG], F32)
    lnw = A("lnw_sb", [128, C], BF16)
    lnb = A("lnb_sb", [128, C], BF16)
    wkvp = A("wkvp_sb", [128, NSEG * KD], BF16)
    mask2 = A("mask2_sb", [128, 256], BF16)
    lhsTu = A("lhsTu_sb", [128, NSEG * 16], BF16)
    sel = A("sel_sb", [128, 8], F32)
    hahb = A("hahb_sb", [128, 4], F32)
    xhalo = A("xhalo_sb", [128, NSEG], F32)
    o128 = A("o128_sb", [128, 1], BF16)
    ok1 = A("ok1_sb", [1, 128], BF16)
    ok1f = A("ok1f_sb", [1, 128], F32)
    gsink = A("gsink", [128, 1], F32)
    yngT = A("yngT", [128, NSEG * TL], BF16)
    xn = A("xn", [128, NSEG * SEGW], BF16)

    PJ = [nc.alloc_psum_tensor(f"pj{i}", [128, TL], F32) for i in range(4)]
    ss_sl = [[PJ[3][0:64, 0:KD], PJ[3][64:128, 512:512 + KD]],
             [PJ[2][0:64, 0:KD], PJ[2][64:128, 512:512 + KD]]]
    str0 = PJ[2][0:1, 0:512]
    str1 = PJ[3][0:1, 0:512]

    def Rg(es, name, shape, dt):
        return es.enter_context(nc.sbuf_tensor(name, shape, dt, side="right"))


    x3 = d_x.ap().rearrange("(s p) t -> p s t", p=128)
    S.dma("sy", iota[:], d_iota.ap(), inc="ld0")
    for n in d_cols:
        S.dma("sy", cols[n][:], d_cols[n].ap(), inc="ld0")
    S.dma("sy", sel[:], d_sel.ap(), inc="ld0")
    S.dma("sy", xhalo[:], d_xhalo.ap(), inc="ld0")
    S.dma("sy", hahb[:], d_hahb.ap(), inc="ld0")
    v_cols = v_iota = S.cnt["ld0"]
    S.dma("po", lnw[:], d_lnw.ap(), inc="w0")
    S.dma("po", lnb[:], d_lnb.ap(), inc="w0")
    S.dma("po", mask2[:], d_mask.ap(), inc="w0")
    S.dma("po", wkvp[:].rearrange("p (s k) -> p s k", s=NSEG),
          d_wkvp.ap().rearrange("(s p) k -> p s k", p=128), inc="w0")
    S.dma("po", lhsTu[:].rearrange("p (s h) -> p s h", s=NSEG),
          d_bones.ap().rearrange("(s p) h -> p s h", p=128), inc="w0")
    S.dma("po", o128[:], t_ones128.ap(), inc="w0")
    S.dma("po", ok1[:], t_onesk1.ap(), inc="w0")
    S.dma("po", ok1f[:], t_onesk1f.ap(), inc="w0")
    v_wconst = S.cnt["w0"]

    v_e = S.op("ac", lambda e: e.activation(e_sb[:], cols["dec"][:], AF.Exp),
               waits=[("ld0", v_cols)], inc="ac")
    S.op("dv", lambda e: e.tensor_scalar_mul(ebias[:], e_sb[:], -2047.0),
         waits=[("ac", v_e)], inc="dv")
    S.op("dv", lambda e: e.tensor_scalar_mul(eneg[:], e_sb[:], -1.0),
         waits=[("ac", v_e)], inc="dv")
    for n in tm1m:
        S.op("dv", lambda e, n=n: e.tensor_scalar(
            tm1m[n][:], cols[n][:], -1.0, 1.0, ALU.mult, ALU.add),
            waits=[("ld0", v_cols)], inc="dv")
    v_tiny = None
    for s in range(NSEG):
        v_tiny = S.op("dv", lambda e, s=s: e.tensor_scalar(
            lhsTu[:, s * 16:(s + 1) * 16], lhsTu[:, s * 16:(s + 1) * 16],
            cols["uu"][:, s:s + 1], None, ALU.mult),
            waits=[("w0", v_wconst), ("ld0", v_cols)], inc="dv")

    es_ln = contextlib.ExitStack()
    xpc = [Rg(es_ln, f"xpc{i}", [128, 512], F32) for i in range(4)]
    xbc = [Rg(es_ln, f"xbc{i}", [128, 512], BF16) for i in range(2)]
    xqc = [Rg(es_ln, f"xqc{i}", [128, 512], BF16) for i in range(2)]
    mrow = Rg(es_ln, "mrow", [1, SEGW], F32)
    e2row = Rg(es_ln, "e2row", [1, SEGW], F32)
    varrow = Rg(es_ln, "varrow", [1, SEGW], F32)
    mbf = Rg(es_ln, "mbf", [1, SEGW], BF16)
    rbf = Rg(es_ln, "rbf", [1, SEGW], BF16)

    chunks = [(0, 512), (512, 512), (1024, 1)]
    v_cpd, v_sqd, v_stev = {}, {}, {}
    v_rowev = 0
    for ci, (off, cw) in enumerate(chunks):
        for s in range(NSEG):
            idx = ci * NSEG + s
            pb, b2 = idx % 4, idx % 2
            wd = []
            if idx - 4 in v_cpd:
                wd += [("dv", v_cpd[idx - 4]), ("ac", v_sqd[idx - 4])]
            v_pc = S.dma("sy", xpc[pb][:, 0:cw], x3[:, s, off:off + cw],
                         inc=f"xp{pb}", waits=wd,
                         allow_slow_non_contiguous=(cw == 1))
            wst = [("pe", v_stev[idx - 2])] if idx - 2 in v_stev else []
            v_cpd[idx] = S.op("dv", lambda e, pb=pb, b2=b2, cw=cw: e.tensor_copy(
                xbc[b2][:, 0:cw], xpc[pb][:, 0:cw]),
                waits=[(f"xp{pb}", v_pc)] + wst, inc="dv")
            v_sqd[idx] = S.op("ac", lambda e, pb=pb, b2=b2, cw=cw: e.activation(
                xqc[b2][:, 0:cw], xpc[pb][:, 0:cw], AF.Square),
                waits=[(f"xp{pb}", v_pc)] + wst, inc="ac")

            def stat2(e, b2=b2, cw=cw, s=s):
                e.matmul(str0[:, 0:cw], o128[:], xbc[b2][:, 0:cw],
                         start=(s == 0), stop=(s == NSEG - 1))
                return e.matmul(str1[:, 0:cw], o128[:], xqc[b2][:, 0:cw],
                                start=(s == 0), stop=(s == NSEG - 1))
            w = [("dv", v_cpd[idx]), ("ac", v_sqd[idx]), ("w0", v_wconst)]
            if ci > 0 and s == 0:
                w.append(("dv", v_rowev))
            v_stev[idx] = S.op("pe", stat2, waits=w, inc="pe")
        v_pe = v_stev[ci * NSEG + NSEG - 1]
        S.op("dv", lambda e, off=off, cw=cw: e.tensor_scalar_mul(
            mrow[0:1, off:off + cw], str0[:, 0:cw], 1.0 / C),
            waits=[("pe", v_pe)], inc="dv")
        v_rowev = S.op("dv", lambda e, off=off, cw=cw: e.tensor_scalar_mul(
            e2row[0:1, off:off + cw], str1[:, 0:cw], 1.0 / C),
            waits=[("pe", v_pe)], inc="dv")

    S.op("dv", lambda e: e.tensor_mul(varrow[:], mrow[:], mrow[:]),
         waits=[("dv", v_rowev)], inc="dv")
    v_var = S.op("dv", lambda e: e.tensor_sub(varrow[:], e2row[:], varrow[:]),
                 inc="dv")
    v_std = S.op("ac", lambda e: e.activation(varrow[:], varrow[:], AF.Sqrt,
                                              bias=hahb[0:1, 2:3]),
                 waits=[("dv", v_var), ("ld0", v_cols)], inc="ac")
    v_rec = S.op("dv", lambda e: e.reciprocal(varrow[:], varrow[:]),
                 waits=[("ac", v_std)], inc="dv")
    S.op("dv", lambda e: e.tensor_copy(mbf[:], mrow[:]), inc="dv")
    v_rows = S.op("dv", lambda e: e.tensor_copy(rbf[:], varrow[:]),
                  waits=[("dv", v_rec)], inc="dv")

    v_apl = {}
    for ci, (off, cw) in enumerate(chunks):
        def bcast(e, off=off, cw=cw):
            e.matmul(PJ[0][:, 0:cw], ok1[:], mbf[0:1, off:off + cw],
                     start=True, stop=True)
            return e.matmul(PJ[1][:, 0:cw], ok1[:], rbf[0:1, off:off + cw],
                            start=True, stop=True)
        w = [("dv", v_rows)]
        if ci > 0:
            w.append(("dv", v_apl[(ci - 1) * NSEG + NSEG - 1]))
        v_bc = S.op("pe", bcast, waits=w, inc="pe")
        for s in range(NSEG):
            idx = ci * NSEG + s
            pb, b2 = idx % 4, idx % 2
            if idx - 4 in v_apl:
                wd = [("dv", v_apl[idx - 4])]
            else:
                wd = [("dv", v_cpd[44 + pb]), ("ac", v_sqd[44 + pb])]
            v_pc = S.dma("sy", xpc[pb][:, 0:cw], x3[:, s, off:off + cw],
                         inc=f"xp{pb}", waits=wd,
                         allow_slow_non_contiguous=(cw == 1))
            S.op("dv", lambda e, pb=pb, b2=b2, cw=cw: e.tensor_tensor(
                xbc[b2][:, 0:cw], xpc[pb][:, 0:cw], PJ[0][:, 0:cw], ALU.subtract),
                waits=[(f"xp{pb}", v_pc), ("pe", v_bc)], inc="dv")
            v_apl[idx] = S.op("dv", lambda e, s=s, b2=b2, off=off, cw=cw:
                              e.tensor_tensor(
                xn[:, s * SEGW + off: s * SEGW + off + cw], xbc[b2][:, 0:cw],
                PJ[1][:, 0:cw], ALU.mult), inc="dv")
    v_xnraw = v_apl[2 * NSEG + NSEG - 1]
    hv = xn[:].rearrange("p (s t) -> p s t", s=NSEG)[:, :, 0]
    S.op("dv", lambda e: e.tensor_scalar(hv, hv, hahb[:, 0:1], None, ALU.mult),
         waits=[("dv", v_xnraw), ("ld0", v_cols)], inc="dv")
    v_xn = S.op("dv", lambda e: e.scalar_tensor_tensor(
        hv, xhalo[:], hahb[:, 1:2], hv, ALU.mult, ALU.add), inc="dv")
    if TRUNC <= 1:
        S.emit()
        return nc
    es_ln.close()

    es_g = contextlib.ExitStack()
    wslice = [Rg(es_g, f"wslice{i}", [128, 512], BF16) for i in range(4)]
    tmpb = [Rg(es_g, f"tmpb{i}", [128, TL], BF16) for i in range(4)]
    r_pl = Rg(es_g, "r_pl", [128, NPG * TL], BF16)
    kw_pl = Rg(es_g, "kw_pl", [128, NPG * TL], BF16)
    vtm_pl = Rg(es_g, "vtm_pl", [128, NCH * CQ], BF16)
    gtm_pl = Rg(es_g, "gtm_pl", [128, NCH * CQ], BF16)
    ruk_pl = Rg(es_g, "ruk_pl", [128, NPG * TL], BF16)
    gwfor = Rg(es_g, "gwfor", [128, NPG * TL], BF16)
    gwback = Rg(es_g, "gwback", [128, NPG * TL], BF16)
    rwbuf = Rg(es_g, "rwbuf", [128, NPG * G], BF16)
    spre = Rg(es_g, "spre", [128, NPG * NCH * KD], BF16)
    sfin = Rg(es_g, "sfin", [128, NPG * KD], F32)
    shalf = Rg(es_g, "shalf", [128, NPG * KD], F32)
    recv = Rg(es_g, "recv", [128, 8 * NPG * KD], F32)
    bb_sb = Rg(es_g, "bb_sb", [16, TL], BF16)
    kwT = [Rg(es_g, f"kwT{i}", [128, 128], BF16) for i in range(4)]
    attm = [Rg(es_g, f"attm{i}", [128, 256], BF16) for i in range(2)]
    bbT = [Rg(es_g, f"bbT{i}", [128, 16], BF16) for i in range(2)]
    bonus = [Rg(es_g, f"bonus{i}", [128, CQ], BF16) for i in range(2)]
    ybuf = [Rg(es_g, f"ybuf{i}", [128, CQ], BF16) for i in range(2)]
    y2buf = Rg(es_g, "y2buf", [128, CQ], BF16)
    tgn = [Rg(es_g, f"tgn{i}", [128, CQ], BF16) for i in range(2)]
    gn8 = [Rg(es_g, f"gn8_{i}", [128, HPG], F32) for i in range(5)]
    dbgb = Rg(es_g, "dbgb", [128, TL], F32) if DEBUG else None

    def dump_plane(nm, src, width, dst_off=0, waits=()):
        if nm not in dbg:
            return
        np_ = src.shape[0]
        nchunk = (width + TL - 1) // TL
        for i in range(nchunk):
            lo = i * TL
            hi = min(width, lo + TL)
            vv = S.op("dv", lambda e, lo=lo, hi=hi, np_=np_: e.tensor_copy(
                dbgb[0:np_, 0:hi - lo], src[0:np_, lo:hi]),
                waits=list(waits) + [("sto", S.cnt.get("sto", 0))], inc="dv")
            S.dma("sy", dbg[nm].ap()[0:np_, dst_off + lo:dst_off + hi],
                  dbgb[0:np_, 0:hi - lo], inc="sto", waits=[("dv", vv)])

    dump_plane("xn", xn, NSEG * SEGW, waits=[("dv", v_xn)])

    pj_free = [("dv", v_apl[2 * NSEG + NSEG - 1]),
               ("dv", v_apl[2 * NSEG + NSEG - 1]),
               ("dv", v_rowev), ("dv", v_rowev)]

    def emit_proj(wt_dram, GG, nm, prev_pe_kt, evac, extra_kt0=(),
                  first_waits=()):
        v_pe_kt = {}
        v_bl = {}
        v_wd = {}
        for kt in range(NSEG):
            b = kt % 4
            if kt - 4 in v_pe_kt:
                gate = [("pe", v_pe_kt[kt - 4])]
            elif prev_pe_kt:
                gate = [("pe", prev_pe_kt[NSEG - 4 + kt])]
            else:
                gate = [("dv", v_xn)] + list(first_waits)
            v_wd[kt] = S.dma("po", wslice[b][:],
                             wt_dram.ap()[kt * 128:(kt + 1) * 128,
                                          GG * 512:(GG + 1) * 512],
                             inc=f"ws{b}", waits=gate)
            cur = xn[:, kt * SEGW + 1:(kt + 1) * SEGW]
            prev = xn[:, kt * SEGW:(kt + 1) * SEGW - 1]
            vd = S.op("dv", lambda e, b=b, cur=cur, prev=prev: e.tensor_tensor(
                tmpb[b][:], cur, prev, ALU.subtract), waits=gate, inc="dv")
            v_bl[kt] = ("dv", S.op(
                "dv", lambda e, b=b, prev=prev, kt=kt: e.scalar_tensor_tensor(
                    tmpb[b][:], tmpb[b][:], cols[nm][:, kt:kt + 1], prev,
                    ALU.mult, ALU.add),
                waits=[("dv", vd)], inc="dv"))

            def mms(e, b=b, kt=kt):
                ins = None
                for ob in range(NPG):
                    lhsT = wslice[b][:, ob * 128:(ob + 1) * 128]
                    for hf in range(2):
                        ins = e.matmul(PJ[ob][:, hf * 512:(hf + 1) * 512], lhsT,
                                       tmpb[b][:, hf * 512:(hf + 1) * 512],
                                       start=(kt == 0), stop=(kt == NSEG - 1))
                return ins
            w = [(f"ws{b}", v_wd[kt]), v_bl[kt]]
            if kt == 0:
                w += list(pj_free) + list(extra_kt0)
            v_pe_kt[kt] = S.op("pe", mms, waits=w, inc="pe")
        v_last = v_pe_kt[NSEG - 1]
        evs = []
        for i in range(NPG):
            evs.append(evac(i, PJ[i], [("pe", v_last)]))
            pj_free[i] = evs[i]
        return evs, v_last, v_pe_kt

    def emit_proj_tm(wt_dram, GG, nm, prev_pe_kt, evac_tt):
        v_pe_kt = {}
        v_bl = {}
        v_wd = {}
        for kt in range(NSEG):
            b = kt % 4
            if kt - 4 in v_pe_kt:
                gate = [("pe", v_pe_kt[kt - 4])]
            else:
                gate = [("pe", prev_pe_kt[NSEG - 4 + kt])]
            v_wd[kt] = S.dma("po", wslice[b][:],
                             wt_dram.ap()[kt * 128:(kt + 1) * 128,
                                          GG * 512:(GG + 1) * 512],
                             inc=f"ws{b}", waits=gate)
            cur = xn[:, kt * SEGW + 1:(kt + 1) * SEGW]
            prev = xn[:, kt * SEGW:(kt + 1) * SEGW - 1]
            vd = S.op("dv", lambda e, b=b, cur=cur, prev=prev: e.tensor_tensor(
                tmpb[b][:], cur, prev, ALU.subtract), waits=gate, inc="dv")
            v_bl[kt] = ("dv", S.op(
                "dv", lambda e, b=b, prev=prev, kt=kt: e.scalar_tensor_tensor(
                    tmpb[b][:], tmpb[b][:], cols[nm][:, kt:kt + 1], prev,
                    ALU.mult, ALU.add),
                waits=[("dv", vd)], inc="dv"))

            def mms(e, b=b, kt=kt):
                ins = None
                for tt in range(NCH):
                    lhsT = tmpb[b][:, tt * 128:(tt + 1) * 128]
                    ins = e.matmul(
                        PJ[tt // 2][:, (tt % 2) * 512:(tt % 2) * 512 + 512],
                        lhsT, wslice[b][:, 0:512],
                        start=(kt == 0), stop=(kt == NSEG - 1))
                return ins
            w = [(f"ws{b}", v_wd[kt]), v_bl[kt]]
            if kt == 0:
                w += list(pj_free)
            v_pe_kt[kt] = S.op("pe", mms, waits=w, inc="pe")
        v_last = v_pe_kt[NSEG - 1]
        evs = []
        for tt in range(NCH):
            evs.append(evac_tt(
                tt, PJ[tt // 2][:, (tt % 2) * 512:(tt % 2) * 512 + 512],
                [("pe", v_last)]))
        for j in range(4):
            pj_free[j] = evs[2 * j + 1]
        return evs, v_last, v_pe_kt

    v_allmm_prev = 0
    v_evac_prev = None
    v_sfin_prev = 0
    v_selsum_prev = 0
    v_y5T_prev = 0
    v_red_prev = 0
    v_gnpool_prev = 0
    v_bt_prev = [0, 0]
    v_ci_prev = 0
    v_kwlast = 0
    v_rwlast = 0
    kt_prev = None
    v_evac = {}

    for GG in range(NG):
        v_wf = {}
        v_wbk = {}
        for i in range(NPG):
            s = GG * NPG + i
            wv = [("xin", 0), ("dv", v_tiny), ("pe", v_allmm_prev)]
            v_wbk[i] = S.op("ac", lambda e, i=i, s=s: e.activation(
                gwback[:, i * TL:(i + 1) * TL], iota[:], AF.Exp,
                bias=ebias[:, s:s + 1], scale=e_sb[:, s:s + 1]),
                waits=[("ld0", v_iota), ("dv", v_tiny),
                       ("dv", v_kwlast), ("dv", v_rwlast)], inc="ac")
            v_wf[i] = S.op("ac", lambda e, i=i, s=s: e.activation(
                gwfor[:, i * TL:(i + 1) * TL], iota[:], AF.Exp,
                scale=eneg[:, s:s + 1]), inc="ac")

        def evr(i, pj, w):
            eng = "ac" if i < 2 else "dv"
            if eng == "ac":
                v = S.op("ac", lambda e, i=i, pj=pj: e.activation(
                    r_pl[:, i * TL:(i + 1) * TL], pj[:], AF.Copy),
                    waits=w, inc="ac")
            else:
                v = S.op("dv", lambda e, i=i, pj=pj: e.tensor_copy(
                    r_pl[:, i * TL:(i + 1) * TL], pj[:]), waits=w, inc="dv")
            return (eng, v)
        evs, v_pe_r, kt_r = emit_proj(
            d_w["wrt"], GG, "tmr", kt_prev, evr,
            extra_kt0=[v_evac_prev] if v_evac_prev else [],
            first_waits=[("dv", v_xn)] if GG == 0 else
            [("pe", v_allmm_prev)])
        v_r_ac = max([v for d, v in evs if d == "ac"], default=0)
        v_r_dv = max([v for d, v in evs if d == "dv"], default=0)
        v_r = [("ac", v_r_ac), ("dv", v_r_dv)]
        if TRUNC <= 2:
            S.emit()
            return nc

        def evk(i, pj, w):
            eng = "ac" if i < 2 else "dv"
            if eng == "ac":
                v = S.op("ac", lambda e, i=i, pj=pj: e.activation(
                    ruk_pl[:, i * TL:(i + 1) * TL], pj[:], AF.Copy),
                    waits=w, inc="ac")
            else:
                v = S.op("dv", lambda e, i=i, pj=pj: e.tensor_copy(
                    ruk_pl[:, i * TL:(i + 1) * TL], pj[:]), waits=w, inc="dv")
            return (eng, v)
        evs, v_pe_k, kt_k = emit_proj(d_w["wkt"], GG, "tmk", kt_r, evk)
        v_kc_ac = max([v for d, v in evs if d == "ac"], default=0)
        v_kc_dv = max([v for d, v in evs if d == "dv"], default=0)

        v_kwd = {}
        for i in range(NPG):
            v_kwd[i] = S.op("dv", lambda e, i=i: e.tensor_tensor(
                kw_pl[:, i * TL:(i + 1) * TL], ruk_pl[:, i * TL:(i + 1) * TL],
                gwback[:, i * TL:(i + 1) * TL], ALU.mult),
                waits=[("ac", v_wbk[i]), ("ac", v_kc_ac), ("dv", v_kc_dv)],
                inc="dv")
        v_kwlast = v_kwd[NPG - 1]
        v_ruk = {}
        for i in range(NPG):
            v_ruk[i] = S.op("dv", lambda e, i=i: e.tensor_tensor(
                ruk_pl[:, i * TL:(i + 1) * TL], ruk_pl[:, i * TL:(i + 1) * TL],
                r_pl[:, i * TL:(i + 1) * TL], ALU.mult),
                waits=[("dv", v_kwd[i])] + v_r, inc="dv")
        v_k = S.cnt["dv"]

        def bbmm(e, GG=GG):
            ins = None
            for hf in range(2):
                for li in range(NPG):
                    ins = e.matmul(PJ[1][0:16, hf * 512:(hf + 1) * 512],
                                   lhsTu[:, (GG * NPG + li) * 16:
                                         (GG * NPG + li + 1) * 16],
                                   ruk_pl[:, li * TL + hf * 512:
                                          li * TL + hf * 512 + 512],
                                   start=(li == 0), stop=(li == NPG - 1))
            return ins
        v_bbpe = S.op("pe", bbmm,
                      waits=[("dv", v_ruk[NPG - 1])] + [pj_free[1]], inc="pe")
        v_bb = S.op("dv", lambda e: e.tensor_copy(bb_sb[:], PJ[1][0:16, 0:TL]),
                    waits=[("pe", v_bbpe), ("bt0", v_bt_prev[0]),
                           ("bt1", v_bt_prev[1])], inc="dv")
        if TRUNC <= 3:
            S.emit()
            return nc
        pj_free[1] = ("dv", v_bb)
        dump_plane("bb", bb_sb, TL, dst_off=GG * TL, waits=[("dv", v_bb)])

        def evv(tt, pjs, w):
            return ("dv", S.op("dv", lambda e, tt=tt, pjs=pjs: e.tensor_copy(
                vtm_pl[:, tt * CQ:(tt + 1) * CQ], pjs), waits=w, inc="dv"))
        evs, v_pe_v, kt_v = emit_proj_tm(d_w["wvt"], GG, "tmv", kt_k, evv)
        v_v = evs[NCH - 1]
        v_vtm2 = [v_v]

        z = spre[:].rearrange("p (l n k) -> p l n k", l=NPG, n=NCH)[:, :, 0, :]
        v_z = S.op("dv", lambda e: e.memset(z, 0.0),
                   waits=[("pe", v_allmm_prev)], inc="dv")
        v_zf = S.op("dv", lambda e: e.memset(sfin[:], 0.0),
                    waits=[("cci", v_ci_prev)], inc="dv")
        v_supd, v_ssev = {}, {}
        v_sfin = None
        for li in range(NPG):
            ss = ss_sl[li % 2]
            sfs = sfin[:, li * KD:(li + 1) * KD]
            for n in range(NCH):
                j = li * NCH + n
                kb = j % 4
                wt = [("dv", v_kwd[li])] + v_vtm2
                if j - 3 in v_supd:
                    wt.append(("pe", v_supd[j - 3]))
                v_t = S.tpose(kwT[kb][:], kw_pl[:, li * TL + n * 128:
                                                li * TL + n * 128 + 128],
                              inc=f"kwt{kb}", waits=wt)

                def supd(e, li=li, n=n, kb=kb, ss=ss):
                    ins = None
                    for hh in range(2):
                        ins = e.matmul(ss[hh],
                                       kwT[kb][:, hh * 64:(hh + 1) * 64],
                                       vtm_pl[:, n * CQ + li * 128 + hh * 64:
                                              n * CQ + li * 128 + hh * 64 + 64],
                                       start=True, stop=True,
                                       tile_position=(0, hh * 64))
                    return ins
                w = [(f"kwt{kb}", v_t)]
                if (li, n - 1) in v_ssev:
                    w.append(("dv", v_ssev[(li, n - 1)]))
                if n == 0:
                    w.append(("dv", v_zf))
                    if (li - 2, NCH - 1) in v_ssev:
                        w.append(("dv", v_ssev[(li - 2, NCH - 1)]))
                    elif v_sfin_prev:
                        w.append(("dv", v_sfin_prev))
                    if li < 2:
                        w.append(pj_free[3])
                        w.append(pj_free[2])
                v_supd[j] = S.op("pe", supd, waits=w, inc="pe")
                S.op("dv", lambda e, ss=ss, sfs=sfs: e.tensor_add(
                    sfs[0:64, :], sfs[0:64, :], ss[0]),
                    waits=[("pe", v_supd[j])], inc="dv")
                vadd = S.op("dv", lambda e, ss=ss, sfs=sfs: e.tensor_add(
                    sfs[64:128, :], sfs[64:128, :], ss[1]), inc="dv")
                if n < NCH - 1:
                    v_ssev[(li, n)] = S.op(
                        "dv", lambda e, li=li, n=n, sfs=sfs: e.tensor_copy(
                            spre[:, (li * NCH + n + 1) * KD:
                                 (li * NCH + n + 2) * KD], sfs),
                        inc="dv")
                else:
                    v_ssev[(li, n)] = vadd
                    v_sfin = vadd
        v_sfin_prev = v_sfin
        pj_free[3] = ("dv", v_sfin)
        pj_free[2] = ("dv", v_sfin)
        if TRUNC <= 4:
            S.emit()
            return nc

        v_ci = S.dma("po", d_ccin.ap().rearrange("(l p) k -> p l k", p=128),
                     sfin[:].rearrange("p (l k) -> p l k", l=NPG),
                     inc="cci", waits=[("dv", v_sfin)])
        v_ci_prev = v_ci
        v_cc = S.op("po", lambda e: e.collective_compute(
            "AllGather", ALU.bypass, replica_groups=[list(range(8))],
            ins=[d_ccin.ap().opt()], outs=[d_ccout.ap().opt()]),
            waits=[("cci", v_ci)], inc="cc")
        v_rc = S.dma("po", recv[:].rearrange("p (q l k) -> p q l k", q=8, l=NPG),
                     d_ccout.ap().rearrange("(q l p) k -> p q l k", p=128, l=NPG),
                     inc="ccr", waits=[("cc", v_cc), ("dv", v_selsum_prev)])
        S.op("dv", lambda e: e.tensor_scalar(
            shalf[:], recv[:, 0:NPG * KD], sel[:, 0:1], None, ALU.mult),
            waits=[("ccr", v_rc), ("ld0", v_cols)], inc="dv")
        for rr in range(1, 8):
            v_selsum_prev = S.op("dv", lambda e, rr=rr: e.scalar_tensor_tensor(
                shalf[:], recv[:, rr * NPG * KD:(rr + 1) * NPG * KD],
                sel[:, rr:rr + 1], shalf[:], ALU.mult, ALU.add), inc="dv")
        spv = spre[:].rearrange("p (l n k) -> p l n k", l=NPG, n=NCH)
        shv = shalf[:].rearrange("p (l k) -> p l k", l=NPG).unsqueeze(2) \
            .broadcast_to([128, NPG, NCH, KD])
        v_spre = S.op("dv", lambda e: e.tensor_tensor(spv, spv, shv, ALU.add),
                      waits=[("dv", v_z), ("dv", v_sfin)], inc="dv")
        dump_plane("spre", spre, NPG * NCH * KD, dst_off=GG * NPG * NCH * KD,
                   waits=[("dv", v_spre)])
        if GG == 0:
            dump_plane("vtm3", vtm_pl, NCH * CQ, waits=[("dv", v_spre)])
            dump_plane("esb", e_sb, NSEG, dst_off=0, waits=[("dv", v_tiny)])
            dump_plane("esb", eneg, NSEG, dst_off=NSEG, waits=[("dv", v_tiny)])
            dump_plane("esb", ebias, NSEG, dst_off=2 * NSEG,
                       waits=[("dv", v_tiny)])
        if TRUNC <= 5:
            S.emit()
            return nc

        def evg(tt, pjs, w):
            va = S.op("ac", lambda e, tt=tt, pjs=pjs: e.activation(
                gtm_pl[:, tt * CQ:(tt + 1) * CQ], pjs, AF.Silu),
                waits=w, inc="ac")
            return ("dv", S.op("dv", lambda e: e.tensor_copy(
                gsink[:], e_sb[:, 0:1]), waits=[("ac", va)], inc="dv"))
        evs, v_pe_g, kt_g = emit_proj_tm(d_w["wgt"], GG, "tmg", kt_v, evg)
        kt_prev = kt_g
        v_gtm2 = [evs[NCH - 1]]
        if TRUNC <= 51:
            S.emit()
            return nc

        dump_plane("r", r_pl, TL, dst_off=GG * TL, waits=v_r)
        dump_plane("kw", kw_pl, TL, dst_off=GG * TL,
                   waits=[("dv", v_kwd[NPG - 1])])

        v_mask = {}
        v_evac = {}
        v_pairev = {}
        v_y2c = {}
        v_t1c = {}
        for n in range(NCH):
            pj = PJ[n % 2]
            v_rwp = {}
            for li in range(NPG):
                s = GG * NPG + li
                va = S.op("ac", lambda e, li=li, s=s, n=n: e.activation(
                    rwbuf[:, li * G:(li + 1) * G], iota[:, n * G:(n + 1) * G],
                    AF.Exp, scale=eneg[:, s:s + 1]),
                    waits=[("dv", v_tiny), ("ld0", v_iota),
                           ("pe", v_allmm_prev)], inc="ac")
                v_rwp[li] = S.op("dv", lambda e, li=li, n=n: e.tensor_tensor(
                    rwbuf[:, li * G:(li + 1) * G],
                    r_pl[:, li * TL + n * G: li * TL + (n + 1) * G],
                    rwbuf[:, li * G:(li + 1) * G], ALU.mult),
                    waits=[("ac", va)] + v_r, inc="dv")
            nb = n % 2
            hoff = (GG % 2) * 8
            wbt = [("dv", v_bb)]
            if n - 2 in v_evac:
                wbt.append(("dv", v_evac[n - 2]))
            v_bt = S.tpose(bbT[nb][:], bb_sb[0:16, n * 128:(n + 1) * 128],
                           inc=f"bt{nb}", waits=wbt)
            v_bt_prev[nb] = v_bt
            v_bo = S.op("dv", lambda e, n=n, nb=nb, hoff=hoff: e.tensor_tensor(
                bonus[nb][:].rearrange("p (h k) -> p h k", h=HPG),
                vtm_pl[:, n * CQ:(n + 1) * CQ].rearrange(
                    "p (h k) -> p h k", h=HPG),
                bbT[nb][:, hoff:hoff + HPG].unsqueeze(2)
                .broadcast_to([128, HPG, KD]), ALU.mult),
                waits=[(f"bt{nb}", v_bt)] + v_vtm2, inc="dv")
            if TRUNC <= 522 and n == 0:
                S.emit()
                return nc
            for li in range(NPG):
                j = n * NPG + li
                ab = j % 2
                att = PJ[2 + ab]

                def attmm(e, li=li, n=n, att=att):
                    ins = None
                    for hh in range(2):
                        ins = e.matmul(att[:, hh * 512: hh * 512 + 128],
                                       kw_pl[hh * 64:(hh + 1) * 64,
                                             li * TL + n * G:
                                             li * TL + (n + 1) * G],
                                       r_pl[hh * 64:(hh + 1) * 64,
                                            li * TL + n * G:
                                            li * TL + (n + 1) * G],
                                       start=True, stop=True,
                                       tile_position=(hh * 64, 0))
                    return ins
                w = [("dv", v_kwd[li])] + v_r
                if j - 2 in v_mask:
                    w.append(("dv", v_mask[j - 2]))
                elif n == 0 and li < 2:
                    w.append(pj_free[2 + ab])
                v_apc = S.op("pe", attmm, waits=w, inc="pe")
                mb = li % 2
                attv2 = att[:].rearrange("p (b c) -> p b c", b=2)[:, :, 0:128]
                v_mask[j] = S.op("dv", lambda e, mb=mb, attv2=attv2:
                                 e.tensor_tensor(
                    attm[mb][:].rearrange("p (b c) -> p b c", b=2), attv2,
                    mask2[:].rearrange("p (b c) -> p b c", b=2), ALU.mult),
                    waits=[("pe", v_apc), ("w0", v_wconst)], inc="dv")

                if TRUNC <= 523 and n == 0 and li == 1:
                    S.emit()
                    return nc

                def wkvmm(e, li=li, n=n, mb=mb, GG=GG):
                    sl = PJ[(n * NPG + li) % 2]
                    ins = None
                    for hh in range(2):
                        e.matmul(sl[:, hh * 512: hh * 512 + 64],
                                 attm[mb][:, hh * 128:(hh + 1) * 128],
                                 vtm_pl[:, n * CQ + li * 128 + hh * 64:
                                        n * CQ + li * 128 + hh * 64 + 64],
                                 start=True, stop=False)
                        e.matmul(sl[:, hh * 512: hh * 512 + 64],
                                 r_pl[hh * 64:(hh + 1) * 64,
                                      li * TL + n * G: li * TL + (n + 1) * G],
                                 spre[hh * 64:(hh + 1) * 64,
                                      (li * NCH + n) * KD:
                                      (li * NCH + n + 1) * KD],
                                 start=False, stop=False,
                                 tile_position=(hh * 64, 0))
                        ins = e.matmul(sl[:, hh * 512: hh * 512 + 64],
                                       rwbuf[hh * 64:(hh + 1) * 64,
                                             li * G:(li + 1) * G],
                                       wkvp[hh * 64:(hh + 1) * 64,
                                            (GG * NPG + li) * KD:
                                            (GG * NPG + li + 1) * KD],
                                       start=False, stop=True,
                                       tile_position=(hh * 64, 0))
                    return ins
                w = [("dv", v_mask[j]), ("dv", v_rwp[li]), ("dv", v_spre),
                     ("w0", v_wconst)] + v_vtm2
                js = n * NPG + li
                if js - 2 in v_pairev:
                    w.append(("dv", v_pairev[js - 2]))
                elif js < 2:
                    w.append(pj_free[0])
                v_wk = S.op("pe", wkvmm, waits=w, inc="pe")
                if TRUNC <= 524 and n == 0 and li == 0:
                    S.emit()
                    return nc
                wev = []
                if li == 0 and n - 2 in v_y2c:
                    wev = [("ac", v_y2c[n - 2]), ("po", v_t1c[n - 2])]
                slv = PJ[js % 2][:].rearrange("p (b c) -> p b c", b=2)[:, :, 0:64]
                v_pairev[js] = S.op(
                    "dv", lambda e, li=li, nb=n % 2, slv=slv: e.tensor_tensor(
                        ybuf[nb][:, li * 128:(li + 1) * 128].rearrange(
                            "p (b c) -> p b c", b=2),
                        slv,
                        bonus[nb][:, li * 128:(li + 1) * 128].rearrange(
                            "p (b c) -> p b c", b=2), ALU.add),
                    waits=[("pe", v_wk), ("dv", v_bo)] + wev, inc="dv")
            v_allmm = S.cnt["pe"]
            v_allmm_prev = v_allmm
            v_evac[n] = v_pairev[n * NPG + NPG - 1]
            if TRUNC <= 52 and n == 0:
                S.emit()
                return nc
            dump_plane("y", ybuf[nb], CQ, dst_off=(GG * NCH + n) * CQ,
                       waits=[("dv", v_evac[n])])

            v_y2 = S.op("ac", lambda e, nb=nb: e.activation(
                y2buf[:], ybuf[nb][:], AF.Square),
                waits=[("dv", v_evac[n]), ("dv", v_red_prev)], inc="ac")
            v_y2c[n] = v_y2
            yv = ybuf[nb][:].rearrange("p (h k) -> p h k", h=HPG)
            S.op("dv", lambda e, yv=yv: e.tensor_reduce(
                gn8[0][:], yv, AX.X, ALU.add),
                waits=[("dv", v_evac[n]), ("po", v_gnpool_prev)], inc="dv")
            S.op("dv", lambda e: e.tensor_reduce(
                gn8[1][:], y2buf[:].rearrange("p (h k) -> p h k", h=HPG),
                AX.X, ALU.add), waits=[("ac", v_y2)], inc="dv")
            S.op("dv", lambda e: e.tensor_scalar_mul(gn8[0][:], gn8[0][:],
                                                     1.0 / KD), inc="dv")
            S.op("dv", lambda e: e.tensor_scalar_mul(gn8[1][:], gn8[1][:],
                                                     1.0 / KD), inc="dv")
            S.op("dv", lambda e: e.tensor_mul(gn8[2][:], gn8[0][:], gn8[0][:]),
                 inc="dv")
            v_gv = S.op("dv", lambda e: e.tensor_sub(gn8[2][:], gn8[1][:],
                                                     gn8[2][:]), inc="dv")
            v_gs = S.op("ac", lambda e: e.activation(gn8[3][:], gn8[2][:],
                                                     AF.Sqrt, bias=hahb[:, 3:4]),
                        waits=[("dv", v_gv)], inc="ac")
            v_gr = S.op("dv", lambda e: e.reciprocal(gn8[4][:], gn8[3][:]),
                        waits=[("ac", v_gs)], inc="dv")
            v_red_prev = v_gr
            if TRUNC <= 53 and n == 0:
                S.emit()
                return nc
            muv = gn8[0][:].unsqueeze(2).broadcast_to([128, HPG, KD])
            rsv = gn8[4][:].unsqueeze(2).broadcast_to([128, HPG, KD])
            v_t1 = S.op("po", lambda e, yv=yv, muv=muv: e.tensor_tensor(
                tgn[0][:].rearrange("p (h k) -> p h k", h=HPG), yv, muv,
                ALU.subtract),
                waits=[("dv", v_gr), ("yngt", v_y5T_prev)], inc="po")
            v_t1c[n] = v_t1
            v_t2 = S.op("po", lambda e, rsv=rsv: e.tensor_tensor(
                tgn[1][:].rearrange("p (h k) -> p h k", h=HPG),
                tgn[0][:].rearrange("p (h k) -> p h k", h=HPG), rsv, ALU.mult),
                inc="po")
            lo = GG * CQ
            v_t3 = S.op("po", lambda e, lo=lo: e.tensor_tensor(
                tgn[0][:], tgn[1][:], lnw[:, lo:lo + CQ], ALU.mult),
                waits=[("w0", v_wconst)], inc="po")
            v_gnpool_prev = v_t3
            v_t4 = S.op("dv", lambda e, lo=lo: e.tensor_tensor(
                tgn[1][:], tgn[0][:], lnb[:, lo:lo + CQ], ALU.add),
                waits=[("po", v_t3)], inc="dv")
            v_y5 = S.op("dv", lambda e, n=n: e.tensor_tensor(
                tgn[0][:], tgn[1][:], gtm_pl[:, n * CQ:(n + 1) * CQ], ALU.mult),
                waits=v_gtm2 + [("dv", v_t4)], inc="dv")
            if TRUNC <= 54 and n == 0:
                S.emit()
                return nc
            for li in range(NPG):
                v_y5T_prev = S.tpose(
                    yngT[:, (GG * NPG + li) * TL + n * 128:
                         (GG * NPG + li) * TL + n * 128 + 128],
                    tgn[0][:, li * 128:(li + 1) * 128],
                    inc="yngt", waits=[("dv", v_y5)])
            dump_plane("yng", tgn[0], CQ, dst_off=(GG * NCH + n) * CQ,
                       waits=[("dv", v_y5)])
        if TRUNC <= 6:
            S.emit()
            return nc
        v_evac_prev = ("dv", v_evac[NCH - 1])
        last = NCH * NPG - 1
        pj_free[0] = ("dv", v_pairev[last - 1])
        pj_free[1] = ("dv", v_pairev[last])
        pj_free[2] = ("dv", v_mask[last - 1])
        pj_free[3] = ("dv", v_mask[last])
    v_yngT = S.cnt["yngt"]
    dump_plane("yngT", yngT, NSEG * TL, waits=[("yngt", v_yngT)])

    v_dbg_done = S.cnt["dv"]
    era_gate = [("pe", S.cnt["pe"]), ("yngt", S.cnt["yngt"]),
                ("po", S.cnt["po"]), ("ac", S.cnt["ac"]),
                ("dv", v_dbg_done)]
    es_g.close()

    es_f = contextlib.ExitStack()
    gffn = Rg(es_f, "gffn", [128, NSEG * TL], BF16)
    wband = [Rg(es_f, f"wband{i}", [128, NSEG * 128], BF16) for i in range(2)]
    qs = [Rg(es_f, f"qs{i}", [128, C], F32) for i in range(2)]
    qTs = [Rg(es_f, f"qTs{i}", [128, NSEG * 128], BF16) for i in range(2)]
    qmask = Rg(es_f, "qmask", [128, C], BF16)
    th_sb = Rg(es_f, "th_sb", [128, C], F32)
    qth = Rg(es_f, "qth", [128, NSEG], F32)
    thrw = Rg(es_f, "thrw", [1, C], F32)
    x2b = [Rg(es_f, f"x2b{i}", [128, TL], F32) for i in range(2)]
    outb = [Rg(es_f, f"outb{i}", [128, TL], F32) for i in range(2)]

    v_red = {}
    for ot in range(NSEG):
        b = ot % 2
        wq = ([("dv", v_red[ot - 2])] if ot - 2 in v_red else list(era_gate))
        v_qd = S.dma("sy", qs[b][:], d_qw.ap()[ot * 128:(ot + 1) * 128, :],
                     inc=f"q{b}", waits=wq)
        v_red[ot] = S.op("dv", lambda e, b=b, ot=ot: e.tensor_reduce(
            qth[:, ot:ot + 1], qs[b][:], AX.X, ALU.add,
            apply_absolute_value=True), waits=[(f"q{b}", v_qd)], inc="dv")
    v_thd = S.dma("po", d_throw.ap(), qth[:],
                  waits=[("dv", v_red[NSEG - 1])], inc="cci")
    v_thr = S.dma("sy", thrw[:].rearrange("a (t p) -> a t p", t=NSEG),
                  d_throw.ap().transpose([1, 0]).unsqueeze(0),
                  inc="x20", waits=[("cci", v_thd)],
                  allow_slow_non_contiguous=True)
    v_ths = S.op("dv", lambda e: e.tensor_scalar_mul(
        thrw[:], thrw[:], 0.7 / C), waits=[("x20", v_thr)], inc="dv")
    def thbc(e):
        ins = None
        for j in range(4):
            ins = e.matmul(PJ[2 + j // 2][:, (j % 2) * 512:(j % 2) * 512 + 512],
                           ok1f[:], thrw[0:1, j * 512:(j + 1) * 512],
                           start=True, stop=True)
        return ins
    v_thb = S.op("pe", thbc, waits=[("dv", v_ths), ("w0", v_wconst),
                                    pj_free[2], pj_free[3]], inc="pe")
    v_the = {}
    for j in range(4):
        v_the[j] = S.op("dv", lambda e, j=j: e.tensor_copy(
            th_sb[:, j * 512:(j + 1) * 512],
            PJ[2 + j // 2][:, (j % 2) * 512:(j % 2) * 512 + 512]),
            waits=[("pe", v_thb)], inc="dv")
    v_thsb = v_the[3]

    v_ffn, v_qpe, v_msk = {}, {}, {}
    for ot in range(NSEG):
        b = ot % 2
        wq = ([("dv", v_msk[ot - 2])] if ot - 2 in v_msk
              else [("dv", v_red[NSEG - 2 + ot])])
        v_qd = S.dma("sy", qs[b][:], d_qwt.ap()[ot], inc=f"q{b}", waits=wq)
        v_sg = S.op("ac", lambda e, b=b: e.activation(
            qTs[b][:], qs[b][:], AF.Sign),
            waits=[(f"q{b}", v_qd)] +
            ([("pe", v_qpe[ot - 2])] if ot - 2 in v_qpe else []), inc="ac")
        v_ab = S.op("ac", lambda e, b=b: e.activation(
            qs[b][:], qs[b][:], AF.Abs), inc="ac")
        v_mk = S.op("dv", lambda e, b=b, ot=ot: e.tensor_tensor(
            qmask[:].rearrange("p (k c) -> p k c", k=NSEG),
            qs[b][:].rearrange("p (k c) -> p k c", k=NSEG),
            th_sb[:, ot * 128:(ot + 1) * 128].unsqueeze(1)
            .broadcast_to([128, NSEG, 128]), ALU.is_gt),
            waits=[("ac", v_ab), ("dv", v_thsb)], inc="dv")
        v_msk[ot] = v_mk
        v_qn = S.op("dv", lambda e, b=b: e.tensor_tensor(
            qTs[b][:], qmask[:], qTs[b][:], ALU.mult),
            waits=[("ac", v_sg), ("dv", v_mk)], inc="dv")
        pj = PJ[ot % 2]

        def qmms(e, b=b, pj=pj):
            ins = None
            for kt in range(NSEG):
                lhsT = qTs[b][:, kt * 128:(kt + 1) * 128]
                for hf in range(2):
                    ins = e.matmul(pj[:, hf * 512:(hf + 1) * 512], lhsT,
                                   xn[:, kt * SEGW + 1 + hf * 512:
                                      kt * SEGW + 1 + hf * 512 + 512],
                                   start=(kt == 0), stop=(kt == NSEG - 1))
            return ins
        w = [("dv", v_qn), ("dv", v_xn)]
        if ot - 2 in v_ffn:
            w.append(("ac", v_ffn[ot - 2]))
        elif ot < 2:
            w.append(pj_free[ot])
        v_qpe[ot] = S.op("pe", qmms, waits=w, inc="pe")
        v_ffn[ot] = S.op("ac", lambda e, ot=ot, pj=pj: e.activation(
            gffn[:, ot * TL:(ot + 1) * TL], pj[:], AF.Silu,
            bias=cols["qb"][:, ot:ot + 1]),
            waits=[("pe", v_qpe[ot]), ("ld0", v_cols)], inc="ac")
    v_gffn = v_ffn[NSEG - 1]
    if TRUNC <= 8:
        S.emit()
        return nc

    def dump_plane2(nm, src, width):
        if nm not in dbg:
            return
        for i in range((width + TL - 1) // TL):
            lo = i * TL
            hi = min(width, lo + TL)
            vv = S.op("dv", lambda e, lo=lo, hi=hi: e.tensor_copy(
                x2b[0][:, 0:hi - lo], src[:, lo:hi]),
                waits=[("ac", v_gffn)], inc="dv")
            S.dma("sy", dbg[nm].ap()[:, lo:hi], x2b[0][:, 0:hi - lo],
                  inc="sto", waits=[("dv", vv)])
    dump_plane2("gffn", gffn, NSEG * TL)

    v_wo, v_oev, v_st = {}, {}, {}
    v_gfd = S.cnt["dv"]
    for ot in range(NSEG):
        b = ot % 2
        w = ([("pe", v_wo[ot - 2])] if ot - 2 in v_wo
             else list(era_gate))
        v_wd = S.dma("po",
                     wband[b][:].rearrange("p (kt oc) -> p kt oc", kt=NSEG),
                     d_w["wot"].ap().rearrange("(kt p) o -> p kt o", p=128)
                     [:, :, ot * 128:(ot + 1) * 128],
                     inc=f"wb{b}", waits=w)
        w2 = ([("dv", v_oev[ot - 2])] if ot - 2 in v_oev
              else [("dv", v_gfd)] + list(era_gate))
        v_x2 = S.dma("sy", x2b[b][:], d_x.ap()[ot * 128:(ot + 1) * 128, 1:SEGW],
                     inc=f"x2{b}", waits=w2)
        pj = PJ[2 + ot % 2]

        def womms(e, b=b, pj=pj):
            ins = None
            for kt in range(NSEG):
                lhsT = wband[b][:, kt * 128:(kt + 1) * 128]
                for hf in range(2):
                    ins = e.matmul(pj[:, hf * 512:(hf + 1) * 512], lhsT,
                                   yngT[:, kt * TL + hf * 512:
                                        kt * TL + hf * 512 + 512],
                                   start=(kt == 0), stop=(kt == NSEG - 1))
            return ins
        w3 = [(f"wb{b}", v_wd), ("yngt", v_yngT)]
        if ot - 2 in v_oev:
            w3.append(("dv", v_oev[ot - 2]))
        elif ot < 2:
            w3.append(("dv", v_the[2 * ot + 1]))
        v_wo[ot] = S.op("pe", womms, waits=w3, inc="pe")
        wob = [("pe", v_wo[ot]), ("ac", v_gffn)]
        if ot - 2 in v_st:
            wob.append((f"st{b}", v_st[ot - 2]))
        S.op("dv", lambda e, ot=ot, b=b, pj=pj: e.tensor_tensor(
            outb[b][:], pj[:], gffn[:, ot * TL:(ot + 1) * TL], ALU.mult),
            waits=wob, inc="dv")
        v_oev[ot] = S.op("dv", lambda e, b=b: e.tensor_add(
            outb[b][:], outb[b][:], x2b[b][:]),
            waits=[(f"x2{b}", v_x2)], inc="dv")
        v_st[ot] = S.dma("sy", d_out.ap()[ot * 128:(ot + 1) * 128, :], outb[b][:],
                         inc=f"st{b}", waits=[("dv", v_oev[ot])])

    S.op("sy", lambda e: e.nop(),
         waits=[("st0", S.cnt.get("st0", 0)), ("st1", S.cnt.get("st1", 0)),
                ("sto", S.cnt.get("sto", 0))])
    es_f.close()
    S.emit()
    return nc


def _prep_inputs(inputs):
    f32 = np.float32
    x = np.ascontiguousarray(np.asarray(inputs["L0_attn_x"], f32))
    state = np.asarray(inputs["L0_attn_state"], f32).reshape(C)
    wkvstate = np.asarray(inputs["L0_attn_wkvstate"], f32)

    W = {}
    for nm, key in (("wrt", "L0_attn_w_r"), ("wkt", "L0_attn_w_k"),
                    ("wvt", "L0_attn_w_v"), ("wgt", "L0_attn_w_g"),
                    ("wot", "L0_attn_w_o")):
        W[nm] = np.ascontiguousarray(np.asarray(inputs[key], f32).T)
    qw = np.ascontiguousarray(np.asarray(inputs["L0_ffn_weight"], f32))

    def band(WT):
        a = WT.reshape(NSEG, 128, NSEG, 128)
        a = np.ascontiguousarray(a.transpose(2, 1, 0, 3))
        return a.reshape(NSEG, 128, NSEG * 128)

    qwt = band(np.ascontiguousarray(qw.T))

    def fmcols(v):
        return np.ascontiguousarray(np.asarray(v, f32).reshape(NSEG, 128).T)

    colmap = {"tmr": inputs["L0_attn_time_mix_r"],
              "tmk": inputs["L0_attn_time_mix_k"],
              "tmv": inputs["L0_attn_time_mix_v"],
              "tmg": inputs["L0_attn_time_mix_g"],
              "uu": inputs["L0_attn_time_faaaa"],
              "dec": inputs["L0_attn_time_decay"],
              "qb": inputs["L0_ffn_bias"]}
    cols = {k: fmcols(np.asarray(v, f32).reshape(C)) for k, v in colmap.items()}
    lnwbc = np.ascontiguousarray(np.broadcast_to(
        np.asarray(inputs["L0_attn_ln_w"], f32).reshape(1, C), (128, C)))
    lnbbc = np.ascontiguousarray(np.broadcast_to(
        np.asarray(inputs["L0_attn_ln_b"], f32).reshape(1, C), (128, C)))
    wkvp = np.ascontiguousarray(wkvstate.reshape(NSEG, 128, KD)).reshape(
        NSEG * 128, KD)
    mq = (np.arange(128)[:, None] < np.arange(128)[None, :]).astype(f32)
    mask2 = np.ascontiguousarray(np.concatenate([mq, mq], axis=1))
    ct = np.arange(NSEG * 128)
    bones = np.ascontiguousarray(
        ((ct[:, None] // KD) % 16 == np.arange(16)[None, :]).astype(f32))

    common = dict(qw=qw, qwt=qwt, lnwbc=lnwbc, lnbbc=lnbbc, wkvp=wkvp,
                  mask2=mask2, bones=bones, **W, **cols)
    in_maps = []
    for core in range(8):
        b, half = core // 2, core % 2
        t0 = half * TL
        xs = x[b, t0:t0 + TL]
        halo = state if half == 0 else x[b, t0 - 1]
        xfm = np.ascontiguousarray(
            np.concatenate([halo[None, :], xs], axis=0).T)
        xhalov = np.ascontiguousarray(halo.astype(f32).reshape(NSEG, 128).T)
        iota = np.ascontiguousarray(np.broadcast_to(
            (t0 + np.arange(TL)).astype(f32)[None, :], (128, TL)))
        selv = np.zeros((128, 8), f32)
        if half == 1:
            selv[:, core - 1] = 1.0
        hahbv = np.zeros((128, 4), f32)
        hahbv[:, 0] = 1.0 if half == 1 else 0.0
        hahbv[:, 1] = 0.0 if half == 1 else 1.0
        hahbv[:, 2] = EPS_LN
        hahbv[:, 3] = EPS_GN
        im = dict(common)
        im.update(x=xfm, xhalo=xhalov, iota=iota, sel=selv, hahb=hahbv)
        in_maps.append(im)
    return in_maps


_CACHED = {}


def kernel(**inputs):
    if "nc" not in _CACHED:
        _CACHED["nc"] = build_program()
    nc = _CACHED["nc"]
    in_maps = _prep_inputs(inputs)
    res = run_bass_kernel_spmd(nc, in_maps, core_ids=list(range(8)))
    out = np.zeros((B, T, C), np.float32)
    for core in range(8):
        b, half = core // 2, core % 2
        o = res.results[core]["out"]
        out[b, half * TL:(half + 1) * TL, :] = o.T
    if DEBUG:
        kernel.dbg = [{k: np.asarray(res.results[c]["dbg_" + k]) for k in DEBUG}
                      for c in range(8)]
    return out


# version 05
# speedup vs baseline: 1.5025x; 1.0069x over previous
import contextlib
import numpy as np
import ml_dtypes
import concourse.bass as bass
import concourse.mybir as mybir
from concourse.bass_utils import run_bass_kernel_spmd

F32 = mybir.dt.float32
BF16 = mybir.dt.bfloat16
AF = mybir.ActivationFunctionType
ALU = mybir.AluOpType
AX = mybir.AxisListType
bfd = ml_dtypes.bfloat16

B, T, C, H, KD = 4, 2048, 2048, 32, 64
TL = 1024
NSEG = 16
SEGW = TL + 1
G = 128
NCH = TL // G
NG = 4
NPG = NSEG // NG
HPG = H // NG
CQ = NPG * 128
EPS_LN = 1e-5
EPS_GN = 0.00064

DEBUG = []
TRUNC = 9999


class Sched:
    ENG = ("pe", "dv", "ac", "po", "sy")

    def __init__(self, nc):
        self.nc = nc
        self.items = {e: [] for e in self.ENG}
        self.sems = {}
        self.cnt = {}

    def sem(self, name):
        if name not in self.sems:
            self.sems[name] = self.nc.alloc_semaphore(name)
            self.cnt[name] = 0
        return self.sems[name]

    def op(self, eng, fn, waits=(), inc=None, amt=1):
        v = None
        waits = [w for w in waits if w is not None and w[1] > 0]
        if eng in ("dv", "ac", "po"):
            self.sem(eng)
            if self.cnt[eng] > 0:
                waits.append((eng, self.cnt[eng]))
        if inc is not None:
            self.sem(inc)
            self.cnt[inc] += amt
            v = self.cnt[inc]
        for s, _ in waits:
            self.sem(s)
        self.items[eng].append((fn, tuple(waits), inc, amt))
        return v

    def dma(self, eng, out, in_, inc, waits=(), **kw):
        return self.op(eng, lambda e: e.dma_start(out, in_, **kw),
                       waits=waits, inc=inc, amt=16)

    def tpose(self, out, in_, inc, waits=()):
        return self.op("sy", lambda e: e.dma_start_transpose(out, in_),
                       waits=waits, inc=inc, amt=16)

    def emit(self):
        nc = self.nc
        lastw = {}

        def run(eng_obj, name):
            for fn, waits, inc, amt in self.items[name]:
                for s, v in waits:
                    k = (name, s)
                    if lastw.get(k, 0) < v:
                        eng_obj.wait_ge(self.sems[s], v)
                        lastw[k] = v
                ins = fn(eng_obj)
                if inc is not None:
                    ins.then_inc(self.sems[inc], amt)

        with nc.Block() as block:
            @block.tensor
            def _(e):
                run(e, "pe")

            @block.vector
            def _(e):
                run(e, "dv")

            @block.scalar
            def _(e):
                run(e, "ac")

            @block.gpsimd
            def _(e):
                run(e, "po")

            @block.sync
            def _(e):
                run(e, "sy")


def build_program():
    nc = bass.Bass("TRN2", target_bir_lowering=False)
    S = Sched(nc)

    d_x = nc.dram_tensor("x", [C, SEGW], F32, kind="ExternalInput")
    d_w = {n: nc.dram_tensor(n, [C, C], F32, kind="ExternalInput")
           for n in ("wrt", "wkt", "wvt", "wgt", "wot")}
    d_qw = nc.dram_tensor("qw", [C, C], F32, kind="ExternalInput")
    d_qwt = nc.dram_tensor("qwt", [NSEG, 128, NSEG * 128], F32,
                           kind="ExternalInput")
    d_throw = nc.dram_tensor("throw", [128, NSEG], F32)
    d_cols = {n: nc.dram_tensor(n, [128, NSEG], F32, kind="ExternalInput")
              for n in ("tmr", "tmk", "tmv", "tmg", "uu", "dec", "qb")}
    d_xhalo = nc.dram_tensor("xhalo", [128, NSEG], F32, kind="ExternalInput")
    d_iota = nc.dram_tensor("iota", [128, TL], F32, kind="ExternalInput")
    d_lnw = nc.dram_tensor("lnwbc", [128, C], F32, kind="ExternalInput")
    d_lnb = nc.dram_tensor("lnbbc", [128, C], F32, kind="ExternalInput")
    d_wkvp = nc.dram_tensor("wkvp", [NSEG * 128, KD], F32, kind="ExternalInput")
    d_mask = nc.dram_tensor("mask2", [128, 256], F32, kind="ExternalInput")
    d_bones = nc.dram_tensor("bones", [NSEG * 128, 16], F32, kind="ExternalInput")
    d_sel = nc.dram_tensor("sel", [128, 8], F32, kind="ExternalInput")
    d_hahb = nc.dram_tensor("hahb", [128, 4], F32, kind="ExternalInput")
    d_out = nc.dram_tensor("out", [C, TL], F32, kind="ExternalOutput")
    d_ccin = nc.dram_tensor("ccin", [NPG * 128, KD], F32)
    d_ccout = nc.dram_tensor("ccout", [8 * NPG * 128, KD], F32)
    dbg = {}
    for nm in DEBUG:
        shp = {"xn": [128, NSEG * SEGW], "y": [128, NCH * NG * CQ],
               "yng": [128, NCH * NG * CQ], "gffn": [128, NSEG * TL],
               "yngT": [128, NSEG * TL], "spre": [128, NG * NPG * NCH * KD],
               "bb": [16, NG * TL], "r": [128, NG * TL],
               "kw": [128, NG * TL], "vtm": [128, NCH * CQ],
               "vtm2": [128, NCH * CQ], "vtm3": [128, NCH * CQ],
               "esb": [128, 3 * NSEG],
               "bonus0": [128, CQ], "attm0": [128, 256],
               "rw0": [128, NPG * G]}[nm]
        dbg[nm] = nc.dram_tensor("dbg_" + nm, shp, F32, kind="ExternalOutput")

    t_ones128 = nc.inline_tensor(np.ones((128, 1), dtype=bfd), name="i_ones128")
    t_onesk1 = nc.inline_tensor(np.ones((1, 128), dtype=bfd), name="i_onesk1")
    t_onesk1f = nc.inline_tensor(np.ones((1, 128), dtype=np.float32),
                                 name="i_onesk1f")

    def A(name, shape, dt):
        return nc.alloc_sbuf_tensor(name, shape, dt, side="left")

    iota = A("iota_sb", [128, TL], F32)
    cols = {n: A(n + "_sb", [128, NSEG], F32) for n in d_cols}
    tm1m = {n: A(n + "_1m", [128, NSEG], F32) for n in ("tmr", "tmk", "tmv", "tmg")}
    e_sb = A("e_sb", [128, NSEG], F32)
    ebias = A("ebias_sb", [128, NSEG], F32)
    eneg = A("eneg_sb", [128, NSEG], F32)
    lnw = A("lnw_sb", [128, C], BF16)
    lnb = A("lnb_sb", [128, C], BF16)
    wkvp = A("wkvp_sb", [128, NSEG * KD], BF16)
    mask2 = A("mask2_sb", [128, 256], BF16)
    lhsTu = A("lhsTu_sb", [128, NSEG * 16], BF16)
    sel = A("sel_sb", [128, 8], F32)
    hahb = A("hahb_sb", [128, 4], F32)
    xhalo = A("xhalo_sb", [128, NSEG], F32)
    o128 = A("o128_sb", [128, 1], BF16)
    ok1 = A("ok1_sb", [1, 128], BF16)
    ok1f = A("ok1f_sb", [1, 128], F32)
    gsink = A("gsink", [128, 1], F32)
    yngT = A("yngT", [128, NSEG * TL], BF16)
    xn = A("xn", [128, NSEG * SEGW], BF16)

    PJ = [nc.alloc_psum_tensor(f"pj{i}", [128, TL], F32) for i in range(4)]
    ss_sl = [[PJ[3][0:64, 0:KD], PJ[3][64:128, 512:512 + KD]],
             [PJ[2][0:64, 0:KD], PJ[2][64:128, 512:512 + KD]]]
    str0 = PJ[2][0:1, 0:512]
    str1 = PJ[3][0:1, 0:512]

    def Rg(es, name, shape, dt):
        return es.enter_context(nc.sbuf_tensor(name, shape, dt, side="right"))


    x3 = d_x.ap().rearrange("(s p) t -> p s t", p=128)
    S.dma("sy", iota[:], d_iota.ap(), inc="ld0")
    for n in d_cols:
        S.dma("sy", cols[n][:], d_cols[n].ap(), inc="ld0")
    S.dma("sy", sel[:], d_sel.ap(), inc="ld0")
    S.dma("sy", xhalo[:], d_xhalo.ap(), inc="ld0")
    S.dma("sy", hahb[:], d_hahb.ap(), inc="ld0")
    v_cols = v_iota = S.cnt["ld0"]
    S.dma("po", lnw[:], d_lnw.ap(), inc="w0")
    S.dma("po", lnb[:], d_lnb.ap(), inc="w0")
    S.dma("po", mask2[:], d_mask.ap(), inc="w0")
    S.dma("po", wkvp[:].rearrange("p (s k) -> p s k", s=NSEG),
          d_wkvp.ap().rearrange("(s p) k -> p s k", p=128), inc="w0")
    S.dma("po", lhsTu[:].rearrange("p (s h) -> p s h", s=NSEG),
          d_bones.ap().rearrange("(s p) h -> p s h", p=128), inc="w0")
    S.dma("po", o128[:], t_ones128.ap(), inc="w0")
    S.dma("po", ok1[:], t_onesk1.ap(), inc="w0")
    S.dma("po", ok1f[:], t_onesk1f.ap(), inc="w0")
    v_wconst = S.cnt["w0"]

    v_e = S.op("ac", lambda e: e.activation(e_sb[:], cols["dec"][:], AF.Exp),
               waits=[("ld0", v_cols)], inc="ac")
    S.op("dv", lambda e: e.tensor_scalar_mul(ebias[:], e_sb[:], -2047.0),
         waits=[("ac", v_e)], inc="dv")
    S.op("dv", lambda e: e.tensor_scalar_mul(eneg[:], e_sb[:], -1.0),
         waits=[("ac", v_e)], inc="dv")
    for n in tm1m:
        S.op("dv", lambda e, n=n: e.tensor_scalar(
            tm1m[n][:], cols[n][:], -1.0, 1.0, ALU.mult, ALU.add),
            waits=[("ld0", v_cols)], inc="dv")
    v_tiny = None
    for s in range(NSEG):
        v_tiny = S.op("dv", lambda e, s=s: e.tensor_scalar(
            lhsTu[:, s * 16:(s + 1) * 16], lhsTu[:, s * 16:(s + 1) * 16],
            cols["uu"][:, s:s + 1], None, ALU.mult),
            waits=[("w0", v_wconst), ("ld0", v_cols)], inc="dv")

    es_ln = contextlib.ExitStack()
    xpc = [Rg(es_ln, f"xpc{i}", [128, 512], F32) for i in range(4)]
    xbc = [Rg(es_ln, f"xbc{i}", [128, 512], BF16) for i in range(2)]
    xqc = [Rg(es_ln, f"xqc{i}", [128, 512], BF16) for i in range(2)]
    mrow = Rg(es_ln, "mrow", [1, SEGW], F32)
    e2row = Rg(es_ln, "e2row", [1, SEGW], F32)
    varrow = Rg(es_ln, "varrow", [1, SEGW], F32)
    mbf = Rg(es_ln, "mbf", [1, SEGW], BF16)
    rbf = Rg(es_ln, "rbf", [1, SEGW], BF16)

    chunks = [(0, 512), (512, 512), (1024, 1)]
    v_cpd, v_sqd, v_stev = {}, {}, {}
    v_rowev = 0
    for ci, (off, cw) in enumerate(chunks):
        for s in range(NSEG):
            idx = ci * NSEG + s
            pb, b2 = idx % 4, idx % 2
            wd = []
            if idx - 4 in v_cpd:
                wd += [("dv", v_cpd[idx - 4]), ("ac", v_sqd[idx - 4])]
            v_pc = S.dma("sy", xpc[pb][:, 0:cw], x3[:, s, off:off + cw],
                         inc=f"xp{pb}", waits=wd,
                         allow_slow_non_contiguous=(cw == 1))
            wst = [("pe", v_stev[idx - 2])] if idx - 2 in v_stev else []
            v_cpd[idx] = S.op("dv", lambda e, pb=pb, b2=b2, cw=cw: e.tensor_copy(
                xbc[b2][:, 0:cw], xpc[pb][:, 0:cw]),
                waits=[(f"xp{pb}", v_pc)] + wst, inc="dv")
            v_sqd[idx] = S.op("ac", lambda e, pb=pb, b2=b2, cw=cw: e.activation(
                xqc[b2][:, 0:cw], xpc[pb][:, 0:cw], AF.Square),
                waits=[(f"xp{pb}", v_pc)] + wst, inc="ac")

            def stat2(e, b2=b2, cw=cw, s=s):
                e.matmul(str0[:, 0:cw], o128[:], xbc[b2][:, 0:cw],
                         start=(s == 0), stop=(s == NSEG - 1))
                return e.matmul(str1[:, 0:cw], o128[:], xqc[b2][:, 0:cw],
                                start=(s == 0), stop=(s == NSEG - 1))
            w = [("dv", v_cpd[idx]), ("ac", v_sqd[idx]), ("w0", v_wconst)]
            if ci > 0 and s == 0:
                w.append(("dv", v_rowev))
            v_stev[idx] = S.op("pe", stat2, waits=w, inc="pe")
        v_pe = v_stev[ci * NSEG + NSEG - 1]
        S.op("dv", lambda e, off=off, cw=cw: e.tensor_scalar_mul(
            mrow[0:1, off:off + cw], str0[:, 0:cw], 1.0 / C),
            waits=[("pe", v_pe)], inc="dv")
        v_rowev = S.op("dv", lambda e, off=off, cw=cw: e.tensor_scalar_mul(
            e2row[0:1, off:off + cw], str1[:, 0:cw], 1.0 / C),
            waits=[("pe", v_pe)], inc="dv")

    S.op("dv", lambda e: e.tensor_mul(varrow[:], mrow[:], mrow[:]),
         waits=[("dv", v_rowev)], inc="dv")
    v_var = S.op("dv", lambda e: e.tensor_sub(varrow[:], e2row[:], varrow[:]),
                 inc="dv")
    v_std = S.op("ac", lambda e: e.activation(varrow[:], varrow[:], AF.Sqrt,
                                              bias=hahb[0:1, 2:3]),
                 waits=[("dv", v_var), ("ld0", v_cols)], inc="ac")
    v_rec = S.op("dv", lambda e: e.reciprocal(varrow[:], varrow[:]),
                 waits=[("ac", v_std)], inc="dv")
    S.op("dv", lambda e: e.tensor_copy(mbf[:], mrow[:]), inc="dv")
    v_rows = S.op("dv", lambda e: e.tensor_copy(rbf[:], varrow[:]),
                  waits=[("dv", v_rec)], inc="dv")

    v_apl = {}
    for ci, (off, cw) in enumerate(chunks):
        def bcast(e, off=off, cw=cw):
            e.matmul(PJ[0][:, 0:cw], ok1[:], mbf[0:1, off:off + cw],
                     start=True, stop=True)
            return e.matmul(PJ[1][:, 0:cw], ok1[:], rbf[0:1, off:off + cw],
                            start=True, stop=True)
        w = [("dv", v_rows)]
        if ci > 0:
            w.append(("dv", v_apl[(ci - 1) * NSEG + NSEG - 1]))
        v_bc = S.op("pe", bcast, waits=w, inc="pe")
        for s in range(NSEG):
            idx = ci * NSEG + s
            pb, b2 = idx % 4, idx % 2
            if idx - 4 in v_apl:
                wd = [("dv", v_apl[idx - 4])]
            else:
                wd = [("dv", v_cpd[44 + pb]), ("ac", v_sqd[44 + pb])]
            v_pc = S.dma("sy", xpc[pb][:, 0:cw], x3[:, s, off:off + cw],
                         inc=f"xp{pb}", waits=wd,
                         allow_slow_non_contiguous=(cw == 1))
            S.op("dv", lambda e, pb=pb, b2=b2, cw=cw: e.tensor_tensor(
                xbc[b2][:, 0:cw], xpc[pb][:, 0:cw], PJ[0][:, 0:cw], ALU.subtract),
                waits=[(f"xp{pb}", v_pc), ("pe", v_bc)], inc="dv")
            v_apl[idx] = S.op("dv", lambda e, s=s, b2=b2, off=off, cw=cw:
                              e.tensor_tensor(
                xn[:, s * SEGW + off: s * SEGW + off + cw], xbc[b2][:, 0:cw],
                PJ[1][:, 0:cw], ALU.mult), inc="dv")
    v_xnraw = v_apl[2 * NSEG + NSEG - 1]
    hv = xn[:].rearrange("p (s t) -> p s t", s=NSEG)[:, :, 0]
    S.op("dv", lambda e: e.tensor_scalar(hv, hv, hahb[:, 0:1], None, ALU.mult),
         waits=[("dv", v_xnraw), ("ld0", v_cols)], inc="dv")
    v_xn = S.op("dv", lambda e: e.scalar_tensor_tensor(
        hv, xhalo[:], hahb[:, 1:2], hv, ALU.mult, ALU.add), inc="dv")
    if TRUNC <= 1:
        S.emit()
        return nc
    es_ln.close()

    es_g = contextlib.ExitStack()
    wslice = [Rg(es_g, f"wslice{i}", [128, 512], BF16) for i in range(4)]
    tmpb = [Rg(es_g, f"tmpb{i}", [128, TL], BF16) for i in range(4)]
    r_pl = Rg(es_g, "r_pl", [128, NPG * TL], BF16)
    kw_pl = Rg(es_g, "kw_pl", [128, NPG * TL], BF16)
    vtm_pl = Rg(es_g, "vtm_pl", [128, NCH * CQ], BF16)
    gtm_pl = Rg(es_g, "gtm_pl", [128, NCH * CQ], BF16)
    ruk_pl = Rg(es_g, "ruk_pl", [128, NPG * TL], BF16)
    gwfor = Rg(es_g, "gwfor", [128, NPG * TL], BF16)
    gwback = Rg(es_g, "gwback", [128, NPG * TL], BF16)
    rwbuf = Rg(es_g, "rwbuf", [128, NPG * G], BF16)
    spre = Rg(es_g, "spre", [128, NPG * NCH * KD], BF16)
    sfin = Rg(es_g, "sfin", [128, NPG * KD], F32)
    shalf = Rg(es_g, "shalf", [128, NPG * KD], F32)
    recv = Rg(es_g, "recv", [128, 8 * NPG * KD], F32)
    bb_sb = Rg(es_g, "bb_sb", [16, TL], BF16)
    kwT = [Rg(es_g, f"kwT{i}", [128, 128], BF16) for i in range(4)]
    attm = [Rg(es_g, f"attm{i}", [128, 256], BF16) for i in range(2)]
    bbT = [Rg(es_g, f"bbT{i}", [128, 16], BF16) for i in range(2)]
    bonus = [Rg(es_g, f"bonus{i}", [128, CQ], BF16) for i in range(2)]
    ybuf = [Rg(es_g, f"ybuf{i}", [128, CQ], BF16) for i in range(2)]
    y2buf = Rg(es_g, "y2buf", [128, CQ], BF16)
    tgn = [Rg(es_g, f"tgn{i}", [128, CQ], BF16) for i in range(2)]
    gn8 = [Rg(es_g, f"gn8_{i}", [128, HPG], F32) for i in range(5)]
    dbgb = Rg(es_g, "dbgb", [128, TL], F32) if DEBUG else None

    def dump_plane(nm, src, width, dst_off=0, waits=()):
        if nm not in dbg:
            return
        np_ = src.shape[0]
        nchunk = (width + TL - 1) // TL
        for i in range(nchunk):
            lo = i * TL
            hi = min(width, lo + TL)
            vv = S.op("dv", lambda e, lo=lo, hi=hi, np_=np_: e.tensor_copy(
                dbgb[0:np_, 0:hi - lo], src[0:np_, lo:hi]),
                waits=list(waits) + [("sto", S.cnt.get("sto", 0))], inc="dv")
            S.dma("sy", dbg[nm].ap()[0:np_, dst_off + lo:dst_off + hi],
                  dbgb[0:np_, 0:hi - lo], inc="sto", waits=[("dv", vv)])

    dump_plane("xn", xn, NSEG * SEGW, waits=[("dv", v_xn)])

    pj_free = [("dv", v_apl[2 * NSEG + NSEG - 1]),
               ("dv", v_apl[2 * NSEG + NSEG - 1]),
               ("dv", v_rowev), ("dv", v_rowev)]

    def emit_proj(wt_dram, GG, nm, prev_pe_kt, evac, extra_kt0=(),
                  first_waits=()):
        v_pe_kt = {}
        v_bl = {}
        v_wd = {}
        for kt in range(NSEG):
            b = kt % 4
            if kt - 4 in v_pe_kt:
                gate = [("pe", v_pe_kt[kt - 4])]
            elif prev_pe_kt:
                gate = [("pe", prev_pe_kt[NSEG - 4 + kt])]
            else:
                gate = [("dv", v_xn)] + list(first_waits)
            v_wd[kt] = S.dma("po", wslice[b][:],
                             wt_dram.ap()[kt * 128:(kt + 1) * 128,
                                          GG * 512:(GG + 1) * 512],
                             inc=f"ws{b}", waits=gate)
            cur = xn[:, kt * SEGW + 1:(kt + 1) * SEGW]
            prev = xn[:, kt * SEGW:(kt + 1) * SEGW - 1]
            vd = S.op("dv", lambda e, b=b, cur=cur, prev=prev: e.tensor_tensor(
                tmpb[b][:], cur, prev, ALU.subtract), waits=gate, inc="dv")
            v_bl[kt] = ("dv", S.op(
                "dv", lambda e, b=b, prev=prev, kt=kt: e.scalar_tensor_tensor(
                    tmpb[b][:], tmpb[b][:], cols[nm][:, kt:kt + 1], prev,
                    ALU.mult, ALU.add),
                waits=[("dv", vd)], inc="dv"))

            def mms(e, b=b, kt=kt):
                ins = None
                for ob in range(NPG):
                    lhsT = wslice[b][:, ob * 128:(ob + 1) * 128]
                    for hf in range(2):
                        ins = e.matmul(PJ[ob][:, hf * 512:(hf + 1) * 512], lhsT,
                                       tmpb[b][:, hf * 512:(hf + 1) * 512],
                                       start=(kt == 0), stop=(kt == NSEG - 1))
                return ins
            w = [(f"ws{b}", v_wd[kt]), v_bl[kt]]
            if kt == 0:
                w += list(pj_free) + list(extra_kt0)
            v_pe_kt[kt] = S.op("pe", mms, waits=w, inc="pe")
        v_last = v_pe_kt[NSEG - 1]
        evs = []
        for i in range(NPG):
            evs.append(evac(i, PJ[i], [("pe", v_last)]))
            pj_free[i] = evs[i]
        return evs, v_last, v_pe_kt

    def emit_proj_tm(wt_dram, GG, nm, prev_pe_kt, evac_tt):
        v_pe_kt = {}
        v_bl = {}
        v_wd = {}
        for kt in range(NSEG):
            b = kt % 4
            if kt - 4 in v_pe_kt:
                gate = [("pe", v_pe_kt[kt - 4])]
            else:
                gate = [("pe", prev_pe_kt[NSEG - 4 + kt])]
            v_wd[kt] = S.dma("po", wslice[b][:],
                             wt_dram.ap()[kt * 128:(kt + 1) * 128,
                                          GG * 512:(GG + 1) * 512],
                             inc=f"ws{b}", waits=gate)
            cur = xn[:, kt * SEGW + 1:(kt + 1) * SEGW]
            prev = xn[:, kt * SEGW:(kt + 1) * SEGW - 1]
            vd = S.op("dv", lambda e, b=b, cur=cur, prev=prev: e.tensor_tensor(
                tmpb[b][:], cur, prev, ALU.subtract), waits=gate, inc="dv")
            v_bl[kt] = ("dv", S.op(
                "dv", lambda e, b=b, prev=prev, kt=kt: e.scalar_tensor_tensor(
                    tmpb[b][:], tmpb[b][:], cols[nm][:, kt:kt + 1], prev,
                    ALU.mult, ALU.add),
                waits=[("dv", vd)], inc="dv"))

            def mms(e, b=b, kt=kt):
                ins = None
                for tt in range(NCH):
                    lhsT = tmpb[b][:, tt * 128:(tt + 1) * 128]
                    ins = e.matmul(
                        PJ[tt // 2][:, (tt % 2) * 512:(tt % 2) * 512 + 512],
                        lhsT, wslice[b][:, 0:512],
                        start=(kt == 0), stop=(kt == NSEG - 1))
                return ins
            w = [(f"ws{b}", v_wd[kt]), v_bl[kt]]
            if kt == 0:
                w += list(pj_free)
            v_pe_kt[kt] = S.op("pe", mms, waits=w, inc="pe")
        v_last = v_pe_kt[NSEG - 1]
        evs = []
        for tt in range(NCH):
            evs.append(evac_tt(
                tt, PJ[tt // 2][:, (tt % 2) * 512:(tt % 2) * 512 + 512],
                [("pe", v_last)]))
        for j in range(4):
            pj_free[j] = evs[2 * j + 1]
        return evs, v_last, v_pe_kt

    v_allmm_prev = 0
    v_evac_prev = None
    v_sfin_prev = 0
    v_selsum_prev = 0
    v_y5T_prev = 0
    v_red_prev = 0
    v_gnpool_prev = 0
    v_bt_prev = [0, 0]
    v_ci_prev = 0
    v_kwlast = 0
    v_rwlast = 0
    kt_prev = None
    v_evac = {}

    for GG in range(NG):
        v_wf = {}
        v_wbk = {}
        for i in range(NPG):
            s = GG * NPG + i
            wv = [("xin", 0), ("dv", v_tiny), ("pe", v_allmm_prev)]
            v_wbk[i] = S.op("ac", lambda e, i=i, s=s: e.activation(
                gwback[:, i * TL:(i + 1) * TL], iota[:], AF.Exp,
                bias=ebias[:, s:s + 1], scale=e_sb[:, s:s + 1]),
                waits=[("ld0", v_iota), ("dv", v_tiny),
                       ("dv", v_kwlast), ("dv", v_rwlast)], inc="ac")
            v_wf[i] = S.op("ac", lambda e, i=i, s=s: e.activation(
                gwfor[:, i * TL:(i + 1) * TL], iota[:], AF.Exp,
                scale=eneg[:, s:s + 1]), inc="ac")

        def evr(i, pj, w):
            eng = "ac" if i < 2 else "dv"
            if eng == "ac":
                v = S.op("ac", lambda e, i=i, pj=pj: e.activation(
                    r_pl[:, i * TL:(i + 1) * TL], pj[:], AF.Copy),
                    waits=w, inc="ac")
            else:
                v = S.op("dv", lambda e, i=i, pj=pj: e.tensor_copy(
                    r_pl[:, i * TL:(i + 1) * TL], pj[:]), waits=w, inc="dv")
            return (eng, v)
        evs, v_pe_r, kt_r = emit_proj(
            d_w["wrt"], GG, "tmr", kt_prev, evr,
            extra_kt0=[v_evac_prev] if v_evac_prev else [],
            first_waits=[("dv", v_xn)] if GG == 0 else
            [("pe", v_allmm_prev)])
        v_r_ac = max([v for d, v in evs if d == "ac"], default=0)
        v_r_dv = max([v for d, v in evs if d == "dv"], default=0)
        v_r = [("ac", v_r_ac), ("dv", v_r_dv)]
        if TRUNC <= 2:
            S.emit()
            return nc

        def evk(i, pj, w):
            eng = "ac" if i < 2 else "dv"
            if eng == "ac":
                v = S.op("ac", lambda e, i=i, pj=pj: e.activation(
                    ruk_pl[:, i * TL:(i + 1) * TL], pj[:], AF.Copy),
                    waits=w, inc="ac")
            else:
                v = S.op("dv", lambda e, i=i, pj=pj: e.tensor_copy(
                    ruk_pl[:, i * TL:(i + 1) * TL], pj[:]), waits=w, inc="dv")
            return (eng, v)
        evs, v_pe_k, kt_k = emit_proj(d_w["wkt"], GG, "tmk", kt_r, evk)
        v_kc_ac = max([v for d, v in evs if d == "ac"], default=0)
        v_kc_dv = max([v for d, v in evs if d == "dv"], default=0)

        v_kwd = {}
        for i in range(NPG):
            v_kwd[i] = S.op("dv", lambda e, i=i: e.tensor_tensor(
                kw_pl[:, i * TL:(i + 1) * TL], ruk_pl[:, i * TL:(i + 1) * TL],
                gwback[:, i * TL:(i + 1) * TL], ALU.mult),
                waits=[("ac", v_wbk[i]), ("ac", v_kc_ac), ("dv", v_kc_dv)],
                inc="dv")
        v_kwlast = v_kwd[NPG - 1]
        v_ruk = {}
        for i in range(NPG):
            v_ruk[i] = S.op("dv", lambda e, i=i: e.tensor_tensor(
                ruk_pl[:, i * TL:(i + 1) * TL], ruk_pl[:, i * TL:(i + 1) * TL],
                r_pl[:, i * TL:(i + 1) * TL], ALU.mult),
                waits=[("dv", v_kwd[i])] + v_r, inc="dv")
        v_k = S.cnt["dv"]

        def bbmm(e, GG=GG):
            ins = None
            for hf in range(2):
                for li in range(NPG):
                    ins = e.matmul(PJ[1][0:16, hf * 512:(hf + 1) * 512],
                                   lhsTu[:, (GG * NPG + li) * 16:
                                         (GG * NPG + li + 1) * 16],
                                   ruk_pl[:, li * TL + hf * 512:
                                          li * TL + hf * 512 + 512],
                                   start=(li == 0), stop=(li == NPG - 1))
            return ins
        v_bbpe = S.op("pe", bbmm,
                      waits=[("dv", v_ruk[NPG - 1])] + [pj_free[1]], inc="pe")
        v_bb = S.op("dv", lambda e: e.tensor_copy(bb_sb[:], PJ[1][0:16, 0:TL]),
                    waits=[("pe", v_bbpe), ("bt0", v_bt_prev[0]),
                           ("bt1", v_bt_prev[1])], inc="dv")
        if TRUNC <= 3:
            S.emit()
            return nc
        pj_free[1] = ("dv", v_bb)
        dump_plane("bb", bb_sb, TL, dst_off=GG * TL, waits=[("dv", v_bb)])

        def evv(tt, pjs, w):
            return ("dv", S.op("dv", lambda e, tt=tt, pjs=pjs: e.tensor_copy(
                vtm_pl[:, tt * CQ:(tt + 1) * CQ], pjs), waits=w, inc="dv"))
        evs, v_pe_v, kt_v = emit_proj_tm(d_w["wvt"], GG, "tmv", kt_k, evv)
        v_v = evs[NCH - 1]
        v_vtm2 = [v_v]

        z = spre[:].rearrange("p (l n k) -> p l n k", l=NPG, n=NCH)[:, :, 0, :]
        v_z = S.op("dv", lambda e: e.memset(z, 0.0),
                   waits=[("pe", v_allmm_prev)], inc="dv")
        v_zf = S.op("dv", lambda e: e.memset(sfin[:], 0.0),
                    waits=[("cci", v_ci_prev)], inc="dv")
        v_supd, v_ssev = {}, {}
        v_sfin = None
        for li in range(NPG):
            ss = ss_sl[li % 2]
            sfs = sfin[:, li * KD:(li + 1) * KD]
            for n in range(NCH):
                j = li * NCH + n
                kb = j % 4
                wt = [("dv", v_kwd[li])] + v_vtm2
                if j - 3 in v_supd:
                    wt.append(("pe", v_supd[j - 3]))
                v_t = S.tpose(kwT[kb][:], kw_pl[:, li * TL + n * 128:
                                                li * TL + n * 128 + 128],
                              inc=f"kwt{kb}", waits=wt)

                def supd(e, li=li, n=n, kb=kb, ss=ss):
                    ins = None
                    for hh in range(2):
                        ins = e.matmul(ss[hh],
                                       kwT[kb][:, hh * 64:(hh + 1) * 64],
                                       vtm_pl[:, n * CQ + li * 128 + hh * 64:
                                              n * CQ + li * 128 + hh * 64 + 64],
                                       start=True, stop=True,
                                       tile_position=(0, hh * 64))
                    return ins
                w = [(f"kwt{kb}", v_t)]
                if (li, n - 1) in v_ssev:
                    w.append(("dv", v_ssev[(li, n - 1)]))
                if n == 0:
                    w.append(("dv", v_zf))
                    if (li - 2, NCH - 1) in v_ssev:
                        w.append(("dv", v_ssev[(li - 2, NCH - 1)]))
                    elif v_sfin_prev:
                        w.append(("dv", v_sfin_prev))
                    if li < 2:
                        w.append(pj_free[3])
                        w.append(pj_free[2])
                v_supd[j] = S.op("pe", supd, waits=w, inc="pe")
                S.op("dv", lambda e, ss=ss, sfs=sfs: e.tensor_add(
                    sfs[0:64, :], sfs[0:64, :], ss[0]),
                    waits=[("pe", v_supd[j])], inc="dv")
                vadd = S.op("dv", lambda e, ss=ss, sfs=sfs: e.tensor_add(
                    sfs[64:128, :], sfs[64:128, :], ss[1]), inc="dv")
                if n < NCH - 1:
                    v_ssev[(li, n)] = S.op(
                        "dv", lambda e, li=li, n=n, sfs=sfs: e.tensor_copy(
                            spre[:, (li * NCH + n + 1) * KD:
                                 (li * NCH + n + 2) * KD], sfs),
                        inc="dv")
                else:
                    v_ssev[(li, n)] = vadd
                    v_sfin = vadd
        v_sfin_prev = v_sfin
        pj_free[3] = ("dv", v_sfin)
        pj_free[2] = ("dv", v_sfin)
        if TRUNC <= 4:
            S.emit()
            return nc

        v_ci = S.dma("po", d_ccin.ap().rearrange("(l p) k -> p l k", p=128),
                     sfin[:].rearrange("p (l k) -> p l k", l=NPG),
                     inc="cci", waits=[("dv", v_sfin)])
        v_ci_prev = v_ci
        v_cc = S.op("po", lambda e: e.collective_compute(
            "AllGather", ALU.bypass, replica_groups=[list(range(8))],
            ins=[d_ccin.ap().opt()], outs=[d_ccout.ap().opt()]),
            waits=[("cci", v_ci)], inc="cc")
        v_rc = S.dma("po", recv[:].rearrange("p (q l k) -> p q l k", q=8, l=NPG),
                     d_ccout.ap().rearrange("(q l p) k -> p q l k", p=128, l=NPG),
                     inc="ccr", waits=[("cc", v_cc), ("dv", v_selsum_prev)])
        S.op("dv", lambda e: e.tensor_scalar(
            shalf[:], recv[:, 0:NPG * KD], sel[:, 0:1], None, ALU.mult),
            waits=[("ccr", v_rc), ("ld0", v_cols)], inc="dv")
        for rr in range(1, 8):
            v_selsum_prev = S.op("dv", lambda e, rr=rr: e.scalar_tensor_tensor(
                shalf[:], recv[:, rr * NPG * KD:(rr + 1) * NPG * KD],
                sel[:, rr:rr + 1], shalf[:], ALU.mult, ALU.add), inc="dv")
        spv = spre[:].rearrange("p (l n k) -> p l n k", l=NPG, n=NCH)
        shv = shalf[:].rearrange("p (l k) -> p l k", l=NPG).unsqueeze(2) \
            .broadcast_to([128, NPG, NCH, KD])
        v_spre = S.op("dv", lambda e: e.tensor_tensor(spv, spv, shv, ALU.add),
                      waits=[("dv", v_z), ("dv", v_sfin)], inc="dv")
        dump_plane("spre", spre, NPG * NCH * KD, dst_off=GG * NPG * NCH * KD,
                   waits=[("dv", v_spre)])
        if GG == 0:
            dump_plane("vtm3", vtm_pl, NCH * CQ, waits=[("dv", v_spre)])
            dump_plane("esb", e_sb, NSEG, dst_off=0, waits=[("dv", v_tiny)])
            dump_plane("esb", eneg, NSEG, dst_off=NSEG, waits=[("dv", v_tiny)])
            dump_plane("esb", ebias, NSEG, dst_off=2 * NSEG,
                       waits=[("dv", v_tiny)])
        if TRUNC <= 5:
            S.emit()
            return nc

        def evg(tt, pjs, w):
            va = S.op("ac", lambda e, tt=tt, pjs=pjs: e.activation(
                gtm_pl[:, tt * CQ:(tt + 1) * CQ], pjs, AF.Silu),
                waits=w, inc="ac")
            return ("dv", S.op("dv", lambda e: e.tensor_copy(
                gsink[:], e_sb[:, 0:1]), waits=[("ac", va)], inc="dv"))
        evs, v_pe_g, kt_g = emit_proj_tm(d_w["wgt"], GG, "tmg", kt_v, evg)
        kt_prev = kt_g
        v_gtm2 = [evs[NCH - 1]]
        if TRUNC <= 51:
            S.emit()
            return nc

        dump_plane("r", r_pl, TL, dst_off=GG * TL, waits=v_r)
        dump_plane("kw", kw_pl, TL, dst_off=GG * TL,
                   waits=[("dv", v_kwd[NPG - 1])])

        pend_gn = []
        v_mask = {}
        v_evac = {}
        v_pairev = {}
        v_y2c = {}
        v_t1c = {}
        for n in range(NCH):
            pj = PJ[n % 2]
            v_rwp = {}
            for li in range(NPG):
                s = GG * NPG + li
                va = S.op("ac", lambda e, li=li, s=s, n=n: e.activation(
                    rwbuf[:, li * G:(li + 1) * G], iota[:, n * G:(n + 1) * G],
                    AF.Exp, scale=eneg[:, s:s + 1]),
                    waits=[("dv", v_tiny), ("ld0", v_iota),
                           ("pe", v_allmm_prev)], inc="ac")
                v_rwp[li] = S.op("dv", lambda e, li=li, n=n: e.tensor_tensor(
                    rwbuf[:, li * G:(li + 1) * G],
                    r_pl[:, li * TL + n * G: li * TL + (n + 1) * G],
                    rwbuf[:, li * G:(li + 1) * G], ALU.mult),
                    waits=[("ac", va)] + v_r, inc="dv")
            nb = n % 2
            hoff = (GG % 2) * 8
            wbt = [("dv", v_bb)]
            if n - 2 in v_evac:
                wbt.append(("dv", v_evac[n - 2]))
            v_bt = S.tpose(bbT[nb][:], bb_sb[0:16, n * 128:(n + 1) * 128],
                           inc=f"bt{nb}", waits=wbt)
            v_bt_prev[nb] = v_bt
            v_bo = S.op("dv", lambda e, n=n, nb=nb, hoff=hoff: e.tensor_tensor(
                bonus[nb][:].rearrange("p (h k) -> p h k", h=HPG),
                vtm_pl[:, n * CQ:(n + 1) * CQ].rearrange(
                    "p (h k) -> p h k", h=HPG),
                bbT[nb][:, hoff:hoff + HPG].unsqueeze(2)
                .broadcast_to([128, HPG, KD]), ALU.mult),
                waits=[(f"bt{nb}", v_bt)] + v_vtm2, inc="dv")
            if TRUNC <= 522 and n == 0:
                S.emit()
                return nc
            for li in range(NPG):
                j = n * NPG + li
                ab = j % 2
                att = PJ[2 + ab]

                def attmm(e, li=li, n=n, att=att):
                    ins = None
                    for hh in range(2):
                        ins = e.matmul(att[:, hh * 512: hh * 512 + 128],
                                       kw_pl[hh * 64:(hh + 1) * 64,
                                             li * TL + n * G:
                                             li * TL + (n + 1) * G],
                                       r_pl[hh * 64:(hh + 1) * 64,
                                            li * TL + n * G:
                                            li * TL + (n + 1) * G],
                                       start=True, stop=True,
                                       tile_position=(hh * 64, 0))
                    return ins
                w = [("dv", v_kwd[li])] + v_r
                if j - 2 in v_mask:
                    w.append(("dv", v_mask[j - 2]))
                elif n == 0 and li < 2:
                    w.append(pj_free[2 + ab])
                v_apc = S.op("pe", attmm, waits=w, inc="pe")
                mb = li % 2
                attv2 = att[:].rearrange("p (b c) -> p b c", b=2)[:, :, 0:128]
                v_mask[j] = S.op("dv", lambda e, mb=mb, attv2=attv2:
                                 e.tensor_tensor(
                    attm[mb][:].rearrange("p (b c) -> p b c", b=2), attv2,
                    mask2[:].rearrange("p (b c) -> p b c", b=2), ALU.mult),
                    waits=[("pe", v_apc), ("w0", v_wconst)], inc="dv")

                if TRUNC <= 523 and n == 0 and li == 1:
                    S.emit()
                    return nc

                def wkvmm(e, li=li, n=n, mb=mb, GG=GG):
                    sl = PJ[(n * NPG + li) % 2]
                    ins = None
                    for hh in range(2):
                        e.matmul(sl[:, hh * 512: hh * 512 + 64],
                                 attm[mb][:, hh * 128:(hh + 1) * 128],
                                 vtm_pl[:, n * CQ + li * 128 + hh * 64:
                                        n * CQ + li * 128 + hh * 64 + 64],
                                 start=True, stop=False)
                        e.matmul(sl[:, hh * 512: hh * 512 + 64],
                                 r_pl[hh * 64:(hh + 1) * 64,
                                      li * TL + n * G: li * TL + (n + 1) * G],
                                 spre[hh * 64:(hh + 1) * 64,
                                      (li * NCH + n) * KD:
                                      (li * NCH + n + 1) * KD],
                                 start=False, stop=False,
                                 tile_position=(hh * 64, 0))
                        ins = e.matmul(sl[:, hh * 512: hh * 512 + 64],
                                       rwbuf[hh * 64:(hh + 1) * 64,
                                             li * G:(li + 1) * G],
                                       wkvp[hh * 64:(hh + 1) * 64,
                                            (GG * NPG + li) * KD:
                                            (GG * NPG + li + 1) * KD],
                                       start=False, stop=True,
                                       tile_position=(hh * 64, 0))
                    return ins
                w = [("dv", v_mask[j]), ("dv", v_rwp[li]), ("dv", v_spre),
                     ("w0", v_wconst)] + v_vtm2
                js = n * NPG + li
                if js - 2 in v_pairev:
                    w.append(("dv", v_pairev[js - 2]))
                elif js < 2:
                    w.append(pj_free[0])
                v_wk = S.op("pe", wkvmm, waits=w, inc="pe")
                if TRUNC <= 524 and n == 0 and li == 0:
                    S.emit()
                    return nc
                wev = []
                if li == 0 and n - 2 in v_y2c:
                    wev = [("ac", v_y2c[n - 2]), ("po", v_t1c[n - 2])]
                slv = PJ[js % 2][:].rearrange("p (b c) -> p b c", b=2)[:, :, 0:64]
                v_pairev[js] = S.op(
                    "dv", lambda e, li=li, nb=n % 2, slv=slv: e.tensor_tensor(
                        ybuf[nb][:, li * 128:(li + 1) * 128].rearrange(
                            "p (b c) -> p b c", b=2),
                        slv,
                        bonus[nb][:, li * 128:(li + 1) * 128].rearrange(
                            "p (b c) -> p b c", b=2), ALU.add),
                    waits=[("pe", v_wk), ("dv", v_bo)] + wev, inc="dv")
            v_allmm = S.cnt["pe"]
            v_allmm_prev = v_allmm
            v_evac[n] = v_pairev[n * NPG + NPG - 1]
            if TRUNC <= 52 and n == 0:
                S.emit()
                return nc
            dump_plane("y", ybuf[nb], CQ, dst_off=(GG * NCH + n) * CQ,
                       waits=[("dv", v_evac[n])])

            def emit_gn(n, nb, GG=GG):
                nonlocal v_red_prev, v_gnpool_prev, v_y5T_prev
                v_y2 = S.op("ac", lambda e, nb=nb: e.activation(
                    y2buf[:], ybuf[nb][:], AF.Square),
                    waits=[("dv", v_evac[n]), ("dv", v_red_prev)], inc="ac")
                v_y2c[n] = v_y2
                yv = ybuf[nb][:].rearrange("p (h k) -> p h k", h=HPG)
                S.op("dv", lambda e, yv=yv: e.tensor_reduce(
                    gn8[0][:], yv, AX.X, ALU.add),
                    waits=[("dv", v_evac[n]), ("po", v_gnpool_prev)], inc="dv")
                S.op("dv", lambda e: e.tensor_reduce(
                    gn8[1][:], y2buf[:].rearrange("p (h k) -> p h k", h=HPG),
                    AX.X, ALU.add), waits=[("ac", v_y2)], inc="dv")
                S.op("dv", lambda e: e.tensor_scalar_mul(
                    gn8[0][:], gn8[0][:], 1.0 / KD), inc="dv")
                S.op("dv", lambda e: e.tensor_scalar_mul(
                    gn8[1][:], gn8[1][:], 1.0 / KD), inc="dv")
                S.op("dv", lambda e: e.tensor_mul(
                    gn8[2][:], gn8[0][:], gn8[0][:]), inc="dv")
                v_gv = S.op("dv", lambda e: e.tensor_sub(
                    gn8[2][:], gn8[1][:], gn8[2][:]), inc="dv")
                v_gs = S.op("ac", lambda e: e.activation(
                    gn8[3][:], gn8[2][:], AF.Sqrt, bias=hahb[:, 3:4]),
                    waits=[("dv", v_gv)], inc="ac")
                v_gr = S.op("dv", lambda e: e.reciprocal(gn8[4][:], gn8[3][:]),
                            waits=[("ac", v_gs)], inc="dv")
                v_red_prev = v_gr
                v_t1c[n] = 0
                muv = gn8[0][:].unsqueeze(2).broadcast_to([128, HPG, KD])
                rsv = gn8[4][:].unsqueeze(2).broadcast_to([128, HPG, KD])
                v_t1 = S.op("po", lambda e, yv=yv, muv=muv: e.tensor_tensor(
                    tgn[0][:].rearrange("p (h k) -> p h k", h=HPG), yv, muv,
                    ALU.subtract),
                    waits=[("dv", v_gr), ("yngt", v_y5T_prev)], inc="po")
                v_t1c[n] = v_t1
                v_t2 = S.op("po", lambda e, rsv=rsv: e.tensor_tensor(
                    tgn[1][:].rearrange("p (h k) -> p h k", h=HPG),
                    tgn[0][:].rearrange("p (h k) -> p h k", h=HPG), rsv,
                    ALU.mult), inc="po")
                lo = GG * CQ
                v_t3 = S.op("po", lambda e, lo=lo: e.tensor_tensor(
                    tgn[0][:], tgn[1][:], lnw[:, lo:lo + CQ], ALU.mult),
                    waits=[("w0", v_wconst)], inc="po")
                v_gnpool_prev = v_t3
                v_t4 = S.op("dv", lambda e, lo=lo: e.tensor_tensor(
                    tgn[1][:], tgn[0][:], lnb[:, lo:lo + CQ], ALU.add),
                    waits=[("po", v_t3)], inc="dv")
                v_y5 = S.op("dv", lambda e, n=n: e.tensor_tensor(
                    tgn[0][:], tgn[1][:], gtm_pl[:, n * CQ:(n + 1) * CQ],
                    ALU.mult), waits=v_gtm2 + [("dv", v_t4)], inc="dv")
                for li in range(NPG):
                    v_y5T_prev = S.tpose(
                        yngT[:, (GG * NPG + li) * TL + n * 128:
                             (GG * NPG + li) * TL + n * 128 + 128],
                        tgn[0][:, li * 128:(li + 1) * 128],
                        inc="yngt", waits=[("dv", v_y5)])
                dump_plane("yng", tgn[0], CQ, dst_off=(GG * NCH + n) * CQ,
                           waits=[("dv", v_y5)])
            pend_gn.append((n, nb))
            if n > 0:
                emit_gn(*pend_gn.pop(0))
        while pend_gn:
            emit_gn(*pend_gn.pop(0))
        v_evac_prev = ("dv", v_evac[NCH - 1])
        last = NCH * NPG - 1
        pj_free[0] = ("dv", v_pairev[last - 1])
        pj_free[1] = ("dv", v_pairev[last])
        pj_free[2] = ("dv", v_mask[last - 1])
        pj_free[3] = ("dv", v_mask[last])
    v_yngT = S.cnt["yngt"]
    dump_plane("yngT", yngT, NSEG * TL, waits=[("yngt", v_yngT)])

    v_dbg_done = S.cnt["dv"]
    era_gate = [("pe", S.cnt["pe"]), ("yngt", S.cnt["yngt"]),
                ("po", S.cnt["po"]), ("ac", S.cnt["ac"]),
                ("dv", v_dbg_done)]
    es_g.close()

    es_f = contextlib.ExitStack()
    gffn = Rg(es_f, "gffn", [128, NSEG * TL], BF16)
    wband = [Rg(es_f, f"wband{i}", [128, NSEG * 128], BF16) for i in range(2)]
    qs = [Rg(es_f, f"qs{i}", [128, C], F32) for i in range(2)]
    qTs = [Rg(es_f, f"qTs{i}", [128, NSEG * 128], BF16) for i in range(2)]
    qmask = Rg(es_f, "qmask", [128, C], BF16)
    th_sb = Rg(es_f, "th_sb", [128, C], F32)
    qth = Rg(es_f, "qth", [128, NSEG], F32)
    thrw = Rg(es_f, "thrw", [1, C], F32)
    x2b = [Rg(es_f, f"x2b{i}", [128, TL], F32) for i in range(2)]
    outb = [Rg(es_f, f"outb{i}", [128, TL], F32) for i in range(2)]

    v_red = {}
    for ot in range(NSEG):
        b = ot % 2
        wq = ([("dv", v_red[ot - 2])] if ot - 2 in v_red else list(era_gate))
        v_qd = S.dma("sy", qs[b][:], d_qw.ap()[ot * 128:(ot + 1) * 128, :],
                     inc=f"q{b}", waits=wq)
        v_red[ot] = S.op("dv", lambda e, b=b, ot=ot: e.tensor_reduce(
            qth[:, ot:ot + 1], qs[b][:], AX.X, ALU.add,
            apply_absolute_value=True), waits=[(f"q{b}", v_qd)], inc="dv")
    v_thd = S.dma("po", d_throw.ap(), qth[:],
                  waits=[("dv", v_red[NSEG - 1])], inc="cci")
    v_thr = S.dma("sy", thrw[:].rearrange("a (t p) -> a t p", t=NSEG),
                  d_throw.ap().transpose([1, 0]).unsqueeze(0),
                  inc="x20", waits=[("cci", v_thd)],
                  allow_slow_non_contiguous=True)
    v_ths = S.op("dv", lambda e: e.tensor_scalar_mul(
        thrw[:], thrw[:], 0.7 / C), waits=[("x20", v_thr)], inc="dv")
    def thbc(e):
        ins = None
        for j in range(4):
            ins = e.matmul(PJ[2 + j // 2][:, (j % 2) * 512:(j % 2) * 512 + 512],
                           ok1f[:], thrw[0:1, j * 512:(j + 1) * 512],
                           start=True, stop=True)
        return ins
    v_thb = S.op("pe", thbc, waits=[("dv", v_ths), ("w0", v_wconst),
                                    pj_free[2], pj_free[3]], inc="pe")
    v_the = {}
    for j in range(4):
        v_the[j] = S.op("dv", lambda e, j=j: e.tensor_copy(
            th_sb[:, j * 512:(j + 1) * 512],
            PJ[2 + j // 2][:, (j % 2) * 512:(j % 2) * 512 + 512]),
            waits=[("pe", v_thb)], inc="dv")
    v_thsb = v_the[3]

    v_ffn, v_qpe, v_msk = {}, {}, {}
    for ot in range(NSEG):
        b = ot % 2
        wq = ([("dv", v_msk[ot - 2])] if ot - 2 in v_msk
              else [("dv", v_red[NSEG - 2 + ot])])
        v_qd = S.dma("sy", qs[b][:], d_qwt.ap()[ot], inc=f"q{b}", waits=wq)
        v_sg = S.op("ac", lambda e, b=b: e.activation(
            qTs[b][:], qs[b][:], AF.Sign),
            waits=[(f"q{b}", v_qd)] +
            ([("pe", v_qpe[ot - 2])] if ot - 2 in v_qpe else []), inc="ac")
        v_ab = S.op("ac", lambda e, b=b: e.activation(
            qs[b][:], qs[b][:], AF.Abs), inc="ac")
        v_mk = S.op("dv", lambda e, b=b, ot=ot: e.tensor_tensor(
            qmask[:].rearrange("p (k c) -> p k c", k=NSEG),
            qs[b][:].rearrange("p (k c) -> p k c", k=NSEG),
            th_sb[:, ot * 128:(ot + 1) * 128].unsqueeze(1)
            .broadcast_to([128, NSEG, 128]), ALU.is_gt),
            waits=[("ac", v_ab), ("dv", v_thsb)], inc="dv")
        v_msk[ot] = v_mk
        v_qn = S.op("dv", lambda e, b=b: e.tensor_tensor(
            qTs[b][:], qmask[:], qTs[b][:], ALU.mult),
            waits=[("ac", v_sg), ("dv", v_mk)], inc="dv")
        pj = PJ[ot % 2]

        def qmms(e, b=b, pj=pj):
            ins = None
            for kt in range(NSEG):
                lhsT = qTs[b][:, kt * 128:(kt + 1) * 128]
                for hf in range(2):
                    ins = e.matmul(pj[:, hf * 512:(hf + 1) * 512], lhsT,
                                   xn[:, kt * SEGW + 1 + hf * 512:
                                      kt * SEGW + 1 + hf * 512 + 512],
                                   start=(kt == 0), stop=(kt == NSEG - 1))
            return ins
        w = [("dv", v_qn), ("dv", v_xn)]
        if ot - 2 in v_ffn:
            w.append(("ac", v_ffn[ot - 2]))
        elif ot < 2:
            w.append(pj_free[ot])
        v_qpe[ot] = S.op("pe", qmms, waits=w, inc="pe")
        v_ffn[ot] = S.op("ac", lambda e, ot=ot, pj=pj: e.activation(
            gffn[:, ot * TL:(ot + 1) * TL], pj[:], AF.Silu,
            bias=cols["qb"][:, ot:ot + 1]),
            waits=[("pe", v_qpe[ot]), ("ld0", v_cols)], inc="ac")
    v_gffn = v_ffn[NSEG - 1]
    if TRUNC <= 8:
        S.emit()
        return nc

    def dump_plane2(nm, src, width):
        if nm not in dbg:
            return
        for i in range((width + TL - 1) // TL):
            lo = i * TL
            hi = min(width, lo + TL)
            vv = S.op("dv", lambda e, lo=lo, hi=hi: e.tensor_copy(
                x2b[0][:, 0:hi - lo], src[:, lo:hi]),
                waits=[("ac", v_gffn)], inc="dv")
            S.dma("sy", dbg[nm].ap()[:, lo:hi], x2b[0][:, 0:hi - lo],
                  inc="sto", waits=[("dv", vv)])
    dump_plane2("gffn", gffn, NSEG * TL)

    v_wo, v_oev, v_st = {}, {}, {}
    v_gfd = S.cnt["dv"]
    for ot in range(NSEG):
        b = ot % 2
        w = ([("pe", v_wo[ot - 2])] if ot - 2 in v_wo
             else list(era_gate))
        v_wd = S.dma("po",
                     wband[b][:].rearrange("p (kt oc) -> p kt oc", kt=NSEG),
                     d_w["wot"].ap().rearrange("(kt p) o -> p kt o", p=128)
                     [:, :, ot * 128:(ot + 1) * 128],
                     inc=f"wb{b}", waits=w)
        w2 = ([("dv", v_oev[ot - 2])] if ot - 2 in v_oev
              else [("dv", v_gfd)] + list(era_gate))
        v_x2 = S.dma("sy", x2b[b][:], d_x.ap()[ot * 128:(ot + 1) * 128, 1:SEGW],
                     inc=f"x2{b}", waits=w2)
        pj = PJ[2 + ot % 2]

        def womms(e, b=b, pj=pj):
            ins = None
            for kt in range(NSEG):
                lhsT = wband[b][:, kt * 128:(kt + 1) * 128]
                for hf in range(2):
                    ins = e.matmul(pj[:, hf * 512:(hf + 1) * 512], lhsT,
                                   yngT[:, kt * TL + hf * 512:
                                        kt * TL + hf * 512 + 512],
                                   start=(kt == 0), stop=(kt == NSEG - 1))
            return ins
        w3 = [(f"wb{b}", v_wd), ("yngt", v_yngT)]
        if ot - 2 in v_oev:
            w3.append(("dv", v_oev[ot - 2]))
        elif ot < 2:
            w3.append(("dv", v_the[2 * ot + 1]))
        v_wo[ot] = S.op("pe", womms, waits=w3, inc="pe")
        wob = [("pe", v_wo[ot]), ("ac", v_gffn)]
        if ot - 2 in v_st:
            wob.append((f"st{b}", v_st[ot - 2]))
        S.op("dv", lambda e, ot=ot, b=b, pj=pj: e.tensor_tensor(
            outb[b][:], pj[:], gffn[:, ot * TL:(ot + 1) * TL], ALU.mult),
            waits=wob, inc="dv")
        v_oev[ot] = S.op("dv", lambda e, b=b: e.tensor_add(
            outb[b][:], outb[b][:], x2b[b][:]),
            waits=[(f"x2{b}", v_x2)], inc="dv")
        v_st[ot] = S.dma("sy", d_out.ap()[ot * 128:(ot + 1) * 128, :], outb[b][:],
                         inc=f"st{b}", waits=[("dv", v_oev[ot])])

    S.op("sy", lambda e: e.nop(),
         waits=[("st0", S.cnt.get("st0", 0)), ("st1", S.cnt.get("st1", 0)),
                ("sto", S.cnt.get("sto", 0))])
    es_f.close()
    S.emit()
    return nc


def _prep_inputs(inputs):
    f32 = np.float32
    x = np.ascontiguousarray(np.asarray(inputs["L0_attn_x"], f32))
    state = np.asarray(inputs["L0_attn_state"], f32).reshape(C)
    wkvstate = np.asarray(inputs["L0_attn_wkvstate"], f32)

    W = {}
    for nm, key in (("wrt", "L0_attn_w_r"), ("wkt", "L0_attn_w_k"),
                    ("wvt", "L0_attn_w_v"), ("wgt", "L0_attn_w_g"),
                    ("wot", "L0_attn_w_o")):
        W[nm] = np.ascontiguousarray(np.asarray(inputs[key], f32).T)
    qw = np.ascontiguousarray(np.asarray(inputs["L0_ffn_weight"], f32))

    def band(WT):
        a = WT.reshape(NSEG, 128, NSEG, 128)
        a = np.ascontiguousarray(a.transpose(2, 1, 0, 3))
        return a.reshape(NSEG, 128, NSEG * 128)

    qwt = band(np.ascontiguousarray(qw.T))

    def fmcols(v):
        return np.ascontiguousarray(np.asarray(v, f32).reshape(NSEG, 128).T)

    colmap = {"tmr": inputs["L0_attn_time_mix_r"],
              "tmk": inputs["L0_attn_time_mix_k"],
              "tmv": inputs["L0_attn_time_mix_v"],
              "tmg": inputs["L0_attn_time_mix_g"],
              "uu": inputs["L0_attn_time_faaaa"],
              "dec": inputs["L0_attn_time_decay"],
              "qb": inputs["L0_ffn_bias"]}
    cols = {k: fmcols(np.asarray(v, f32).reshape(C)) for k, v in colmap.items()}
    lnwbc = np.ascontiguousarray(np.broadcast_to(
        np.asarray(inputs["L0_attn_ln_w"], f32).reshape(1, C), (128, C)))
    lnbbc = np.ascontiguousarray(np.broadcast_to(
        np.asarray(inputs["L0_attn_ln_b"], f32).reshape(1, C), (128, C)))
    wkvp = np.ascontiguousarray(wkvstate.reshape(NSEG, 128, KD)).reshape(
        NSEG * 128, KD)
    mq = (np.arange(128)[:, None] < np.arange(128)[None, :]).astype(f32)
    mask2 = np.ascontiguousarray(np.concatenate([mq, mq], axis=1))
    ct = np.arange(NSEG * 128)
    bones = np.ascontiguousarray(
        ((ct[:, None] // KD) % 16 == np.arange(16)[None, :]).astype(f32))

    common = dict(qw=qw, qwt=qwt, lnwbc=lnwbc, lnbbc=lnbbc, wkvp=wkvp,
                  mask2=mask2, bones=bones, **W, **cols)
    in_maps = []
    for core in range(8):
        b, half = core // 2, core % 2
        t0 = half * TL
        xs = x[b, t0:t0 + TL]
        halo = state if half == 0 else x[b, t0 - 1]
        xfm = np.ascontiguousarray(
            np.concatenate([halo[None, :], xs], axis=0).T)
        xhalov = np.ascontiguousarray(halo.astype(f32).reshape(NSEG, 128).T)
        iota = np.ascontiguousarray(np.broadcast_to(
            (t0 + np.arange(TL)).astype(f32)[None, :], (128, TL)))
        selv = np.zeros((128, 8), f32)
        if half == 1:
            selv[:, core - 1] = 1.0
        hahbv = np.zeros((128, 4), f32)
        hahbv[:, 0] = 1.0 if half == 1 else 0.0
        hahbv[:, 1] = 0.0 if half == 1 else 1.0
        hahbv[:, 2] = EPS_LN
        hahbv[:, 3] = EPS_GN
        im = dict(common)
        im.update(x=xfm, xhalo=xhalov, iota=iota, sel=selv, hahb=hahbv)
        in_maps.append(im)
    return in_maps


_CACHED = {}


def kernel(**inputs):
    if "nc" not in _CACHED:
        _CACHED["nc"] = build_program()
    nc = _CACHED["nc"]
    in_maps = _prep_inputs(inputs)
    res = run_bass_kernel_spmd(nc, in_maps, core_ids=list(range(8)))
    out = np.zeros((B, T, C), np.float32)
    for core in range(8):
        b, half = core // 2, core % 2
        o = res.results[core]["out"]
        out[b, half * TL:(half + 1) * TL, :] = o.T
    if DEBUG:
        kernel.dbg = [{k: np.asarray(res.results[c]["dbg_" + k]) for k in DEBUG}
                      for c in range(8)]
    return out
